# version 01
# baseline (speedup 1.0000x reference)
import math
import numpy as np
import ml_dtypes
from contextlib import ExitStack
import concourse.bass as bass
import concourse.mybir as mybir
from concourse.bass_utils import run_bass_kernel_spmd

F32 = mybir.dt.float32
BF16 = mybir.dt.bfloat16
I32 = mybir.dt.int32
AF = mybir.ActivationFunctionType
ALU = mybir.AluOpType
AX = mybir.AxisListType.X

T = 4096
D = 1024
NT = T // 128
NCORES = 8
ISQ32 = 1.0 / math.sqrt(32.0)


class View:

    def __init__(self, h, shape, base=0):
        self.h = h
        self.shape = list(shape)
        self.base = base
        st = [1] * len(shape)
        for i in range(len(shape) - 2, 0, -1):
            st[i] = st[i + 1] * shape[i + 1]
        width = 1
        for d in h.shape[1:]:
            width *= int(d)
        st[0] = width
        self.st = st

    def __getitem__(self, idx):
        if not isinstance(idx, tuple):
            idx = (idx,)
        assert len(idx) == len(self.shape)
        off = self.base
        dims = []
        for k, ix in enumerate(idx):
            if isinstance(ix, int):
                off += ix * self.st[k]
            else:
                i0 = ix.start or 0
                i1 = ix.stop if ix.stop is not None else self.shape[k]
                step = ix.step or 1
                off += i0 * self.st[k]
                dims.append([self.st[k] * step, (i1 - i0 + step - 1) // step])
        if not dims:
            dims = [[1, 1]]
        return bass.AP(self.h, off, dims)


def np_bf16(x):
    return np.ascontiguousarray(np.asarray(x).astype(ml_dtypes.bfloat16))


def rope_tables_np():
    L = 32
    inv_freq = 1.0 / 10000.0 ** (np.arange(0, L, 2, dtype=np.float32) / L)
    tt = np.arange(T, dtype=np.float32)
    freqs = np.outer(tt, inv_freq)
    cos = np.repeat(np.cos(freqs), 2, axis=-1)[:, :L].astype(np.float32)
    sin = np.repeat(np.sin(freqs), 2, axis=-1)[:, :L].astype(np.float32)
    A = np.tile(cos, (1, 16))
    Bp = sin.copy()
    Bp[:, 0::2] = -Bp[:, 0::2]
    B = np.tile(Bp, (1, 16))
    return A, B


class EngProxy:

    _OPS = {"tensor_copy", "tensor_tensor", "tensor_add", "tensor_sub",
            "tensor_mul", "tensor_max", "tensor_scalar", "tensor_single_scalar",
            "tensor_scalar_add", "tensor_scalar_sub", "tensor_scalar_mul",
            "tensor_scalar_max", "tensor_scalar_min", "tensor_relu",
            "tensor_reduce", "reduce_max", "reduce_sum", "scalar_tensor_tensor",
            "tensor_tensor_scan", "reciprocal", "select", "memset", "memzero",
            "activation", "copy", "mul", "add", "sign", "square", "sqrt"}

    def __init__(self, eng, padfn=None):
        self._eng = eng
        self._padfn = padfn

    def __getattr__(self, name):
        fn = getattr(self._eng, name)
        if name not in EngProxy._OPS:
            return fn

        def wrapped(*args, **kw):
            r = fn(*args, **kw)
            out = args[0] if args else None
            pad = kw.get("accum_out") is not None
            if not pad and isinstance(out, bass.AP):
                free = 1
                for st_, cnt in list(out.ap)[1:]:
                    free *= cnt
                pad = len(list(out.ap)) == 1 or free < 128
            if pad and self._padfn is not None:
                self._padfn()
            return r
        return wrapped


class Build:
    def __init__(self, nc, stack):
        nc.detect_race_conditions = False
        self.nc = nc
        self.stack = stack
        self.sh = {}
        self.sv = {}
        self.consts = {}
        self.cur = None
        self.pend = []
        self.pb = [nc.alloc_psum_tensor(f"pb{i}", [128, 512], F32)
                   for i in range(6)]
        self.last_ins = {}
        self.dirty = {}
        nc.push_inst_callback(self._on_inst)
        self.padA = nc.alloc_sbuf_tensor("padA", [128, 896], F32)
        self.padV = nc.alloc_sbuf_tensor("padV", [128, 1], F32)
        self.padS = nc.alloc_sbuf_tensor("padS", [128, 640], F32)
        self.padG = nc.alloc_sbuf_tensor("padG", [128, 448], F32)
        self.vec = EngProxy(nc.vector,
                            lambda: nc.vector.reduce_sum(self.padV[:, 0:1],
                                                         self.padA[:, 0:896],
                                                         axis=AX))
        self.sca = EngProxy(nc.scalar,
                            lambda: nc.scalar.copy(self.padS[:, 0:640],
                                                   self.padA[:, 0:640]))
        self.gps = EngProxy(nc.gpsimd,
                            lambda: nc.gpsimd.tensor_copy(self.padG[:, 0:448],
                                                          self.padA[:, 0:448]))

    def _on_inst(self, ins):
        try:
            eng = ins.engine
        except Exception:
            return
        if eng == mybir.EngineType.SP:
            return
        nm = type(ins).__name__
        if "DMA" in nm or "Sem" in nm or "Event" in nm or nm.endswith("Nop"):
            return
        if not getattr(ins, "outs", None):
            return
        self.last_ins[eng] = ins
        self.dirty[eng] = True

    def sem(self, name):
        if name not in self.sh:
            self.sh[name] = self.stack.enter_context(self.nc.semaphore(name))
            self.sv[name] = 0
        return self.sh[name]

    def inc(self, instr, name, n=1):
        h = self.sem(name)
        self.sv[name] += n
        instr.then_inc(h, n)
        return self.sv[name]

    def wait(self, eng, name, val=None):
        h = self.sem(name)
        eng.wait_ge(h, self.sv[name] if val is None else val)

    def to(self, eng):
        if self.cur is eng and not self.pend:
            return eng
        if self.cur is not None and self.cur is not eng:
            self.inc(self.cur.drain(), "ch", 1)
            self.wait(eng, "ch")
        for qn in self.pend:
            self.wait(eng, qn)
        self.pend = []
        self.cur = eng
        if eng is self.nc.vector:
            return self.vec
        if eng is self.nc.scalar:
            return self.sca
        if eng is self.nc.gpsimd:
            return self.gps
        return eng

    def _dma(self, q, qn, dst, src, transpose_load=False, **kw):
        if transpose_load:
            ins = q.dma_start_transpose(dst, src)
        else:
            ins = q.dma_start(dst, src, **kw)
        self.inc(ins, qn, 16)
        if qn not in self.pend:
            self.pend.append(qn)

    def ldma(self, dst, src, **kw):
        self.to(self.nc.sync)
        self._dma(self.nc.sync, "sld", dst, src, **kw)

    def tdma(self, dst, src):
        self.to(self.nc.sync)
        self._dma(self.nc.sync, "sld", dst, src, transpose_load=True)

    def gdma(self, dst, src, **kw):
        self.to(self.nc.gpsimd)
        self._dma(self.nc.gpsimd, "gld", dst, src, **kw)

    def finish(self):
        for qn in ("sld", "ast", "gld"):
            if qn in self.sh:
                self.wait(self.nc.sync, qn)
        self.pend = []


def emit_small_transpose(b, dst, src_dram, P, F, stg, psum, idt16):
    nc = b.nc
    b.ldma(stg[0:P, 0:F], src_dram[0:P, 0:F])
    t = b.to(nc.tensor)
    t.transpose(psum[0:F, 0:P], stg[0:P, 0:F], idt16[0:P, 0:P])
    v = b.to(nc.vector)
    v.tensor_copy(dst[0:F, 0:P], psum[0:F, 0:P])


def emit_T64(b, dst, dram, r0, c0, stg, psum, idt16):
    nc = b.nc
    b.ldma(stg[0:128, 0:64], dram[r0:r0 + 128, c0:c0 + 64])
    t = b.to(nc.tensor)
    t.transpose(psum[0:64, 0:128], stg[0:128, 0:64], idt16[0:128, 0:128])
    v = b.to(nc.vector)
    v.tensor_copy(dst[0:64, 0:128], psum[0:64, 0:128])


def emit_identity(b, idt):
    g = b.to(b.nc.gpsimd)
    g.memset(idt[:, :], 1.0)
    n = idt.shape[1]
    g.affine_select(idt[:, :], idt[:, :], [[-1, n]], ALU.is_ge, 0.0,
                    channel_multiplier=1)
    g.affine_select(idt[:, :], idt[:, :], [[1, n]], ALU.is_ge, 0.0,
                    channel_multiplier=-1)


def emit_layernorm(b, x_src, xn_tm, fst, bst, stats, epsc, xlo_tm=None):
    nc = b.nc
    for i in range(NT):
        sl = i % 2
        xt = fst[:, sl, 0:1024]
        sqt = fst[:, sl, 1024:2048]
        b.ldma(xt, x_src[i * 128:(i + 1) * 128, :])
        v = b.to(nc.vector)
        v.reduce_sum(stats[:, 0:1], xt, axis=AX)
        sc = b.to(nc.scalar)
        sc.activation(sqt, xt, AF.Square, accum_out=stats[:, 1:2])
        v = b.to(nc.vector)
        v.tensor_scalar(stats[:, 2:3], stats[:, 0:1], 1.0 / D, None, ALU.mult)
        v.tensor_scalar(stats[:, 3:4], stats[:, 1:2], 1.0 / D, None, ALU.mult)
        v.tensor_tensor(stats[:, 4:5], stats[:, 2:3], stats[:, 2:3], ALU.mult)
        v.tensor_sub(stats[:, 3:4], stats[:, 3:4], stats[:, 4:5])
        sc = b.to(nc.scalar)
        sc.activation(stats[:, 4:5], stats[:, 3:4], AF.Sqrt, bias=epsc[:, 0:1])
        v = b.to(nc.vector)
        v.reciprocal(stats[:, 4:5], stats[:, 4:5])
        v.tensor_tensor(stats[:, 5:6], stats[:, 2:3], stats[:, 4:5], ALU.mult)
        v.tensor_scalar(stats[:, 5:6], stats[:, 5:6], -1.0, None, ALU.mult)
        sc = b.to(nc.scalar)
        sc.activation(bst[:, sl, :], xt, AF.Identity,
                      bias=stats[:, 5:6], scale=stats[:, 4:5])
        b.ldma(xn_tm[i * 128:(i + 1) * 128, :], bst[:, sl, :])
        if xlo_tm is not None:
            sc = b.to(nc.scalar)
            sc.activation(sqt, xt, AF.Identity,
                          bias=stats[:, 5:6], scale=stats[:, 4:5])
            v = b.to(nc.vector)
            v.tensor_copy(fst[:, 1 - sl, 0:1024], bst[:, sl, :])
            v.tensor_sub(sqt, sqt, fst[:, 1 - sl, 0:1024])
            v.tensor_copy(bst[:, 1 - sl, 0:1024], sqt)
            b.ldma(xlo_tm[i * 128:(i + 1) * 128, :], bst[:, 1 - sl, :])


def load_xnT(b, xn_tm, xnT):
    for f in range(8):
        b.tdma(xnT[:, f, 0:T], xn_tm[:, f * 128:(f + 1) * 128])



class NS:
    pass


def alloc_common(b):
    nc = b.nc
    C = NS()
    C.AB1h = nc.alloc_sbuf_tensor("AB1", [128, 32768], BF16)
    C.AB2h = nc.alloc_sbuf_tensor("AB2", [128, 32768], BF16)
    C.AF3h = nc.alloc_sbuf_tensor("AF3", [128, 8192], F32)
    C.WBh = nc.alloc_sbuf_tensor("WB", [128, 8192], BF16)
    C.idt32 = nc.alloc_sbuf_tensor("idt32", [128, 128], F32)
    C.idt16 = nc.alloc_sbuf_tensor("idt16", [128, 128], BF16)
    C.pt16 = nc.alloc_psum_tensor("pt16", [128, 256], BF16)
    C.stg16 = nc.alloc_sbuf_tensor("stg16", [128, 64], BF16)
    C.cpool = nc.alloc_sbuf_tensor("cpool", [128, 8], F32)
    C.stats = nc.alloc_sbuf_tensor("stats", [128, 8], F32)
    C.smallA = nc.alloc_sbuf_tensor("smallA", [128, 64], F32)
    C.lnrow = nc.alloc_sbuf_tensor("lnrow", [1, 512], F32)
    C.onesc = nc.alloc_sbuf_tensor("onesc", [128, 64], F32)
    C.bis = nc.alloc_sbuf_tensor("bis", [8, 8], F32)
    C.sb8 = nc.alloc_sbuf_tensor("sb8", [8, 128], F32)
    C.coeff = nc.alloc_sbuf_tensor("coeff", [128, NT, 8], F32)
    C.ltm = nc.alloc_sbuf_tensor("ltm", [128, NT, 8], F32)
    C.htm = nc.alloc_sbuf_tensor("htm", [128, NT, 8], F32)
    C.xnT = View(C.AB1h, [128, 8, T])
    C.fst = View(C.AF3h, [128, 2, 2048])
    C.af = View(C.AF3h, [128, 8192])
    C.em = View(C.AF3h, [8, 4096])
    C.emt = View(C.AF3h, [8, 4096], base=4096)
    C.wslot = View(C.WBh, [128, 2, 8, 512])
    C.bst = View(C.AB2h, [128, 2, 1024], base=30720)
    C.acc = View(C.AB2h, [128, NT, 1024])
    C.xn_tm = nc.dram_tensor("xn_tm", [T, D], BF16)
    return C


def emit_common_init(b, C):
    nc = b.nc
    g = b.to(nc.gpsimd)
    g.memset(b.padA[:, :], 0.0)
    emit_identity(b, C.idt32)
    emit_identity(b, C.idt16)
    g = b.to(nc.gpsimd)
    g.memset(C.onesc[:, :], 1.0)
    g.memset(C.cpool[:, 0:1], 1e-5)
    g.memset(C.cpool[:, 1:2], -math.pi)


def my_fori(nc, start, end, engines, body_fn):
    loop_id = nc.next_id()
    name = f"myfori_{loop_id}"
    registers = nc.alloc_registers(f"{name}_i", engines=engines)
    nc.regs_mov(registers, start)
    nc.br(name + "_loop", engines=engines)
    with nc.body(name + "_loop", valid_engines=engines, nocache=True):
        iv = nc.snap(registers, min_val=start, max_val=end - 1)
        body_fn(iv)
        nc.regs_alu(registers, registers, 1, op=ALU.add)
        nc.br_lt(registers, end, on_true=name + "_loop", on_false=name + "_end",
                 engines=engines)
    nc.switch_bb(name + "_end")
    for h in registers:
        nc.free_register(h)
    try:
        for h in iv.val:
            nc.free_register(h)
    except Exception:
        pass


def build_layer0(b, C, upto=99, dbg=False, dbg2=False):
    nc = b.nc

    def snap(name, src_ap, shape, dt):
        if not dbg2:
            return
        od = nc.dram_tensor(name, shape, dt, kind="ExternalOutput")
        b.ldma(od.ap(), src_ap)

    x_in = nc.dram_tensor("x", [T, D], F32, kind="ExternalInput")
    cw = nc.dram_tensor("L0_attn_c_attn_w", [3 * D, D], F32, kind="ExternalInput")
    cbv = nc.dram_tensor("L0_attn_c_attn_b", [3 * D], F32, kind="ExternalInput")
    q2w = nc.dram_tensor("L0_attn_q2l_w", [32, 64], F32, kind="ExternalInput")
    q2b = nc.dram_tensor("L0_attn_q2l_b", [32], F32, kind="ExternalInput")
    kv2w = nc.dram_tensor("L0_attn_kv2l_w", [32, 64], F32, kind="ExternalInput")
    kv2b = nc.dram_tensor("L0_attn_kv2l_b", [32], F32, kind="ExternalInput")
    l2ow = nc.dram_tensor("L0_attn_l2o_w", [64, 32], F32, kind="ExternalInput")
    l2ob = nc.dram_tensor("L0_attn_l2o_b", [64], F32, kind="ExternalInput")
    wqkw = nc.dram_tensor("L0_attn_wqk_w", [32, 32], F32, kind="ExternalInput")
    wqkb = nc.dram_tensor("L0_attn_wqk_b", [32], F32, kind="ExternalInput")
    cpw = nc.dram_tensor("L0_attn_cproj_w", [D, D], F32, kind="ExternalInput")
    cpb = nc.dram_tensor("L0_attn_cproj_b", [D], F32, kind="ExternalInput")
    gwt = nc.dram_tensor("L0_ffn_gate_w", [1, D, 8], F32, kind="ExternalInput")
    exps = nc.dram_tensor("L0_ffn_experts", [8, D, D], F32, kind="ExternalInput")
    ctab = nc.dram_tensor("ropeA", [T, 512], BF16, kind="ExternalInput")
    stabt = nc.dram_tensor("ropeB", [T, 512], BF16, kind="ExternalInput")
    A, Bt = rope_tables_np()
    b.consts["ropeA"] = np_bf16(A)
    b.consts["ropeB"] = np_bf16(Bt)
    x1_out = nc.dram_tensor("x1", [T, D], F32, kind="ExternalOutput")

    cw16 = nc.dram_tensor("cw16", [3 * D, D], BF16)
    cb16 = nc.dram_tensor("cb16", [48, 64], BF16)
    q2w16 = nc.dram_tensor("q2w16", [32, 64], BF16)
    kv2w16 = nc.dram_tensor("kv2w16", [32, 64], BF16)
    wqkw16 = nc.dram_tensor("wqkw16", [32, 32], BF16)
    cpw16 = nc.dram_tensor("cpw16", [D, D], BF16)
    l2ow16 = nc.dram_tensor("l2ow16", [64, 32], BF16)
    gw16 = nc.dram_tensor("gw16", [D, 8], BF16)
    ex16 = nc.dram_tensor("ex16", [8, D, D], BF16)
    qrope_tm = nc.dram_tensor("qrope_tm", [T, 512], BF16)
    krope_tm = nc.dram_tensor("krope_tm", [T, 512], BF16)
    vlat_tm = nc.dram_tensor("vlat_tm", [T, 512], BF16)
    latb_dram = nc.dram_tensor("latb_dram", [48, 32], F32)
    sb8_dram = nc.dram_tensor("sb8_dram", [8, 128], F32)
    s_dram = nc.dram_tensor("s_dram", [T, D], F32)

    latb_bc = nc.alloc_sbuf_tensor("latb_bc", [128, 1536], F32)
    sbias_bc = nc.alloc_sbuf_tensor("sbias_bc", [128, 1024], F32)
    latb32 = nc.alloc_sbuf_tensor("latb32", [32, 48], F32)
    q2T = nc.alloc_sbuf_tensor("q2T", [64, 32], BF16)
    kv2T = nc.alloc_sbuf_tensor("kv2T", [64, 32], BF16)
    bseg = nc.alloc_sbuf_tensor("bseg", [64, 48], BF16)
    wqkT = nc.alloc_sbuf_tensor("wqkT", [32, 32], BF16)
    wqkT4 = nc.alloc_sbuf_tensor("wqkT4", [128, 32], BF16)
    wqkb_sb = nc.alloc_sbuf_tensor("wqkb_sb", [32, 1], F32)
    l2o_sb = nc.alloc_sbuf_tensor("l2o_sb", [64, 32], BF16)
    masks = nc.alloc_sbuf_tensor("masks", [128, 4, 512], BF16)

    cstg = View(C.AB2h, [128, 2, 1024])
    ct16 = View(C.AB2h, [128, 2, 512], base=2048)
    st16 = View(C.AB2h, [128, 2, 512], base=3072)
    ost = View(C.AB2h, [128, 2, 3, 512], base=4096)
    wlatT = View(C.AB2h, [128, 8, 1536], base=8192)
    MoT = View(C.WBh, [128, 4, 1024])
    ptile = View(C.AB1h, [128, 2, 512])
    va = View(C.AB1h, [128, NT, 33], base=1024)
    qT = View(C.AB1h, [128, 4096], base=2112)
    kT = View(C.AB1h, [128, 4096], base=6208)
    qfkT = View(C.AB1h, [128, 4096], base=10304)
    ylatT = View(C.AB1h, [128, 4, T], base=14400)

    pb = b.pb

    b.gdma(cw16[:, :], cw[:, :])
    b.gdma(cb16.ap(), bass.AP(cbv, 0, [[64, 48], [1, 64]]))
    b.gdma(q2w16[:, :], q2w[:, :])
    b.gdma(kv2w16[:, :], kv2w[:, :])
    b.gdma(wqkw16[:, :], wqkw[:, :])
    b.gdma(cpw16[:, :], cpw[:, :])
    b.gdma(l2ow16[:, :], l2ow[:, :])
    b.gdma(gw16[:, :], gwt[0, :, :])
    for e in range(8):
        b.gdma(ex16[e, :, :], exps[e, :, :])
    g = b.to(nc.gpsimd)
    for mi in range(4):
        g.memset(masks[:, mi, :], 1.0)
        g.affine_select(masks[:, mi, :], masks[:, mi, :], [[1, 512]],
                        ALU.is_ge, 0.0, base=-(mi * 128), channel_multiplier=-1)

    emit_layernorm(b, x_in, C.xn_tm, C.fst, C.bst, C.stats, C.cpool)
    load_xnT(b, C.xn_tm, C.xnT)
    if upto <= 1:
        b.finish()
        return

    emit_small_transpose(b, q2T, q2w16, 32, 64, C.stg16, C.pt16, C.idt16)
    emit_small_transpose(b, kv2T, kv2w16, 32, 64, C.stg16, C.pt16, C.idt16)
    emit_small_transpose(b, bseg, cb16, 48, 64, C.stg16, C.pt16, C.idt16)
    for s in range(48):
        b.ldma(cstg[0:64, (s % 2), :], cw16[s * 64:(s + 1) * 64, :])
        lat = q2T if s < 16 else kv2T
        for m in range(8):
            t = b.to(nc.tensor)
            t.matmul(pb[0][:, 0:32], cstg[0:64, (s % 2), m * 128:(m + 1) * 128],
                     lat[:, :], start=True, stop=True)
            v = b.to(nc.vector)
            v.tensor_copy(wlatT[:, m, s * 32:(s + 1) * 32], pb[0][:, 0:32])
    t = b.to(nc.tensor)
    t.matmul(pb[1][0:32, 0:16], q2T[:, :], bseg[:, 0:16], start=True, stop=True)
    t.matmul(pb[1][0:32, 16:48], kv2T[:, :], bseg[:, 16:48], start=True, stop=True)
    b.ldma(C.smallA[0:32, 0:1], bass.AP(q2b, 0, [[1, 32], [1, 1]]))
    b.ldma(C.smallA[0:32, 1:2], bass.AP(kv2b, 0, [[1, 32], [1, 1]]))
    v = b.to(nc.vector)
    v.tensor_scalar(latb32[:, 0:16], pb[1][0:32, 0:16], C.smallA[0:32, 0:1], None, ALU.add)
    v.tensor_scalar(latb32[:, 16:48], pb[1][0:32, 16:48], C.smallA[0:32, 1:2], None, ALU.add)
    t = b.to(nc.tensor)
    t.transpose(pb[2][0:48, 0:32], latb32[:, :], C.idt32[0:32, 0:32])
    v = b.to(nc.vector)
    v.tensor_copy(C.smallA[0:48, 16:48], pb[2][0:48, 0:32])
    b.ldma(latb_dram[:, :], C.smallA[0:48, 16:48])
    b.ldma(latb_bc[:, :], bass.AP(latb_dram, 0, [[0, 128], [1, 1536]]))
    if upto <= 2:
        b.finish()
        return

    for i in range(NT):
        sl = i % 2
        b.ldma(ct16[:, sl, :], ctab[i * 128:(i + 1) * 128, :])
        b.ldma(st16[:, sl, :], stabt[i * 128:(i + 1) * 128, :])
        t = b.to(nc.tensor)
        for n in range(3):
            for k in range(8):
                t.matmul(pb[n][:, :], C.xnT[:, k, i * 128:(i + 1) * 128],
                         wlatT[:, k, n * 512:(n + 1) * 512],
                         start=(k == 0), stop=(k == 7))
        v = b.to(nc.vector)
        qf, kf = C.af[:, 0:512], C.af[:, 512:1024]
        tmp, t2 = C.af[:, 1024:1536], C.af[:, 1536:2048]
        for nn in (0, 1):
            dst_f = qf if nn == 0 else kf
            doff = nn * 512
            v.tensor_add(dst_f, pb[nn][:, :], latb_bc[:, nn * 512:(nn + 1) * 512])
            v.tensor_copy(bass.AP(C.AF3h, 1024, [[8192, 128], [2, 256]]),
                          bass.AP(C.AF3h, doff + 1, [[8192, 128], [2, 256]]))
            v.tensor_copy(bass.AP(C.AF3h, 1024 + 1, [[8192, 128], [2, 256]]),
                          bass.AP(C.AF3h, doff, [[8192, 128], [2, 256]]))
            v.tensor_mul(dst_f, dst_f, ct16[:, sl, :])
            v.tensor_mul(tmp, tmp, st16[:, sl, :])
            v.tensor_add(t2, dst_f, tmp)
            v.tensor_copy(ost[:, sl, nn, :], t2)
        v.tensor_add(ost[:, sl, 2, :], pb[2][:, :], latb_bc[:, 1024:1536])
        b.ldma(qrope_tm[i * 128:(i + 1) * 128, :], ost[:, sl, 0, :])
        b.ldma(krope_tm[i * 128:(i + 1) * 128, :], ost[:, sl, 1, :])
        b.ldma(vlat_tm[i * 128:(i + 1) * 128, :], ost[:, sl, 2, :])
    if upto <= 3:
        b.finish()
        return

    gwsb = View(C.WBh, [128, 8, 8])
    b.ldma(gwsb[:, 0:8, 0:8], gw16.rearrange("(k p) e -> p k e", p=128))
    for i in range(NT):
        t = b.to(nc.tensor)
        for k in range(8):
            t.matmul(pb[i % 2][:, 0:8], C.xnT[:, k, i * 128:(i + 1) * 128],
                     gwsb[:, k, 0:8], start=(k == 0), stop=(k == 7))
        v = b.to(nc.vector)
        v.tensor_scalar_max(C.ltm[:, i, :], pb[i % 2][:, 0:8], 1e-6)
    sc = b.to(nc.scalar)
    sc.activation(C.ltm[:, 0:NT, 0:8], C.ltm[:, 0:NT, 0:8], AF.Ln)
    for i in range(NT):
        t = b.to(nc.tensor)
        t.transpose(pb[i % 2][0:8, 0:128], C.ltm[:, i, :], C.idt32[:, :])
        v = b.to(nc.vector)
        v.tensor_copy(C.em[0:8, i * 128:(i + 1) * 128], pb[i % 2][0:8, 0:128])
    snap("z_em0", C.em[0:8, 0:T], [8, T], F32)
    for it in range(8):
        sc = b.to(nc.scalar)
        sc.activation(C.emt[0:8, 0:T], C.em[0:8, 0:T], AF.Exp, accum_out=C.bis[0:8, 0:1])
        sc.activation(C.bis[0:8, 1:2], C.bis[0:8, 0:1], AF.Ln)
        v = b.to(nc.vector)
        v.tensor_scalar(C.em[0:8, 0:T], C.em[0:8, 0:T], C.bis[0:8, 1:2], None, ALU.subtract)
        sc = b.to(nc.scalar)
        sc.activation(C.emt[0:8, 0:T], C.em[0:8, 0:T], AF.Exp)
        for c in range(8):
            t = b.to(nc.tensor)
            t.matmul(pb[0][0:1, 0:512], C.onesc[0:8, 0:1],
                     C.emt[0:8, c * 512:(c + 1) * 512], start=True, stop=True)
            sc = b.to(nc.scalar)
            sc.activation(C.lnrow[0:1, 0:512], pb[0][0:1, 0:512], AF.Ln)
            t = b.to(nc.tensor)
            t.matmul(pb[1][0:8, 0:512], C.onesc[0:1, 0:8],
                     C.lnrow[0:1, 0:512], start=True, stop=True)
            v = b.to(nc.vector)
            v.tensor_sub(C.em[0:8, c * 512:(c + 1) * 512],
                         C.em[0:8, c * 512:(c + 1) * 512], pb[1][0:8, 0:512])
    sc = b.to(nc.scalar)
    sc.activation(C.em[0:8, 0:T], C.em[0:8, 0:T], AF.Exp)
    snap("z_gates", C.em[0:8, 0:T], [8, T], F32)
    v = b.to(nc.vector)
    v.memset(C.bis[0:8, 0:1], 0.0)
    v.memset(C.bis[0:8, 1:2], 1.0)
    for it in range(40):
        v.tensor_tensor(C.bis[0:8, 2:3], C.bis[0:8, 0:1], C.bis[0:8, 1:2], ALU.add)
        v.tensor_scalar(C.bis[0:8, 2:3], C.bis[0:8, 2:3], 0.5, None, ALU.mult)
        v.tensor_scalar(C.emt[0:8, 0:T], C.em[0:8, 0:T], C.bis[0:8, 2:3], None,
                        ALU.is_ge, ALU.add, accum_out=C.bis[0:8, 3:4])
        v.tensor_scalar(C.bis[0:8, 4:5], C.bis[0:8, 3:4], 512.0, None, ALU.is_ge)
        v.tensor_sub(C.bis[0:8, 5:6], C.bis[0:8, 2:3], C.bis[0:8, 0:1])
        v.tensor_tensor(C.bis[0:8, 5:6], C.bis[0:8, 5:6], C.bis[0:8, 4:5], ALU.mult)
        v.tensor_add(C.bis[0:8, 0:1], C.bis[0:8, 0:1], C.bis[0:8, 5:6])
        v.tensor_scalar(C.bis[0:8, 6:7], C.bis[0:8, 4:5], -1.0, 1.0, ALU.mult, ALU.add)
        v.tensor_sub(C.bis[0:8, 5:6], C.bis[0:8, 2:3], C.bis[0:8, 1:2])
        v.tensor_tensor(C.bis[0:8, 5:6], C.bis[0:8, 5:6], C.bis[0:8, 6:7], ALU.mult)
        v.tensor_add(C.bis[0:8, 1:2], C.bis[0:8, 1:2], C.bis[0:8, 5:6])
    snap("z_bis", C.bis[0:8, 0:8], [8, 8], F32)
    v = b.to(nc.vector)
    v.tensor_scalar(C.emt[0:8, 0:T], C.em[0:8, 0:T], C.bis[0:8, 0:1], None, ALU.is_ge)
    v.tensor_scalar(C.em[0:8, 0:T], C.em[0:8, 0:T], 0.5, None, ALU.is_gt)
    v.tensor_tensor(C.em[0:8, 0:T], C.em[0:8, 0:T], C.emt[0:8, 0:T], ALU.mult)
    snap("z_m", C.emt[0:8, 0:T], [8, T], F32)
    for src, dst in ((C.emt, C.ltm), (C.em, C.htm)):
        for i in range(NT):
            t = b.to(nc.tensor)
            t.transpose(pb[i % 2][0:128, 0:8], src[0:8, i * 128:(i + 1) * 128],
                        C.idt32[0:8, 0:8])
            v = b.to(nc.vector)
            v.tensor_copy(dst[:, i, :], pb[i % 2][0:128, 0:8])
    lw_flush = nc.dram_tensor("lw_flush", [128, NT * 8], F32)
    b.ldma(lw_flush.ap(), C.ltm[:, 0:NT, 0:8])
    b.ldma(lw_flush.ap(), C.htm[:, 0:NT, 0:8])
    snap("z_ltm", C.ltm[:, 0:NT, 0:8], [128, NT * 8], F32)
    snap("z_htm", C.htm[:, 0:NT, 0:8], [128, NT * 8], F32)
    v = b.to(nc.vector)
    for i in range(NT):
        nt = C.smallA[:, 2:3]
        v.memset(nt, 1.0)
        for e in range(7, -1, -1):
            v.tensor_mul(C.coeff[:, i, e:e + 1], C.htm[:, i, e:e + 1], nt)
            v.tensor_scalar(C.smallA[:, 3:4], C.ltm[:, i, e:e + 1], -1.0, 1.0,
                            ALU.mult, ALU.add)
            v.tensor_mul(nt, nt, C.smallA[:, 3:4])
    snap("z_cf4", C.coeff[:, 0:NT, 0:8], [128, NT * 8], F32)
    if upto <= 4:
        b.finish()
        return

    v = b.to(nc.vector)
    v.memset(C.acc[:, 0:NT, 0:1024], 0.0)
    for e in range(8):
        exr = ex16[e, :, :].rearrange("(k p) n -> p k n", p=128)
        for n in range(2):
            sl = (e * 2 + n) % 2
            b.ldma(C.wslot[:, sl, 0:8, 0:512], exr[:, 0:8, n * 512:(n + 1) * 512])
            for i in range(NT):
                t = b.to(nc.tensor)
                for k in range(8):
                    t.matmul(pb[i % 2][:, :], C.xnT[:, k, i * 128:(i + 1) * 128],
                             C.wslot[:, sl, k, 0:512], start=(k == 0), stop=(k == 7))
                v = b.to(nc.vector)
                if e == 0:
                    v.tensor_scalar(C.acc[:, i, n * 512:(n + 1) * 512],
                                    pb[i % 2][:, :], C.coeff[:, i, e:e + 1],
                                    None, ALU.mult)
                else:
                    v.scalar_tensor_tensor(C.acc[:, i, n * 512:(n + 1) * 512],
                                           pb[i % 2][:, :], C.coeff[:, i, e:e + 1],
                                           C.acc[:, i, n * 512:(n + 1) * 512],
                                           ALU.mult, ALU.add)
    if upto <= 5:
        b.finish()
        return

    emit_small_transpose(b, wqkT, wqkw16, 32, 32, C.stg16, C.pt16, C.idt16)
    v = b.to(nc.vector)
    for h4 in range(4):
        v.tensor_copy(wqkT4[h4 * 32:(h4 + 1) * 32, 0:32], wqkT[0:32, 0:32])
    b.ldma(wqkb_sb[:, :], bass.AP(wqkb, 0, [[1, 32], [1, 1]]))
    b.ldma(l2o_sb[:, :], l2ow16[:, :])
    cslc = View(C.WBh, [64, 512], base=6144)
    for h in range(16):
        for dn in range(2):
            for q4 in range(4):
                emit_T64(b, View(C.WBh, [64, 128], base=6144 + q4 * 128),
                         cpw16, dn * 512 + q4 * 128, h * 64, C.stg16, C.pt16, C.idt16)
            t = b.to(nc.tensor)
            t.matmul(pb[0][0:32, 0:512], l2o_sb[:, :], cslc[0:64, 0:512],
                     start=True, stop=True)
            v = b.to(nc.vector)
            v.tensor_copy(MoT[(h % 4) * 32:(h % 4) * 32 + 32, h // 4,
                              dn * 512:(dn + 1) * 512], pb[0][0:32, 0:512])
    l2ob16 = nc.alloc_sbuf_tensor("l2ob16", [64, 1], BF16)
    b.gdma(l2ob16[:, :], bass.AP(l2ob, 0, [[1, 64], [1, 1]]))
    for dc in range(8):
        for h in range(16):
            cslc2 = View(C.WBh, [64, 128], base=6144)
            emit_T64(b, cslc2, cpw16, dc * 128, h * 64, C.stg16, C.pt16, C.idt16)
            t = b.to(nc.tensor)
            t.matmul(pb[3][:, dc:dc + 1], cslc2[0:64, 0:128],
                     l2ob16[:, 0:1], start=(h == 0), stop=(h == 15))
    v = b.to(nc.vector)
    v.tensor_copy(C.smallA[:, 16:24], pb[3][:, 0:8])
    t = b.to(nc.tensor)
    t.transpose(pb[4][0:8, 0:128], C.smallA[:, 16:24], C.idt32[:, :])
    b.ldma(C.sb8[0:8, 0:128], bass.AP(cpb, 0, [[128, 8], [1, 128]]))
    v = b.to(nc.vector)
    v.tensor_add(C.sb8[0:8, 0:128], C.sb8[0:8, 0:128], pb[4][0:8, 0:128])
    b.ldma(sb8_dram[:, :], C.sb8[0:8, 0:128])
    b.ldma(sbias_bc[:, :], bass.AP(sb8_dram, 0, [[0, 128], [1, 1024]]))
    g = b.to(nc.gpsimd)
    g.memset(va[:, 0:NT, 32:33], 1.0)
    for gidx in range(4):
        b.tdma(qT[0:128, 0:T], qrope_tm[:, gidx * 128:(gidx + 1) * 128])
        b.tdma(kT[0:128, 0:T], krope_tm[:, gidx * 128:(gidx + 1) * 128])
        for h4 in range(4):
            p0 = h4 * 32
            for qc in range(8):
                t = b.to(nc.tensor)
                t.matmul(pb[4][0:32, 0:512], wqkT4[p0:p0 + 32, 0:32],
                         qT[p0:p0 + 32, qc * 512:(qc + 1) * 512],
                         start=True, stop=True, tile_position=(p0, 0))
                sc = b.to(nc.scalar)
                sc.activation(qfkT[p0:p0 + 32, qc * 512:(qc + 1) * 512],
                              pb[4][0:32, 0:512], AF.Identity, bias=wqkb_sb[:, 0:1])
        for h4 in range(4):
            h = gidx * 4 + h4
            p0 = h4 * 32
            b.ldma(va[:, 0:NT, 0:32],
                   bass.AP(vlat_tm, h * 32, [[512, 128], [512 * 128, NT], [1, 32]]))
            for qb in range(8):
                kcn = 4 * (qb + 1)
                for kc in range(kcn):
                    t = b.to(nc.tensor)
                    t.matmul(pb[kc % 2][:, :], kT[p0:p0 + 32, kc * 128:(kc + 1) * 128],
                             qfkT[p0:p0 + 32, qb * 512:(qb + 1) * 512],
                             start=True, stop=True, tile_position=(p0, 0))
                    sc = b.to(nc.scalar)
                    sc.activation(ptile[:, kc % 2, :], pb[kc % 2][:, :], AF.Exp,
                                  scale=ISQ32)
                    cdiag = kc * 128 - qb * 512
                    if cdiag in (0, 128, 256, 384):
                        v = b.to(nc.vector)
                        v.tensor_mul(ptile[:, kc % 2, :], ptile[:, kc % 2, :],
                                     masks[:, cdiag // 128, :])
                    t = b.to(nc.tensor)
                    t.matmul(pb[2][0:33, 0:512], va[:, kc, 0:33], ptile[:, kc % 2, :],
                             start=(kc == 0), stop=(kc == kcn - 1))
                v = b.to(nc.vector)
                v.reciprocal(C.lnrow[0:1, 0:512], pb[2][32:33, 0:512])
                t = b.to(nc.tensor)
                t.matmul(pb[3][0:32, 0:512], C.onesc[0:1, 0:32], C.lnrow[0:1, 0:512],
                         start=True, stop=True)
                v = b.to(nc.vector)
                v.tensor_copy(C.af[0:32, 0:512], pb[3][0:32, 0:512])
                v.tensor_mul(ylatT[p0:p0 + 32, gidx, qb * 512:(qb + 1) * 512],
                             pb[2][0:32, 0:512], C.af[0:32, 0:512])
    for i in range(NT):
        for n in range(2):
            t = b.to(nc.tensor)
            for gc in range(4):
                t.matmul(pb[i % 2][:, :], ylatT[:, gc, i * 128:(i + 1) * 128],
                         MoT[:, gc, n * 512:(n + 1) * 512],
                         start=(gc == 0), stop=(gc == 3))
            v = b.to(nc.vector)
            v.tensor_add(C.af[:, 1024 + n * 512:1024 + (n + 1) * 512], pb[i % 2][:, :],
                         sbias_bc[:, n * 512:(n + 1) * 512])
        b.ldma(s_dram[i * 128:(i + 1) * 128, :], C.af[:, 1024:2048])
    if upto <= 6:
        b.finish()
        return

    for i in range(NT):
        b.ldma(C.af[:, 0:1024], x_in[i * 128:(i + 1) * 128, :])
        b.ldma(C.af[:, 1024:2048], s_dram[i * 128:(i + 1) * 128, :])
        sc = b.to(nc.scalar)
        sc.activation(C.af[:, 2048:3072], C.acc[:, i, :], AF.Silu)
        v = b.to(nc.vector)
        v.tensor_mul(C.af[:, 2048:3072], C.af[:, 2048:3072], C.af[:, 1024:2048])
        v.tensor_add(C.af[:, 3072:4096], C.af[:, 0:1024], C.af[:, 2048:3072])
        b.ldma(x1_out[i * 128:(i + 1) * 128, :], C.af[:, 3072:4096])
    if dbg:
        for nm, src, shp, dt in [("d_xn", C.xn_tm, [T, D], BF16),
                                 ("d_qr", qrope_tm, [T, 512], BF16),
                                 ("d_kr", krope_tm, [T, 512], BF16),
                                 ("d_vl", vlat_tm, [T, 512], BF16),
                                 ("d_s", s_dram, [T, D], F32)]:
            od = nc.dram_tensor(nm, shp, dt, kind="ExternalOutput")
            b.ldma(od[:, :], src[:, :])
        oacc = nc.dram_tensor("d_acc", [128, NT * 1024], BF16, kind="ExternalOutput")
        b.ldma(oacc[:, :], C.acc[:, 0:NT, 0:1024])
        ocf = nc.dram_tensor("d_cf", [128, NT * 8], F32, kind="ExternalOutput")
        b.ldma(ocf[:, :], C.coeff[:, 0:NT, 0:8])
    b.finish()



L0_KEYS = ["L0_attn_c_attn_w", "L0_attn_c_attn_b", "L0_attn_q2l_w",
           "L0_attn_q2l_b", "L0_attn_kv2l_w", "L0_attn_kv2l_b",
           "L0_attn_l2o_w", "L0_attn_l2o_b", "L0_attn_wqk_w",
           "L0_attn_wqk_b", "L0_attn_cproj_w", "L0_attn_cproj_b",
           "L0_ffn_gate_w", "L0_ffn_experts"]


def run_layer0(x_full, weights, upto=99):
    with ExitStack() as stack:
        nc = bass.Bass("TRN2", target_bir_lowering=False)
        b = Build(nc, stack)
        C = alloc_common(b)
        emit_common_init(b, C)
        build_layer0(b, C, upto=upto)
        in_maps = []
        for c in range(NCORES):
            m = {"x": np.ascontiguousarray(x_full[c])}
            for k in L0_KEYS:
                m[k] = np.ascontiguousarray(weights[k])
            m.update(b.consts)
            in_maps.append(m)
        res = run_bass_kernel_spmd(nc, in_maps, core_ids=list(range(NCORES)))
        return np.stack([res.results[c]["x1"] for c in range(NCORES)])





L1_KEYS = ["L1_attn_Wi", "L1_attn_bi", "L1_attn_Wh", "L1_attn_bh",
           "L1_ffn_gate_w", "L1_ffn_w_gate", "L1_ffn_w_up", "L1_ffn_w_down",
           "L1_ffn_sw_gate", "L1_ffn_sw_up", "L1_ffn_sw_down"]


def build_layer1(b, C, upto=99, dbg=False):
    nc = b.nc
    ET = mybir.EngineType

    x_in = nc.dram_tensor("x", [T, D], F32, kind="ExternalInput")
    Wi = nc.dram_tensor("L1_attn_Wi", [D, D], F32, kind="ExternalInput")
    bi = nc.dram_tensor("L1_attn_bi", [D], F32, kind="ExternalInput")
    Wh = nc.dram_tensor("L1_attn_Wh", [D, D], F32, kind="ExternalInput")
    bh = nc.dram_tensor("L1_attn_bh", [D], F32, kind="ExternalInput")
    gw = nc.dram_tensor("L1_ffn_gate_w", [8, D], F32, kind="ExternalInput")
    wg = nc.dram_tensor("L1_ffn_w_gate", [8, D, 1408], F32, kind="ExternalInput")
    wu = nc.dram_tensor("L1_ffn_w_up", [8, D, 1408], F32, kind="ExternalInput")
    wd = nc.dram_tensor("L1_ffn_w_down", [8, 1408, D], F32, kind="ExternalInput")
    swg = nc.dram_tensor("L1_ffn_sw_gate", [D, 2816], F32, kind="ExternalInput")
    swu = nc.dram_tensor("L1_ffn_sw_up", [D, 2816], F32, kind="ExternalInput")
    swd = nc.dram_tensor("L1_ffn_sw_down", [2816, D], F32, kind="ExternalInput")
    x2_out = nc.dram_tensor("x2", [T, D], F32, kind="ExternalOutput")

    wi16 = nc.dram_tensor("wi16", [D, D], BF16)
    wh16 = nc.dram_tensor("wh16", [D, D], BF16)
    gw16 = nc.dram_tensor("gw16l1", [8, D], BF16)
    wg16 = nc.dram_tensor("wg16", [8, D, 1408], BF16)
    wu16 = nc.dram_tensor("wu16", [8, D, 1408], BF16)
    wd16 = nc.dram_tensor("wd16", [8, 1408, D], BF16)
    swg16 = nc.dram_tensor("swg16", [D, 2816], BF16)
    swu16 = nc.dram_tensor("swu16", [D, 2816], BF16)
    swd16 = nc.dram_tensor("swd16", [2816, D], BF16)
    xi2_dram = nc.dram_tensor("xi2_dram", [8, 128, 512, 8], F32)
    sT_dram = nc.dram_tensor("sT_dram", [D, T], BF16)
    gt_dram = nc.dram_tensor("gt_dram", [8, 11, 128, T], BF16)
    gt2_dram = nc.dram_tensor("gt2_dram", [22, 128, T], BF16)

    pb = b.pb

    b.gdma(wi16[:, :], Wi[:, :])
    b.gdma(wh16[:, :], Wh[:, :])
    b.gdma(gw16[:, :], gw[:, :])
    for e in range(8):
        b.gdma(wg16[e, :, :], wg[e, :, :])
        b.gdma(wu16[e, :, :], wu[e, :, :])
        b.gdma(wd16[e, :, :], wd[e, :, :])
    b.gdma(swg16[:, :], swg[:, :])
    b.gdma(swu16[:, :], swu[:, :])
    b.gdma(swd16[:, :], swd[:, :])

    emit_layernorm(b, x_in, C.xn_tm, C.fst, C.bst, C.stats, C.cpool)
    load_xnT(b, C.xn_tm, C.xnT)
    if upto <= 1:
        b.finish()
        return

    wiT = View(C.WBh, [128, 8, 8, 128])
    for hc in range(8):
        for kf in range(8):
            b.tdma(wiT[:, hc, kf, 0:128],
                   wi16[hc * 128:(hc + 1) * 128, kf * 128:(kf + 1) * 128])
    bib = nc.alloc_sbuf_tensor("bib", [128, 8], F32)
    for hc in range(8):
        b.ldma(C.smallA[:, 0:1], bass.AP(bi, hc * 128, [[1, 128], [1, 1]]))
        b.ldma(C.smallA[:, 1:2], bass.AP(bh, hc * 128, [[1, 128], [1, 1]]))
        v = b.to(nc.vector)
        v.tensor_add(bib[:, hc:hc + 1], C.smallA[:, 0:1], C.smallA[:, 1:2])
    stage = View(C.AF3h, [128, 512, 8])
    for tb in range(8):
        for hc in range(8):
            t = b.to(nc.tensor)
            for kf in range(8):
                t.matmul(pb[hc % 2][:, :], wiT[:, hc, kf, 0:128],
                         C.xnT[:, kf, tb * 512:(tb + 1) * 512],
                         start=(kf == 0), stop=(kf == 7))
            sc = b.to(nc.scalar)
            sc.activation(bass.AP(C.AF3h, hc, [[8192, 128], [8, 512], [1, 1]]),
                          pb[hc % 2][:, :], AF.Identity, bias=bib[:, hc:hc + 1])
        b.ldma(xi2_dram[tb, :, :, :], stage[:, 0:512, 0:8])
    if upto <= 2:
        b.finish()
        return

    whT = View(C.AB2h, [128, 8, 8, 128])
    for m in range(8):
        for k in range(8):
            b.tdma(whT[:, k, m, 0:128],
                   wh16[m * 128:(m + 1) * 128, k * 128:(k + 1) * 128])
    shs = View(C.AB2h, [128, 2, 8, 512], base=8192)
    hsb = nc.alloc_sbuf_tensor("hsb", [128, 8], BF16)
    zf = nc.alloc_sbuf_tensor("zf", [128, 8], F32)
    v = b.to(nc.vector)
    v.memset(hsb[:, :], 0.0)
    mm = b.sem("r_mm")
    zz = b.sem("r_zz")
    hb = b.sem("r_hb")
    b.to(nc.sync)
    b.wait(nc.sync, "sld")
    ch = b.sem("ch")
    b.inc(nc.sync.nop(), "ch", 1)
    for eng in (nc.tensor, nc.vector, nc.scalar):
        eng.wait_ge(ch, b.sv["ch"])
    xslot = View(C.AF3h, [128, 2, 512, 8])
    sldv = b.sv["sld"]
    for blk in range(8):
        ins = nc.sync.dma_start(xslot[:, blk % 2, 0:512, 0:8],
                                xi2_dram[blk, :, :, :])
        sldv += 16
        ins.then_inc(b.sh["sld"], 16)
        b.sv["sld"] = sldv
        nc.tensor.wait_ge(b.sh["sld"], sldv)
        nc.vector.wait_ge(b.sh["sld"], sldv)
        base = blk * 512

        def rnn_body(i, blk=blk, base=base):
            nc.tensor.wait_ge(hb, base + i)
            nc.tensor.wait_ge(zz, base + i)
            ins = None
            for m in range(8):
                for k in range(8):
                    ins = nc.tensor.matmul(pb[5][:, m:m + 1], whT[:, k, m, 0:128],
                                           hsb[:, k:k + 1],
                                           start=(k == 0), stop=(k == 7))
            ins.then_inc(mm, 1)
            nc.vector.wait_ge(mm, base + i + 1)
            nc.vector.tensor_add(zf[:, :], pb[5][:, 0:8],
                                 bass.AP(C.AF3h, (blk % 2) * 4096 + i * 8,
                                         [[8192, 128], [1, 8]])).then_inc(zz, 1)
            nc.scalar.wait_ge(zz, base + i + 1)
            nc.scalar.activation(hsb[:, :], zf[:, :], AF.Tanh)
            nc.scalar.activation(
                bass.AP(C.AB2h, 8192 + (blk % 2) * 4096 + i,
                        [[32768, 128], [512, 8], [1, 1]]),
                zf[:, :], AF.Tanh).then_inc(hb, 1)

        my_fori(nc, 0, 512, [ET.PE, ET.DVE, ET.Activation], rnn_body)
        nc.sync.wait_ge(hb, base + 512)
        ins = nc.sync.dma_start(
            bass.AP(sT_dram, blk * 512, [[4096, 128], [4096 * 128, 8], [1, 512]]),
            shs[:, blk % 2, 0:8, 0:512])
        sldv += 16
        ins.then_inc(b.sh["sld"], 16)
        b.sv["sld"] = sldv
        nc.scalar.wait_ge(b.sh["sld"], sldv - 16)
    nc.sync.wait_ge(b.sh["sld"], sldv)
    b.cur = nc.sync
    b.pend = []
    if upto <= 3:
        b.finish()
        return

    gwsb8 = nc.alloc_sbuf_tensor("gwsb8", [8, D], BF16)
    gwT = View(C.WBh, [128, 8, 8])
    b.ldma(gwsb8[:, :], gw16[:, :])
    for kf in range(9):
        if kf < 8:
            t = b.to(nc.tensor)
            t.transpose(C.pt16[0:128, (kf % 2) * 128:(kf % 2) * 128 + 8],
                        gwsb8[0:8, kf * 128:(kf + 1) * 128], C.idt16[0:8, 0:8])
        if kf > 0:
            v = b.to(nc.vector)
            v.tensor_copy(gwT[:, kf - 1, 0:8],
                          C.pt16[0:128, ((kf - 1) % 2) * 128:((kf - 1) % 2) * 128 + 8])
    z8 = nc.alloc_sbuf_tensor("z8", [128, 8], F32)
    v = b.to(nc.vector)
    v.memset(z8[:, :], 0.0)
    af = C.af
    for i in range(NT):
        t = b.to(nc.tensor)
        for kf in range(8):
            t.matmul(pb[i % 2][:, 0:8], C.xnT[:, kf, i * 128:(i + 1) * 128],
                     gwT[:, kf, 0:8], start=(kf == 0), stop=(kf == 7))
        v = b.to(nc.vector)
        lg = af[:, 0:8]
        pr = af[:, 8:16]
        t2 = af[:, 16:24]
        m1 = af[:, 24:25]
        nm1 = af[:, 25:26]
        den = af[:, 26:27]
        m2 = af[:, 27:28]
        scl = af[:, 28:29]
        fo1 = af[:, 32:40]
        fo2 = af[:, 40:48]
        v.tensor_copy(lg, pb[i % 2][:, 0:8])
        v.reduce_max(m1, lg, axis=AX)
        v.tensor_scalar(nm1, m1, -1.0, None, ALU.mult)
        sc = b.to(nc.scalar)
        sc.activation(pr, lg, AF.Exp, bias=nm1[0:128, 0:1], accum_out=den)
        v = b.to(nc.vector)
        v.reciprocal(den, den)
        v.tensor_scalar(pr, pr, den[0:128, 0:1], None, ALU.mult)
        v.reduce_max(m1, pr, axis=AX)
        v.tensor_scalar(t2, pr, m1[0:128, 0:1], None, ALU.is_equal)
        v.tensor_tensor_scan(fo1, t2, z8[:, :], 0.0, ALU.add, ALU.add)
        v.tensor_scalar(fo1, fo1, 1.0, None, ALU.is_equal)
        v.tensor_tensor(fo1, fo1, t2, ALU.mult)
        v.tensor_scalar(t2, fo1, -1.0, 1.0, ALU.mult, ALU.add)
        v.tensor_tensor(t2, pr, t2, ALU.mult)
        v.reduce_max(m2, t2, axis=AX)
        v.tensor_scalar(pr, t2, m2[0:128, 0:1], None, ALU.is_equal)
        v.tensor_tensor_scan(fo2, pr, z8[:, :], 0.0, ALU.add, ALU.add)
        v.tensor_scalar(fo2, fo2, 1.0, None, ALU.is_equal)
        v.tensor_tensor(fo2, fo2, pr, ALU.mult)
        v.tensor_tensor(den, m1, m2, ALU.add)
        v.tensor_scalar(den, den, 1e-20, None, ALU.add)
        v.reciprocal(den, den)
        v.tensor_scalar(scl, den, 2.5, None, ALU.mult)
        v.tensor_scalar(fo1, fo1, m1[0:128, 0:1], None, ALU.mult)
        v.tensor_scalar(fo2, fo2, m2[0:128, 0:1], None, ALU.mult)
        v.tensor_add(fo1, fo1, fo2)
        v.tensor_scalar(C.coeff[:, i, :], fo1, scl[0:128, 0:1], None, ALU.mult)
    if dbg:
        ocf = nc.dram_tensor("d_cf", [128, NT * 8], F32, kind="ExternalOutput")
        b.ldma(ocf[:, :], C.coeff[:, 0:NT, 0:8])
    if upto <= 4:
        b.finish()
        return

    wgu = View(C.WBh, [128, 2, 8, 128])
    gtile = nc.alloc_sbuf_tensor("gtile", [128, 2, 512], BF16)
    for e in range(9):
        nch = 11 if e < 8 else 22
        for ffc in range(nch):
            if e < 8:
                b.ldma(wgu[:, 0, 0:8, 0:128],
                       wg16[e].rearrange("(kf p) f -> p kf f", p=128)[:, :, ffc * 128:(ffc + 1) * 128])
                b.ldma(wgu[:, 1, 0:8, 0:128],
                       wu16[e].rearrange("(kf p) f -> p kf f", p=128)[:, :, ffc * 128:(ffc + 1) * 128])
            else:
                b.ldma(wgu[:, 0, 0:8, 0:128],
                       swg16.rearrange("(kf p) f -> p kf f", p=128)[:, :, ffc * 128:(ffc + 1) * 128])
                b.ldma(wgu[:, 1, 0:8, 0:128],
                       swu16.rearrange("(kf p) f -> p kf f", p=128)[:, :, ffc * 128:(ffc + 1) * 128])
            for tb in range(8):
                t = b.to(nc.tensor)
                for kf in range(8):
                    t.matmul(pb[0][:, :], wgu[:, 0, kf, 0:128],
                             C.xnT[:, kf, tb * 512:(tb + 1) * 512],
                             start=(kf == 0), stop=(kf == 7))
                for kf in range(8):
                    t.matmul(pb[1][:, :], wgu[:, 1, kf, 0:128],
                             C.xnT[:, kf, tb * 512:(tb + 1) * 512],
                             start=(kf == 0), stop=(kf == 7))
                sc = b.to(nc.scalar)
                sc.activation(gtile[:, 0, :], pb[0][:, :], AF.Silu)
                v = b.to(nc.vector)
                v.tensor_tensor(gtile[:, 1, :], gtile[:, 0, :], pb[1][:, :], ALU.mult)
                if e < 8:
                    b.ldma(gt_dram[e, ffc, :, tb * 512:(tb + 1) * 512], gtile[:, 1, :])
                else:
                    b.ldma(gt2_dram[ffc, :, tb * 512:(tb + 1) * 512], gtile[:, 1, :])
    if upto <= 5:
        b.finish()
        return

    wdv = View(C.AB1h, [128, 22, 512])
    gtv = View(C.AB1h, [128, 22, 512], base=22 * 512)
    for e in range(9):
        nch = 11 if e < 8 else 22
        for n in range(2):
            if e < 8:
                b.ldma(wdv[:, 0:11, 0:512],
                       wd16[e, :, n * 512:(n + 1) * 512].rearrange("(c p) n -> p c n", p=128))
            else:
                b.ldma(wdv[:, 0:22, 0:512],
                       swd16[:, n * 512:(n + 1) * 512].rearrange("(c p) n -> p c n", p=128))
            for tb in range(8):
                if e < 8:
                    b.ldma(gtv[:, 0:11, 0:512],
                           gt_dram[e, :, :, tb * 512:(tb + 1) * 512].rearrange("c p n -> p c n"))
                else:
                    b.ldma(gtv[:, 0:22, 0:512],
                           gt2_dram[:, :, tb * 512:(tb + 1) * 512].rearrange("c p n -> p c n"))
                for t4 in range(4):
                    i = tb * 4 + t4
                    t = b.to(nc.tensor)
                    for ffc in range(nch):
                        t.matmul(pb[i % 2][:, :], gtv[:, ffc, t4 * 128:(t4 + 1) * 128],
                                 wdv[:, ffc, n * 512:(n + 1) * 512] if False else wdv[:, ffc, 0:512],
                                 start=(ffc == 0), stop=(ffc == nch - 1))
                    v = b.to(nc.vector)
                    if e == 0:
                        v.tensor_scalar(C.acc[:, i, n * 512:(n + 1) * 512],
                                        pb[i % 2][:, :], C.coeff[:, i, e:e + 1],
                                        None, ALU.mult)
                    elif e < 8:
                        v.scalar_tensor_tensor(C.acc[:, i, n * 512:(n + 1) * 512],
                                               pb[i % 2][:, :], C.coeff[:, i, e:e + 1],
                                               C.acc[:, i, n * 512:(n + 1) * 512],
                                               ALU.mult, ALU.add)
                    else:
                        v.scalar_tensor_tensor(C.acc[:, i, n * 512:(n + 1) * 512],
                                               pb[i % 2][:, :], 1.0,
                                               C.acc[:, i, n * 512:(n + 1) * 512],
                                               ALU.mult, ALU.add)
    if upto <= 6:
        b.finish()
        return

    af = C.af
    stile = nc.alloc_sbuf_tensor("stile", [128, 1024], BF16)
    for i in range(NT):
        b.ldma(af[:, 0:1024], x_in[i * 128:(i + 1) * 128, :])
        for f in range(8):
            b.tdma(stile[:, f * 128:(f + 1) * 128],
                   sT_dram[f * 128:(f + 1) * 128, i * 128:(i + 1) * 128])
        sc = b.to(nc.scalar)
        sc.activation(af[:, 2048:3072], C.acc[:, i, :], AF.Silu)
        v = b.to(nc.vector)
        v.tensor_mul(af[:, 2048:3072], af[:, 2048:3072], stile[:, :])
        v.tensor_add(af[:, 3072:4096], af[:, 0:1024], af[:, 2048:3072])
        b.ldma(x2_out[i * 128:(i + 1) * 128, :], af[:, 3072:4096])
    b.finish()


def idt32c(C):
    return C.idt32


def run_layer1(x_full, weights, upto=99):
    with ExitStack() as stack:
        nc = bass.Bass("TRN2", target_bir_lowering=False)
        b = Build(nc, stack)
        C = alloc_common(b)
        emit_common_init(b, C)
        build_layer1(b, C, upto=upto)
        in_maps = []
        for c in range(NCORES):
            m = {"x": np.ascontiguousarray(x_full[c])}
            for k in L1_KEYS:
                m[k] = np.ascontiguousarray(weights[k])
            m.update(b.consts)
            in_maps.append(m)
        res = run_bass_kernel_spmd(nc, in_maps, core_ids=list(range(NCORES)))
        return np.stack([res.results[c]["x2"] for c in range(NCORES)])



L2_KEYS = ["L2_attn_W_qkv", "L2_attn_W_out", "L2_attn_b_out", "L2_attn_rel_emb",
           "L2_ffn_w_gate", "L2_ffn_w_up", "L2_ffn_w_down"]


def build_layer2(b, C, upto=99, dbg=False):
    nc = b.nc
    x_in = nc.dram_tensor("x", [T, D], F32, kind="ExternalInput")
    wqkv = nc.dram_tensor("L2_attn_W_qkv", [3 * D, D], F32, kind="ExternalInput")
    wout = nc.dram_tensor("L2_attn_W_out", [D, D], F32, kind="ExternalInput")
    bout = nc.dram_tensor("L2_attn_b_out", [D], F32, kind="ExternalInput")
    remb = nc.dram_tensor("L2_attn_rel_emb", [2049, 16], F32, kind="ExternalInput")
    wgi = nc.dram_tensor("L2_ffn_w_gate", [3072, D], I32, kind="ExternalInput")
    wui = nc.dram_tensor("L2_ffn_w_up", [3072, D], I32, kind="ExternalInput")
    wdi = nc.dram_tensor("L2_ffn_w_down", [D, 3072], I32, kind="ExternalInput")
    x3_out = nc.dram_tensor("x3", [T, D], F32, kind="ExternalOutput")

    wqkv16 = nc.dram_tensor("wqkv16", [3 * D, D], BF16)
    wout16 = nc.dram_tensor("wout16", [D, D], BF16)
    wg16i = nc.dram_tensor("wg16i", [3072, D], BF16)
    wu16i = nc.dram_tensor("wu16i", [3072, D], BF16)
    wd16i = nc.dram_tensor("wd16i", [D, 3072], BF16)
    qkv_tm = nc.dram_tensor("qkv_tm", [T, 3 * D], BF16)
    Vdram = nc.dram_tensor("Vdram", [16, 8448], F32)
    Ddram = nc.dram_tensor("Ddram", [128, 16, 8320], F32)
    gtd = nc.dram_tensor("gtd_l2", [24, 128, T], BF16)
    s_dram = nc.dram_tensor("s_dram_l2", [T, D], F32)
    SCL = 1.0 / 127.0

    pb = b.pb

    b.gdma(wqkv16[:, :], wqkv[:, :])
    b.gdma(wout16[:, :], wout[:, :])
    b.gdma(wg16i[:, :], wgi[:, :])
    b.gdma(wu16i[:, :], wui[:, :])
    b.gdma(wd16i[:, :], wdi[:, :])

    emit_layernorm(b, x_in, C.xn_tm, C.fst, C.bst, C.stats, C.cpool)
    load_xnT(b, C.xn_tm, C.xnT)
    if upto <= 1:
        b.finish()
        return

    otile = View(C.AB2h, [128, 2, 512])
    for n in range(6):
        sl = n % 2
        for kf in range(8):
            b.tdma(C.wslot[:, sl, kf, 0:512],
                   wqkv16[n * 512:(n + 1) * 512, kf * 128:(kf + 1) * 128])
        for i in range(NT):
            t = b.to(nc.tensor)
            for kf in range(8):
                t.matmul(pb[i % 2][:, :], C.xnT[:, kf, i * 128:(i + 1) * 128],
                         C.wslot[:, sl, kf, 0:512], start=(kf == 0), stop=(kf == 7))
            v = b.to(nc.vector)
            v.tensor_copy(otile[:, i % 2, :], pb[i % 2][:, :])
            b.ldma(qkv_tm[i * 128:(i + 1) * 128, n * 512:(n + 1) * 512],
                   otile[:, i % 2, :])
    if upto <= 2:
        b.finish()
        return

    wgu = View(C.WBh, [128, 2, 8, 128])
    gtile = View(C.WBh, [128, 2, 512], base=2 * 8 * 128)
    for ffc in range(24):
        for kf in range(8):
            b.tdma(bass.AP(C.WBh, 0 * 2048 + 0 + kf * 128,
                           [[8192, 128], [1, 128]]) if False else wgu[:, 0, kf, 0:128],
                   wg16i[ffc * 128:(ffc + 1) * 128, kf * 128:(kf + 1) * 128])
            b.tdma(wgu[:, 1, kf, 0:128],
                   wu16i[ffc * 128:(ffc + 1) * 128, kf * 128:(kf + 1) * 128])
        for tb in range(8):
            t = b.to(nc.tensor)
            for kf in range(8):
                t.matmul(pb[0][:, :], wgu[:, 0, kf, 0:128],
                         C.xnT[:, kf, tb * 512:(tb + 1) * 512],
                         start=(kf == 0), stop=(kf == 7))
            for kf in range(8):
                t.matmul(pb[1][:, :], wgu[:, 1, kf, 0:128],
                         C.xnT[:, kf, tb * 512:(tb + 1) * 512],
                         start=(kf == 0), stop=(kf == 7))
            sc = b.to(nc.scalar)
            sc.activation(gtile[:, 0, :], pb[0][:, :], AF.Silu, scale=SCL)
            v = b.to(nc.vector)
            v.tensor_tensor(gtile[:, 1, :], gtile[:, 0, :], pb[1][:, :], ALU.mult)
            b.ldma(gtd[ffc, :, tb * 512:(tb + 1) * 512], gtile[:, 1, :])
    if upto <= 3:
        b.finish()
        return

    embs = nc.alloc_sbuf_tensor("embs", [128, 17, 16], F32)
    vrevT = nc.alloc_sbuf_tensor("vrevT", [16, 2064], F32)
    anti = nc.alloc_sbuf_tensor("anti", [128, 128], F32)
    g = b.to(nc.gpsimd)
    g.memset(anti[:, :], 1.0)
    g.affine_select(anti[:, :], anti[:, :], [[1, 128]], ALU.is_ge, 0.0,
                    base=-127, channel_multiplier=1)
    g.affine_select(anti[:, :], anti[:, :], [[-1, 128]], ALU.is_ge, 0.0,
                    base=127, channel_multiplier=-1)
    for ck in range(16):
        b.ldma(embs[:, ck, 0:16], remb[ck * 128:(ck + 1) * 128, 0:16])
    b.ldma(embs[0:1, 16, 0:16], remb[2048:2049, 0:16])
    for ck in range(17):
        if ck < 16:
            t = b.to(nc.tensor)
            t.matmul(pb[ck % 2][0:16, 0:128], embs[:, ck, 0:16], anti[:, :],
                     start=True, stop=True)
        if ck > 0:
            v = b.to(nc.vector)
            c0 = 1921 - (ck - 1) * 128
            v.tensor_copy(vrevT[0:16, c0:c0 + 128], pb[(ck - 1) % 2][0:16, 0:128])
    b.gdma(vrevT[0:16, 0:1], embs[0:1, 16, 0:16])
    ones1 = nc.alloc_sbuf_tensor("ones1", [1, 128], F32)
    bcv = nc.alloc_sbuf_tensor("bcv", [128, 32], F32)
    v = b.to(nc.vector)
    v.memset(ones1[0:1, :], 1.0)
    t = b.to(nc.tensor)
    t.matmul(pb[0][:, 0:16], ones1[0:1, 0:128], embs[0:1, 16, 0:16],
             start=True, stop=True)
    t.matmul(pb[1][:, 0:16], ones1[0:1, 0:128], embs[0:1, 0, 0:16],
             start=True, stop=True)
    v = b.to(nc.vector)
    v.tensor_copy(bcv[:, 0:16], pb[0][:, 0:16])
    v.tensor_copy(bcv[:, 16:32], pb[1][:, 0:16])
    cA = View(C.AF3h, [128, 32])
    for h in range(16):
        v = b.to(nc.vector)
        v.memset(cA[:, 0:26], 0.0)
        v.tensor_scalar(cA[:, 0:26], cA[:, 0:26], bcv[:, h:h + 1], None, ALU.add)
        b.ldma(bass.AP(Vdram, h * 8448, [[26, 128], [1, 26]]), cA[:, 0:26])
        b.gdma(bass.AP(Vdram, h * 8448 + 3327, [[2049, 1], [1, 2049]]),
               vrevT[h:h + 1, 0:2049])
        v = b.to(nc.vector)
        v.memset(cA[:, 0:24], 0.0)
        v.tensor_scalar(cA[:, 0:24], cA[:, 0:24], bcv[:, 16 + h:17 + h], None, ALU.add)
        b.ldma(bass.AP(Vdram, h * 8448 + 5376, [[24, 128], [1, 24]]), cA[:, 0:24])
    for p in range(128):
        b.ldma(Ddram[p, :, :], bass.AP(Vdram, 128 - p, [[8448, 16], [1, 8320]]))
    if upto <= 4:
        b.finish()
        return

    qT2 = View(C.AB1h, [128, 4096])
    kT2 = View(C.AB1h, [128, 4096], base=4096)
    va = View(C.AB1h, [128, NT, 65], base=8192)
    Dres = View(C.AB1h, [128, 8320], base=10304)
    PT = View(C.AB1h, [128, 2, 512], base=18624)
    yT = View(C.AB2h, [128, 8, T])
    g = b.to(nc.gpsimd)
    g.memset(va[:, 0:NT, 64:65], 1.0)
    af = C.af
    for gi in [1, 0, 2, 3, 4, 5, 6, 7]:
      for h in (gi * 2, gi * 2 + 1):
        p0 = (h % 2) * 64
        if h % 2 == 0:
            b.tdma(qT2[0:128, 0:T], qkv_tm[:, gi * 128:(gi + 1) * 128])
            b.tdma(kT2[0:128, 0:T], qkv_tm[:, 1024 + gi * 128:1024 + (gi + 1) * 128])
        b.gdma(Dres[:, 0:8320],
               bass.AP(Ddram, h * 8320, [[16 * 8320, 128], [1, 8320]]))
        b.ldma(va[:, 0:NT, 0:64],
               bass.AP(qkv_tm, 2048 + h * 64, [[3072, 128], [3072 * 128, NT], [1, 64]]))
        for qb in range(8):
            for kc in range(32):
                t = b.to(nc.tensor)
                t.matmul(pb[kc % 2][:, :], kT2[p0:p0 + 64, kc * 128:(kc + 1) * 128],
                         qT2[p0:p0 + 64, qb * 512:(qb + 1) * 512],
                         start=True, stop=True,
                         tile_position=(p0, 0) if p0 else None)
                v = b.to(nc.vector)
                dbase = 4223 - kc * 128 + qb * 512
                v.scalar_tensor_tensor(af[:, 0:512], pb[kc % 2][:, :], 0.125,
                                       Dres[:, dbase:dbase + 512],
                                       ALU.mult, ALU.add)
                if dbg and qb == 0 and kc == 0 and h in (0, 2):
                    dsc = nc.dram_tensor(f"dbg_sc{h}", [128, 512], F32,
                                         kind="ExternalOutput")
                    b.ldma(dsc[:, :], af[:, 0:512])
                    drw = nc.dram_tensor(f"dbg_raw{h}", [128, 512], F32,
                                         kind="ExternalOutput")
                    v = b.to(nc.vector)
                    v.tensor_copy(af[:, 512:1024], pb[kc % 2][:, :])
                    b.ldma(drw[:, :], af[:, 512:1024])
                sc = b.to(nc.scalar)
                sc.activation(PT[:, kc % 2, :], af[:, 0:512], AF.Exp)
                t = b.to(nc.tensor)
                t.matmul(pb[2][0:65, 0:512], va[:, kc, 0:65], PT[:, kc % 2, :],
                         start=(kc == 0), stop=(kc == 31))
            if dbg and qb == 0 and h in (0, 2):
                dyr = nc.dram_tensor(f"dbg_yraw{h}", [65, 512], F32,
                                     kind="ExternalOutput")
                v = b.to(nc.vector)
                v.tensor_copy(af[0:65, 512:1024], pb[2][0:65, 0:512])
                b.ldma(dyr[:, :], af[0:65, 512:1024])
            v = b.to(nc.vector)
            v.reciprocal(C.lnrow[0:1, 0:512], pb[2][64:65, 0:512])
            t = b.to(nc.tensor)
            t.matmul(pb[3][0:64, 0:512], C.onesc[0:1, 0:64], C.lnrow[0:1, 0:512],
                     start=True, stop=True)
            v = b.to(nc.vector)
            v.tensor_copy(af[0:64, 512:1024], pb[3][0:64, 0:512])
            v.tensor_mul(yT[p0:p0 + 64, gi, qb * 512:(qb + 1) * 512],
                         pb[2][0:64, 0:512], af[0:64, 512:1024])
    if upto <= 5:
        b.finish()
        return

    bout_bc = nc.alloc_sbuf_tensor("bout_bc", [128, 1024], F32)
    b.ldma(bout_bc[:, :], bass.AP(bout, 0, [[0, 128], [1, 1024]]))
    for n in range(2):
        sl = n % 2
        for kf in range(8):
            b.tdma(C.wslot[:, sl, kf, 0:512],
                   wout16[n * 512:(n + 1) * 512, kf * 128:(kf + 1) * 128])
        for i in range(NT):
            t = b.to(nc.tensor)
            for kf in range(8):
                t.matmul(pb[i % 2][:, :], yT[:, kf, i * 128:(i + 1) * 128],
                         C.wslot[:, sl, kf, 0:512], start=(kf == 0), stop=(kf == 7))
            v = b.to(nc.vector)
            v.tensor_add(af[:, 0:512], pb[i % 2][:, :],
                         bout_bc[:, n * 512:(n + 1) * 512])
            b.ldma(s_dram[i * 128:(i + 1) * 128, n * 512:(n + 1) * 512],
                   af[:, 0:512])
    if upto <= 6:
        b.finish()
        return

    if dbg:
        d_yt = nc.dram_tensor("dbg_yt", [128, 8 * T], BF16, kind="ExternalOutput")
        b.ldma(d_yt[:, :], yT[:, 0:8, 0:T])
        d_qkv = nc.dram_tensor("dbg_qkv", [T, 3 * D], BF16, kind="ExternalOutput")
        b.ldma(d_qkv[:, :], qkv_tm[:, :])
        d_dd0 = nc.dram_tensor("dbg_dd0", [128, 8320], F32, kind="ExternalOutput")
        b.ldma(d_dd0[:, :], bass.AP(Ddram, 0, [[16 * 8320, 128], [1, 8320]]))
        d_dd2 = nc.dram_tensor("dbg_dd2", [128, 8320], F32, kind="ExternalOutput")
        b.ldma(d_dd2[:, :], bass.AP(Ddram, 2 * 8320, [[16 * 8320, 128], [1, 8320]]))
        d_vd = nc.dram_tensor("dbg_vd", [16, 8448], F32, kind="ExternalOutput")
        b.ldma(d_vd[:, :], Vdram[:, :])
    wdv = View(C.AB1h, [128, 24, 512])
    gtv = View(C.AB1h, [128, 24, 512], base=24 * 512)
    for n in range(2):
        for kc24 in range(3):
            for kf in range(8):
                b.tdma(wdv[:, kc24 * 8 + kf, 0:512],
                       wd16i[n * 512:(n + 1) * 512,
                             (kc24 * 8 + kf) * 128:(kc24 * 8 + kf + 1) * 128])
        for tb in range(8):
            b.ldma(gtv[:, 0:24, 0:512],
                   gtd[:, :, tb * 512:(tb + 1) * 512].rearrange("c p n -> p c n"))
            for t4 in range(4):
                i = tb * 4 + t4
                t = b.to(nc.tensor)
                for ffc in range(24):
                    t.matmul(pb[i % 2][:, :], gtv[:, ffc, t4 * 128:(t4 + 1) * 128],
                             wdv[:, ffc, 0:512], start=(ffc == 0), stop=(ffc == 23))
                v = b.to(nc.vector)
                v.tensor_scalar(C.acc[:, i, n * 512:(n + 1) * 512],
                                pb[i % 2][:, :], SCL * SCL, None, ALU.mult)
    if upto <= 7:
        b.finish()
        return

    if dbg:
        d_s = nc.dram_tensor("dbg_s", [T, D], F32, kind="ExternalOutput")
        b.ldma(d_s[:, :], s_dram[:, :])
        d_acc = nc.dram_tensor("dbg_acc", [128, NT * 1024], BF16, kind="ExternalOutput")
        b.ldma(d_acc[:, :], C.acc[:, 0:NT, 0:1024])
    for i in range(NT):
        b.ldma(af[:, 0:1024], x_in[i * 128:(i + 1) * 128, :])
        b.ldma(af[:, 1024:2048], s_dram[i * 128:(i + 1) * 128, :])
        sc = b.to(nc.scalar)
        sc.activation(af[:, 2048:3072], C.acc[:, i, :], AF.Silu)
        v = b.to(nc.vector)
        v.tensor_mul(af[:, 2048:3072], af[:, 2048:3072], af[:, 1024:2048])
        v.tensor_add(af[:, 3072:4096], af[:, 0:1024], af[:, 2048:3072])
        b.ldma(x3_out[i * 128:(i + 1) * 128, :], af[:, 3072:4096])
    b.finish()


def run_layer2(x_full, weights, upto=99):
    with ExitStack() as stack:
        nc = bass.Bass("TRN2", target_bir_lowering=False)
        b = Build(nc, stack)
        C = alloc_common(b)
        emit_common_init(b, C)
        build_layer2(b, C, upto=upto)
        in_maps = []
        for c in range(NCORES):
            m = {"x": np.ascontiguousarray(x_full[c])}
            for k in L2_KEYS:
                m[k] = np.ascontiguousarray(weights[k])
            m.update(b.consts)
            in_maps.append(m)
        res = run_bass_kernel_spmd(nc, in_maps, core_ids=list(range(NCORES)))
        return np.stack([res.results[c]["x3"] for c in range(NCORES)])



L3_KEYS = ["L3_attn_Wq", "L3_attn_bq", "L3_attn_Wk", "L3_attn_bk",
           "L3_attn_Wv", "L3_attn_bv", "L3_attn_phase_bias",
           "L3_ffn_Wr", "L3_ffn_W1", "L3_ffn_b1", "L3_ffn_W2", "L3_ffn_b2",
           "L3_ffn_fw1", "L3_ffn_fb1", "L3_ffn_fw2", "L3_ffn_fb2"]

TWO_PI = 2.0 * math.pi
CAP3 = 10240


def build_layer3(b, C, upto=99, sim1=False, dbg=False):
    nc = b.nc
    x_in = nc.dram_tensor("x", [T, D], F32, kind="ExternalInput")
    Wq = nc.dram_tensor("L3_attn_Wq", [D, D], F32, kind="ExternalInput")
    bq = nc.dram_tensor("L3_attn_bq", [D], F32, kind="ExternalInput")
    Wk = nc.dram_tensor("L3_attn_Wk", [D, D], F32, kind="ExternalInput")
    bk = nc.dram_tensor("L3_attn_bk", [D], F32, kind="ExternalInput")
    Wv = nc.dram_tensor("L3_attn_Wv", [D, D], F32, kind="ExternalInput")
    bv = nc.dram_tensor("L3_attn_bv", [D], F32, kind="ExternalInput")
    pbv = nc.dram_tensor("L3_attn_phase_bias", [D], F32, kind="ExternalInput")
    Wr = nc.dram_tensor("L3_ffn_Wr", [8, D], F32, kind="ExternalInput")
    W1 = nc.dram_tensor("L3_ffn_W1", [8, D, 4096], F32, kind="ExternalInput")
    b1 = nc.dram_tensor("L3_ffn_b1", [8, 4096], F32, kind="ExternalInput")
    W2 = nc.dram_tensor("L3_ffn_W2", [8, 4096, D], F32, kind="ExternalInput")
    b2 = nc.dram_tensor("L3_ffn_b2", [8, D], F32, kind="ExternalInput")
    fw1 = nc.dram_tensor("L3_ffn_fw1", [D, 4096], F32, kind="ExternalInput")
    fb1 = nc.dram_tensor("L3_ffn_fb1", [4096], F32, kind="ExternalInput")
    fw2 = nc.dram_tensor("L3_ffn_fw2", [4096, D], F32, kind="ExternalInput")
    fb2 = nc.dram_tensor("L3_ffn_fb2", [D], F32, kind="ExternalInput")
    invwl_in = nc.dram_tensor("invwl", [1, D], F32, kind="ExternalInput")
    coremask = nc.dram_tensor("coremask", [8, 1], F32, kind="ExternalInput")
    iw = 1.0 / (np.arange(1, D + 1, dtype=np.float32) * np.float32(TWO_PI / D)
                + np.float32(1e-8))
    b.consts["invwl"] = np.ascontiguousarray(iw[None, :].astype(np.float32))
    x4_out = nc.dram_tensor("x4", [T, D], F32, kind="ExternalOutput")

    wv16 = nc.dram_tensor("wv16", [D, D], BF16)
    wr16 = nc.dram_tensor("wr16", [8, D], BF16)
    w116 = nc.dram_tensor("w116", [8, D, 4096], BF16)
    w216 = nc.dram_tensor("w216", [8, 4096, D], BF16)
    b216 = nc.dram_tensor("b216", [8, D], BF16)
    fw116 = nc.dram_tensor("fw116", [D, 4096], BF16)
    fw216 = nc.dram_tensor("fw216", [4096, D], BF16)
    wqhi = nc.dram_tensor("wqhi", [D, D], BF16)
    wqlo = nc.dram_tensor("wqlo", [D, D], BF16)
    wkhi = nc.dram_tensor("wkhi", [D, D], BF16)
    wklo = nc.dram_tensor("wklo", [D, D], BF16)
    xnlo_tm = nc.dram_tensor("xnlo_tm", [T, D], BF16)
    cq_tm = nc.dram_tensor("cq_tm", [T, D], BF16)
    sq_tm = nc.dram_tensor("sq_tm", [T, D], BF16)
    ck_tm = nc.dram_tensor("ck_tm", [T, D], BF16)
    sk_tm = nc.dram_tensor("sk_tm", [T, D], BF16)
    v_tm = nc.dram_tensor("v_tm_l3", [T, D], BF16)
    s_dram = nc.dram_tensor("s_dram_l3", [T, D], F32)
    gt3a = nc.dram_tensor("gt3a", [5, 32, 128, T], BF16)
    gt3b = nc.dram_tensor("gt3b", [4, 32, 128, T], BF16)
    def gt3s(src):
        return (gt3a, src) if src < 5 else (gt3b, src - 5)
    yfb_tm = nc.dram_tensor("yfb_tm", [T, D], BF16)
    cnt_dram = nc.dram_tensor("cnt_dram", [1, 8], F32)
    cntg_dram = nc.dram_tensor("cntg_dram", [8, 8], F32)
    em_dram = nc.dram_tensor("em_dram", [8, T], F32)
    kept_dram = nc.dram_tensor("kept_dram", [8, T], F32)
    fb_dram = nc.dram_tensor("fb_dram", [1, T], F32)

    pb = b.pb
    af = C.af
    stage6 = nc.alloc_sbuf_tensor("l3stg", [128, 3, 1024], BF16)
    small3 = nc.alloc_sbuf_tensor("small3", [8, 16], F32)
    cmsk = nc.alloc_sbuf_tensor("cmsk", [8, 1], F32)
    dent = nc.alloc_sbuf_tensor("dent", [128, 4], F32)
    onesk = nc.alloc_sbuf_tensor("onesk", [128, 1], BF16)
    pden = nc.alloc_psum_tensor("pden", [1, 512], F32)

    b.gdma(wv16[:, :], Wv[:, :])
    b.gdma(wr16[:, :], Wr[:, :])
    b.gdma(b216[:, :], b2[:, :])
    for e in range(8):
        b.gdma(w116[e, :, :], W1[e, :, :])
        b.gdma(w216[e, :, :], W2[e, :, :])
    b.gdma(fw116[:, :], fw1[:, :])
    b.gdma(fw216[:, :], fw2[:, :])

    emit_layernorm(b, x_in, C.xn_tm, C.fst, C.bst, C.stats, C.cpool,
                   xlo_tm=xnlo_tm)
    load_xnT(b, C.xn_tm, C.xnT)
    if upto <= 1:
        b.finish()
        return

    biasb = View(C.AF3h, [128, 4, 1024], base=4096)
    invwl_bc = View(C.AF3h, [128, 1024], base=2048)
    b.ldma(biasb[:, 0, :], bass.AP(bq, 0, [[0, 128], [1, 1024]]))
    b.ldma(biasb[:, 1, :], bass.AP(bk, 0, [[0, 128], [1, 1024]]))
    b.ldma(biasb[:, 2, :], bass.AP(bv, 0, [[0, 128], [1, 1024]]))
    b.ldma(biasb[:, 3, :], bass.AP(pbv, 0, [[0, 128], [1, 1024]]))
    b.ldma(invwl_bc[:, :], bass.AP(invwl_in, 0, [[0, 128], [1, 1024]]))
    v = b.to(nc.vector)
    for wi in (0, 1):
        v.tensor_mul(biasb[:, wi, :], biasb[:, wi, :], invwl_bc[:, :])
        v.tensor_add(biasb[:, wi, :], biasb[:, wi, :], biasb[:, 3, :])
    iwcol = nc.alloc_sbuf_tensor("iwcol", [128, 1], F32)
    for wsrc, whid, wlod in ((Wq, wqhi, wqlo), (Wk, wkhi, wklo)):
        for rt in range(8):
            b.ldma(af[:, 0:1024], wsrc[rt * 128:(rt + 1) * 128, :])
            b.ldma(iwcol[:, 0:1], bass.AP(invwl_in, rt * 128, [[1, 128], [1, 1]]))
            v = b.to(nc.vector)
            v.tensor_scalar(af[:, 0:1024], af[:, 0:1024], iwcol[0:128, 0:1],
                            None, ALU.mult)
            v.tensor_copy(stage6[:, 0, :], af[:, 0:1024])
            v.tensor_copy(af[:, 1024:2048], stage6[:, 0, :])
            v.tensor_sub(af[:, 1024:2048], af[:, 0:1024], af[:, 1024:2048])
            v.tensor_copy(stage6[:, 1, :], af[:, 1024:2048])
            b.ldma(whid[rt * 128:(rt + 1) * 128, :], stage6[:, 0, :])
            b.ldma(wlod[rt * 128:(rt + 1) * 128, :], stage6[:, 1, :])
    xloT = View(C.AB2h, [128, 8, T])
    for f in range(8):
        b.tdma(xloT[:, f, 0:T], xnlo_tm[:, f * 128:(f + 1) * 128])
    for wi, (whid, wlod, o_cos, o_sin) in enumerate(
            ((wqhi, wqlo, cq_tm, sq_tm), (wkhi, wklo, ck_tm, sk_tm))):
        for n in range(2):
            for kf in range(8):
                b.tdma(C.wslot[:, 0, kf, 0:512],
                       whid[n * 512:(n + 1) * 512, kf * 128:(kf + 1) * 128])
                b.tdma(C.wslot[:, 1, kf, 0:512],
                       wlod[n * 512:(n + 1) * 512, kf * 128:(kf + 1) * 128])
            ns = slice(n * 512, (n + 1) * 512)
            for i in range(NT):
                t = b.to(nc.tensor)
                for kf in range(8):
                    t.matmul(pb[i % 2][:, :], C.xnT[:, kf, i * 128:(i + 1) * 128],
                             C.wslot[:, 0, kf, 0:512], start=(kf == 0), stop=False)
                for kf in range(8):
                    t.matmul(pb[i % 2][:, :], C.xnT[:, kf, i * 128:(i + 1) * 128],
                             C.wslot[:, 1, kf, 0:512], start=False, stop=False)
                for kf in range(8):
                    t.matmul(pb[i % 2][:, :], xloT[:, kf, i * 128:(i + 1) * 128],
                             C.wslot[:, 0, kf, 0:512], start=False, stop=(kf == 7))
                v = b.to(nc.vector)
                th = af[:, 0:512]
                ps = af[:, 512:1024]
                rr = af[:, 1536:2048]
                v.tensor_add(th, pb[i % 2][:, :], biasb[:, wi, ns])
                for path, offs in ((0, math.pi / 2.0), (1, 0.0)):
                    v = b.to(nc.vector)
                    v.tensor_scalar(ps, th, offs, None, ALU.add)
                    v.tensor_scalar(rr, ps, 1.0 / TWO_PI, None, ALU.mult)
                    v.tensor_scalar(rr, rr, 12582912.0, None, ALU.add)
                    v.tensor_scalar(rr, rr, -12582912.0, None, ALU.add)
                    v.scalar_tensor_tensor(ps, rr, -TWO_PI, ps, ALU.mult, ALU.add)
                    v.tensor_scalar(ps, ps, math.pi, None, ALU.min)
                    v.tensor_scalar(ps, ps, -math.pi, None, ALU.max)
                    sc = b.to(nc.scalar)
                    sc.activation(stage6[:, path, 0:512], ps, AF.Sin)
                b.ldma(o_cos[i * 128:(i + 1) * 128, ns], stage6[:, 0, 0:512])
                b.ldma(o_sin[i * 128:(i + 1) * 128, ns], stage6[:, 1, 0:512])
    for n in range(2):
        for kf in range(8):
            b.tdma(C.wslot[:, 0, kf, 0:512],
                   wv16[n * 512:(n + 1) * 512, kf * 128:(kf + 1) * 128])
        ns = slice(n * 512, (n + 1) * 512)
        for i in range(NT):
            t = b.to(nc.tensor)
            for kf in range(8):
                t.matmul(pb[i % 2][:, :], C.xnT[:, kf, i * 128:(i + 1) * 128],
                         C.wslot[:, 0, kf, 0:512], start=(kf == 0), stop=(kf == 7))
            v = b.to(nc.vector)
            v.tensor_add(stage6[:, 2, 0:512], pb[i % 2][:, :], biasb[:, 2, ns])
            b.ldma(v_tm[i * 128:(i + 1) * 128, ns], stage6[:, 2, 0:512])
    if upto <= 2:
        b.finish()
        return

    ckT = View(C.AB1h, [128, 8, T])
    skT = View(C.AB2h, [128, 8, T])
    for f in range(8):
        b.tdma(ckT[:, f, 0:T], ck_tm[:, f * 128:(f + 1) * 128])
        b.tdma(skT[:, f, 0:T], sk_tm[:, f * 128:(f + 1) * 128])
    cqs = View(C.WBh, [128, 2, 8, 512])
    PT = View(stage6, [128, 2, 512])
    vt3 = View(stage6, [128, 2, 512], base=1024)
    v = b.to(nc.vector)
    v.memset(onesk[:, :], 1.0)
    for qb in range(8):
        for f in range(8):
            b.tdma(cqs[:, 0, f, 0:512], cq_tm[qb * 512:(qb + 1) * 512,
                                              f * 128:(f + 1) * 128])
            b.tdma(cqs[:, 1, f, 0:512], sq_tm[qb * 512:(qb + 1) * 512,
                                              f * 128:(f + 1) * 128])
        for dn in range(2):
            for kc in range(32):
                t = b.to(nc.tensor)
                for f in range(8):
                    t.matmul(pb[kc % 2][:, :], ckT[:, f, kc * 128:(kc + 1) * 128],
                             cqs[:, 0, f, 0:512], start=(f == 0), stop=False)
                for f in range(8):
                    t.matmul(pb[kc % 2][:, :], skT[:, f, kc * 128:(kc + 1) * 128],
                             cqs[:, 1, f, 0:512], start=False, stop=(f == 7))
                sc = b.to(nc.scalar)
                sc.activation(PT[:, kc % 2, :], pb[kc % 2][:, :], AF.Exp,
                              scale=ISQ32 / math.sqrt(32.0))
                b.ldma(vt3[:, kc % 2, 0:512],
                       v_tm[kc * 128:(kc + 1) * 128, dn * 512:(dn + 1) * 512])
                t = b.to(nc.tensor)
                for q4 in range(4):
                    t.matmul(pb[2 + q4][:, :],
                             PT[:, kc % 2, q4 * 128:(q4 + 1) * 128],
                             vt3[:, kc % 2, 0:512],
                             start=(kc == 0), stop=(kc == 31))
                if dn == 0:
                    t.matmul(pden[0:1, 0:512], onesk[:, 0:1], PT[:, kc % 2, :],
                             start=(kc == 0), stop=(kc == 31))
            if dn == 0:
                sc = b.to(nc.scalar)
                sc.copy(C.lnrow[0:1, 0:512], pden[0:1, 0:512])
                t = b.to(nc.tensor)
                for q4 in range(5):
                    if q4 < 4:
                        t.transpose(pb[q4 % 2][0:128, 0:1],
                                    C.lnrow[0:1, q4 * 128:(q4 + 1) * 128],
                                    C.idt32[0:1, 0:1])
                    else:
                        t.transpose(pb[0][0:128, 1:2],
                                    C.lnrow[0:1, 0:128], C.idt32[0:1, 0:1])
                    if q4 > 0:
                        v = b.to(nc.vector)
                        v.tensor_copy(dent[:, q4 - 1:q4],
                                      pb[(q4 - 1) % 2][0:128, 0:1])
                        t = b.to(nc.tensor)
                v = b.to(nc.vector)
                v.reciprocal(dent[:, 0:4], dent[:, 0:4])
            for q4 in range(4):
                v = b.to(nc.vector)
                v.tensor_scalar(af[:, 0:512], pb[2 + q4][:, :], dent[:, q4:q4 + 1],
                                None, ALU.mult)
                b.ldma(s_dram[qb * 512 + q4 * 128:qb * 512 + (q4 + 1) * 128,
                              dn * 512:(dn + 1) * 512], af[:, 0:512])
    if upto <= 3:
        b.finish()
        return

    load_xnT(b, C.xn_tm, C.xnT)
    gwsb8 = nc.alloc_sbuf_tensor("gwsb8_l3", [8, D], BF16)
    gwT = View(C.WBh, [128, 8, 8])
    b.ldma(gwsb8[:, :], wr16[:, :])
    for kf in range(9):
        if kf < 8:
            t = b.to(nc.tensor)
            t.transpose(C.pt16[0:128, (kf % 2) * 128:(kf % 2) * 128 + 8],
                        gwsb8[0:8, kf * 128:(kf + 1) * 128], C.idt16[0:8, 0:8])
        if kf > 0:
            v = b.to(nc.vector)
            v.tensor_copy(gwT[:, kf - 1, 0:8],
                          C.pt16[0:128, ((kf - 1) % 2) * 128:((kf - 1) % 2) * 128 + 8])
    z8 = nc.alloc_sbuf_tensor("z8_l3", [128, 8], F32)
    v = b.to(nc.vector)
    v.memset(z8[:, :], 0.0)
    for i in range(NT):
        t = b.to(nc.tensor)
        for kf in range(8):
            t.matmul(pb[i % 2][:, 0:8], C.xnT[:, kf, i * 128:(i + 1) * 128],
                     gwT[:, kf, 0:8], start=(kf == 0), stop=(kf == 7))
        v = b.to(nc.vector)
        lg = af[:, 0:8]
        pr = af[:, 8:16]
        t2 = af[:, 16:24]
        m1 = af[:, 24:25]
        nm1 = af[:, 25:26]
        den = af[:, 26:27]
        m2 = af[:, 27:28]
        fo1 = af[:, 32:40]
        fo2 = af[:, 40:48]
        v.tensor_copy(lg, pb[i % 2][:, 0:8])
        v.reduce_max(m1, lg, axis=AX)
        v.tensor_scalar(nm1, m1, -1.0, None, ALU.mult)
        sc = b.to(nc.scalar)
        sc.activation(pr, lg, AF.Exp, bias=nm1[0:128, 0:1], accum_out=den)
        v = b.to(nc.vector)
        v.reciprocal(den, den)
        v.tensor_scalar(pr, pr, den[0:128, 0:1], None, ALU.mult)
        v.reduce_max(m1, pr, axis=AX)
        v.tensor_scalar(t2, pr, m1[0:128, 0:1], None, ALU.is_equal)
        v.tensor_tensor_scan(fo1, t2, z8[:, :], 0.0, ALU.add, ALU.add)
        v.tensor_scalar(fo1, fo1, 1.0, None, ALU.is_equal)
        v.tensor_tensor(fo1, fo1, t2, ALU.mult)
        v.tensor_scalar(t2, fo1, -1.0, 1.0, ALU.mult, ALU.add)
        v.tensor_tensor(t2, pr, t2, ALU.mult)
        v.reduce_max(m2, t2, axis=AX)
        v.tensor_scalar(pr, t2, m2[0:128, 0:1], None, ALU.is_equal)
        v.tensor_tensor_scan(fo2, pr, z8[:, :], 0.0, ALU.add, ALU.add)
        v.tensor_scalar(fo2, fo2, 1.0, None, ALU.is_equal)
        v.tensor_tensor(fo2, fo2, pr, ALU.mult)
        v.tensor_add(C.ltm[:, i, :], fo1, fo2)
        v.tensor_tensor(den, m1, m2, ALU.add)
        v.reciprocal(den, den)
        v.tensor_scalar(fo1, fo1, m1[0:128, 0:1], None, ALU.mult)
        v.tensor_scalar(fo2, fo2, m2[0:128, 0:1], None, ALU.mult)
        v.tensor_add(fo1, fo1, fo2)
        v.tensor_scalar(C.coeff[:, i, :], fo1, den[0:128, 0:1], None, ALU.mult)
    el8 = C.em
    cu8 = C.emt
    for i in range(NT + 1):
        if i < NT:
            t = b.to(nc.tensor)
            t.transpose(pb[i % 2][0:8, 0:128], C.ltm[:, i, :], C.idt32[:, :])
        if i > 0:
            v = b.to(nc.vector)
            v.tensor_copy(cu8[0:8, (i - 1) * 128:i * 128], pb[(i - 1) % 2][0:8, 0:128])
    b.ldma(em_dram[:, :], cu8[0:8, 0:T])
    v = b.to(nc.vector)
    v.memset(el8[0:8, 0:T], 0.0)
    b.ldma(fb_dram[0:1, 0:T], el8[0:1, 0:T])
    b.ldma(cmsk[0:8, 0:1], coremask[:, :])
    CAPX = float(2 * CAP3)
    for e in range(8):
        b.ldma(cu8[0:8, 0:T], bass.AP(fb_dram, 0, [[0, 8], [1, T]]))
        b.ldma(el8[0:8, 0:T], em_dram[:, :])
        v = b.to(nc.vector)
        v.tensor_tensor(cu8[0:8, 0:T], cu8[0:8, 0:T], el8[0:8, 0:T], ALU.mult)
        v.tensor_sub(el8[0:8, 0:T], el8[0:8, 0:T], cu8[0:8, 0:T])
        v.reduce_sum(small3[0:8, 0:1], el8[0:8, 0:T], axis=AX)
        t = b.to(nc.tensor)
        t.transpose(pb[0][0:1, 0:8], small3[0:8, 0:1], C.idt32[0:8, 0:8])
        t.transpose(pb[1][0:1, 0:8], small3[0:8, 0:1], C.idt32[0:8, 0:8])
        v = b.to(nc.vector)
        v.tensor_copy(small3[0:1, 8:16], pb[0][0:1, 0:8])
        b.ldma(cnt_dram[0:1, 0:8], small3[0:1, 8:16])
        if sim1:
            for r in range(8):
                b.ldma(cntg_dram[r:r + 1, :], cnt_dram[0:1, :])
        else:
            g = b.to(nc.gpsimd)
            ins_cc = nc.gpsimd.collective_compute(
                "AllGather", ALU.bypass, replica_groups=[list(range(NCORES))],
                ins=[cnt_dram.ap().opt()], outs=[cntg_dram.ap().opt()])
            b.inc(ins_cc, "cc", 1)
            if "cc" not in b.pend:
                b.pend.append("cc")
        b.ldma(small3[0:8, 8:16], cntg_dram[:, :])
        t = b.to(nc.tensor)
        t.matmul(pb[0][0:1, 0:8], cmsk[0:8, 0:1], small3[0:8, 8:16],
                 start=True, stop=True)
        v = b.to(nc.vector)
        v.tensor_scalar(small3[0:1, 8:16], pb[0][0:1, 0:8], -2.0, CAPX,
                        ALU.mult, ALU.add)
        t = b.to(nc.tensor)
        t.transpose(pb[1][0:8, 0:1], small3[0:1, 8:16], C.idt32[0:1, 0:1])
        t.transpose(pb[0][0:8, 0:1], small3[0:1, 8:16], C.idt32[0:1, 0:1])
        v = b.to(nc.vector)
        v.tensor_copy(small3[0:8, 0:1], pb[1][0:8, 0:1])
        v.tensor_tensor_scan(cu8[0:8, 0:T], el8[0:8, 0:T], el8[0:8, 0:T],
                             0.0, ALU.add, ALU.add)
        v.tensor_scalar(cu8[0:8, 0:T], cu8[0:8, 0:T], -1.0, None, ALU.mult)
        v.tensor_scalar(cu8[0:8, 0:T], cu8[0:8, 0:T], small3[0:8, 0:1],
                        None, ALU.add)
        v.tensor_scalar(cu8[0:8, 0:T], cu8[0:8, 0:T], 0.0, None, ALU.is_ge)
        v.tensor_tensor(cu8[0:8, 0:T], cu8[0:8, 0:T], el8[0:8, 0:T], ALU.mult)
        b.ldma(kept_dram[e:e + 1, 0:T], cu8[e:e + 1, 0:T])
        v = b.to(nc.vector)
        v.tensor_sub(el8[0:8, 0:T], el8[0:8, 0:T], cu8[0:8, 0:T])
        b.ldma(cu8[0:8, 0:T], bass.AP(fb_dram, 0, [[0, 8], [1, T]]))
        v = b.to(nc.vector)
        v.tensor_add(el8[0:8, 0:T], el8[0:8, 0:T], cu8[0:8, 0:T])
        b.ldma(fb_dram[0:1, 0:T], el8[e:e + 1, 0:T])
    b.ldma(el8[0:8, 0:T], kept_dram[:, :])
    for i in range(NT + 1):
        if i < NT:
            t = b.to(nc.tensor)
            t.transpose(pb[i % 2][0:8, 0:128], C.coeff[:, i, :], C.idt32[:, :])
        if i > 0:
            v = b.to(nc.vector)
            v.tensor_copy(cu8[0:8, (i - 1) * 128:i * 128], pb[(i - 1) % 2][0:8, 0:128])
    v = b.to(nc.vector)
    v.tensor_tensor(el8[0:8, 0:T], el8[0:8, 0:T], cu8[0:8, 0:T], ALU.mult)
    for i in range(NT + 1):
        if i < NT:
            t = b.to(nc.tensor)
            t.transpose(pb[i % 2][0:128, 0:8], el8[0:8, i * 128:(i + 1) * 128],
                        C.idt32[0:8, 0:8])
        if i > 0:
            v = b.to(nc.vector)
            v.tensor_copy(C.htm[:, i - 1, :], pb[(i - 1) % 2][0:128, 0:8])
    b.ldma(cu8[0:8, 0:T], bass.AP(fb_dram, 0, [[0, 8], [1, T]]))
    for i in range(NT + 1):
        if i < NT:
            t = b.to(nc.tensor)
            t.transpose(pb[i % 2][0:128, 0:8], cu8[0:8, i * 128:(i + 1) * 128],
                        C.idt32[0:8, 0:8])
        if i > 0:
            v = b.to(nc.vector)
            v.tensor_copy(C.ltm[:, i - 1, :], pb[(i - 1) % 2][0:128, 0:8])
    if dbg:
        okw = nc.dram_tensor("d_kw", [8, T], F32, kind="ExternalOutput")
        b.ldma(okw[:, :], el8[0:8, 0:T])
        ofb = nc.dram_tensor("d_fb", [128, NT * 8], F32, kind="ExternalOutput")
        b.ldma(ofb[:, :], C.ltm[:, 0:NT, 0:8])
    if upto <= 4:
        b.finish()
        return

    w1v = View(C.WBh, [128, 8, 512])
    gtile = View(stage6, [128, 512], base=2048)
    b1v = nc.alloc_sbuf_tensor("b1v", [128, 32], F32)
    b1s = nc.alloc_sbuf_tensor("b1s", [32, 128], F32)
    for src in range(9):
        if src < 8:
            b.ldma(b1s[0:32, 0:128], bass.AP(b1, src * 4096, [[128, 32], [1, 128]]))
        else:
            b.ldma(b1s[0:32, 0:128], bass.AP(fb1, 0, [[128, 32], [1, 128]]))
        t = b.to(nc.tensor)
        t.transpose(pb[0][0:128, 0:32], b1s[0:32, 0:128], C.idt32[0:32, 0:32])
        t.transpose(pb[1][0:128, 0:32], b1s[0:32, 0:128], C.idt32[0:32, 0:32])
        v = b.to(nc.vector)
        v.tensor_copy(b1v[:, 0:32], pb[0][0:128, 0:32])
        for fg in range(8):
            if src < 8:
                b.ldma(w1v[:, 0:8, 0:512],
                       w116[src].rearrange("(kf p) f -> p kf f", p=128)
                       [:, :, fg * 512:(fg + 1) * 512])
            else:
                b.ldma(w1v[:, 0:8, 0:512],
                       fw116.rearrange("(kf p) f -> p kf f", p=128)
                       [:, :, fg * 512:(fg + 1) * 512])
            for fc4 in range(4):
                ffc = fg * 4 + fc4
                for tb in range(8):
                    t = b.to(nc.tensor)
                    for kf in range(8):
                        t.matmul(pb[tb % 2][:, :],
                                 w1v[:, kf, fc4 * 128:(fc4 + 1) * 128],
                                 C.xnT[:, kf, tb * 512:(tb + 1) * 512],
                                 start=(kf == 0), stop=(kf == 7))
                    sc = b.to(nc.scalar)
                    sc.activation(gtile[:, 0:512], pb[tb % 2][:, :], AF.Gelu,
                                  bias=b1v[:, ffc:ffc + 1])
                    gth, gsi = gt3s(src)
                    b.ldma(gth[gsi, ffc, :, tb * 512:(tb + 1) * 512],
                           gtile[:, 0:512])
    if upto <= 5:
        b.finish()
        return

    wdv = View(C.AB1h, [128, 32, 512])
    gtv = View(C.AB1h, [128, 32, 512], base=32 * 512)
    b2bc = View(C.AF3h, [128, 1024], base=4096)
    b.ldma(b2bc[:, :], bass.AP(fb2, 0, [[0, 128], [1, 1024]]))
    for src in range(9):
        for n in range(2):
            if src < 8:
                b.ldma(wdv[:, 0:32, 0:512],
                       w216[src, :, n * 512:(n + 1) * 512]
                       .rearrange("(c p) n -> p c n", p=128))
            else:
                b.ldma(wdv[:, 0:32, 0:512],
                       fw216[:, n * 512:(n + 1) * 512]
                       .rearrange("(c p) n -> p c n", p=128))
            for tb in range(8):
                gth, gsi = gt3s(src)
                b.ldma(gtv[:, 0:32, 0:512],
                       gth[gsi, :, :, tb * 512:(tb + 1) * 512]
                       .rearrange("c p n -> p c n"))
                for t4 in range(4):
                    i = tb * 4 + t4
                    t = b.to(nc.tensor)
                    for ffc in range(32):
                        t.matmul(pb[i % 2][:, :],
                                 gtv[:, ffc, t4 * 128:(t4 + 1) * 128],
                                 wdv[:, ffc, 0:512],
                                 start=(ffc == 0), stop=(ffc == 31))
                    v = b.to(nc.vector)
                    if src == 0:
                        v.tensor_scalar(C.acc[:, i, n * 512:(n + 1) * 512],
                                        pb[i % 2][:, :], C.htm[:, i, src:src + 1],
                                        None, ALU.mult)
                    elif src < 8:
                        v.scalar_tensor_tensor(C.acc[:, i, n * 512:(n + 1) * 512],
                                               pb[i % 2][:, :],
                                               C.htm[:, i, src:src + 1],
                                               C.acc[:, i, n * 512:(n + 1) * 512],
                                               ALU.mult, ALU.add)
                    else:
                        v.tensor_add(gtile[:, 0:512], pb[i % 2][:, :],
                                     b2bc[:, n * 512:(n + 1) * 512])
                        b.ldma(yfb_tm[i * 128:(i + 1) * 128,
                                      n * 512:(n + 1) * 512], gtile[:, 0:512])
    if upto <= 6:
        b.finish()
        return

    kwb = View(C.WBh, [8, T])
    b2sb = View(C.WBh, [8, 1024], base=T)
    v = b.to(nc.vector)
    v.tensor_copy(kwb[0:8, 0:T], el8[0:8, 0:T])
    b.ldma(b2sb[0:8, 0:1024], b216[:, :])
    yfbs = View(stage6, [128, 1024])
    for i in range(NT):
        t = b.to(nc.tensor)
        t.matmul(pb[2][:, 0:512], kwb[0:8, i * 128:(i + 1) * 128],
                 b2sb[0:8, 0:512], start=True, stop=True)
        t.matmul(pb[3][:, 0:512], kwb[0:8, i * 128:(i + 1) * 128],
                 b2sb[0:8, 512:1024], start=True, stop=True)
        b.ldma(af[:, 0:1024], x_in[i * 128:(i + 1) * 128, :])
        b.ldma(af[:, 1024:2048], s_dram[i * 128:(i + 1) * 128, :])
        b.ldma(yfbs[:, 0:1024], yfb_tm[i * 128:(i + 1) * 128, :])
        v = b.to(nc.vector)
        acc_f = af[:, 2048:3072]
        yv = af[:, 3072:4096]
        v.tensor_copy(acc_f, C.acc[:, i, 0:1024])
        v.tensor_add(af[:, 2048:2560], af[:, 2048:2560], pb[2][:, 0:512])
        v.tensor_add(af[:, 2560:3072], af[:, 2560:3072], pb[3][:, 0:512])
        v.tensor_copy(yv, yfbs[:, 0:1024])
        v.tensor_sub(yv, yv, acc_f)
        v.tensor_scalar(yv, yv, C.ltm[:, i, 0:1], None, ALU.mult)
        v.tensor_add(acc_f, acc_f, yv)
        sc = b.to(nc.scalar)
        sc.activation(yv, acc_f, AF.Silu)
        v = b.to(nc.vector)
        v.tensor_mul(yv, yv, af[:, 1024:2048])
        v.tensor_add(af[:, 0:1024], af[:, 0:1024], yv)
        b.ldma(x4_out[i * 128:(i + 1) * 128, :], af[:, 0:1024])
    b.finish()


def run_layer3(x_full, weights, upto=99):
    with ExitStack() as stack:
        nc = bass.Bass("TRN2", target_bir_lowering=False)
        b = Build(nc, stack)
        C = alloc_common(b)
        emit_common_init(b, C)
        build_layer3(b, C, upto=upto)
        in_maps = []
        for c in range(NCORES):
            m = {"x": np.ascontiguousarray(x_full[c])}
            for k in L3_KEYS:
                m[k] = np.ascontiguousarray(weights[k])
            m.update(b.consts)
            cm = np.zeros((8, 1), np.float32)
            cm[:c, 0] = 1.0
            m["coremask"] = cm
            in_maps.append(m)
        res = run_bass_kernel_spmd(nc, in_maps, core_ids=list(range(NCORES)))
        return np.stack([res.results[c]["x4"] for c in range(NCORES)])


def kernel(**inputs):
    x = np.ascontiguousarray(np.asarray(inputs["L0_attn_x"], dtype=np.float32))
    x = run_layer0(x, inputs)
    x = run_layer1(x, inputs)
    x = run_layer2(x, inputs)
    x = run_layer3(x, inputs)
    return np.ascontiguousarray(x.astype(np.float32, copy=False))


# version 03
# speedup vs baseline: 1.0177x; 1.0177x over previous
import math
import numpy as np
import ml_dtypes
from contextlib import ExitStack
import concourse.bass as bass
import concourse.mybir as mybir
from concourse.bass_utils import run_bass_kernel_spmd

F32 = mybir.dt.float32
BF16 = mybir.dt.bfloat16
I32 = mybir.dt.int32
AF = mybir.ActivationFunctionType
ALU = mybir.AluOpType
AX = mybir.AxisListType.X

T = 4096
D = 1024
NT = T // 128
NCORES = 8
ISQ32 = 1.0 / math.sqrt(32.0)


class View:

    def __init__(self, h, shape, base=0):
        if isinstance(h, View):
            base = base + h.base
            h = h.h
        self.h = h
        self.shape = list(shape)
        self.base = base
        st = [1] * len(shape)
        for i in range(len(shape) - 2, 0, -1):
            st[i] = st[i + 1] * shape[i + 1]
        width = 1
        for d in h.shape[1:]:
            width *= int(d)
        st[0] = width
        self.st = st

    def ap(self):
        return self[tuple(slice(0, d) for d in self.shape)]

    def __getitem__(self, idx):
        if not isinstance(idx, tuple):
            idx = (idx,)
        assert len(idx) == len(self.shape)
        off = self.base
        dims = []
        for k, ix in enumerate(idx):
            if isinstance(ix, int):
                off += ix * self.st[k]
            else:
                i0 = ix.start or 0
                i1 = ix.stop if ix.stop is not None else self.shape[k]
                step = ix.step or 1
                off += i0 * self.st[k]
                dims.append([self.st[k] * step, (i1 - i0 + step - 1) // step])
        if not dims:
            dims = [[1, 1]]
        return bass.AP(self.h, off, dims)


def np_bf16(x):
    return np.ascontiguousarray(np.asarray(x).astype(ml_dtypes.bfloat16))


def rope_tables_np():
    L = 32
    inv_freq = 1.0 / 10000.0 ** (np.arange(0, L, 2, dtype=np.float32) / L)
    tt = np.arange(T, dtype=np.float32)
    freqs = np.outer(tt, inv_freq)
    cos = np.repeat(np.cos(freqs), 2, axis=-1)[:, :L].astype(np.float32)
    sin = np.repeat(np.sin(freqs), 2, axis=-1)[:, :L].astype(np.float32)
    A = np.tile(cos, (1, 16))
    Bp = sin.copy()
    Bp[:, 0::2] = -Bp[:, 0::2]
    B = np.tile(Bp, (1, 16))
    return A, B


class EngProxy:

    _OPS = {"tensor_copy", "tensor_tensor", "tensor_add", "tensor_sub",
            "tensor_mul", "tensor_max", "tensor_scalar", "tensor_single_scalar",
            "tensor_scalar_add", "tensor_scalar_sub", "tensor_scalar_mul",
            "tensor_scalar_max", "tensor_scalar_min", "tensor_relu",
            "tensor_reduce", "reduce_max", "reduce_sum", "scalar_tensor_tensor",
            "tensor_tensor_scan", "reciprocal", "select", "memset", "memzero",
            "activation", "copy", "mul", "add", "sign", "square", "sqrt"}

    def __init__(self, eng, padfn=None):
        self._eng = eng
        self._padfn = padfn

    def __getattr__(self, name):
        fn = getattr(self._eng, name)
        if name not in EngProxy._OPS:
            return fn

        def wrapped(*args, **kw):
            r = fn(*args, **kw)
            out = args[0] if args else None
            pad = kw.get("accum_out") is not None
            if not pad and isinstance(out, bass.AP):
                free = 1
                for st_, cnt in list(out.ap)[1:]:
                    free *= cnt
                pad = len(list(out.ap)) == 1 or free < 128
            if pad and self._padfn is not None:
                self._padfn()
            return r
        return wrapped


class Build:
    def __init__(self, nc, stack):
        nc.detect_race_conditions = False
        self.nc = nc
        self.stack = stack
        self.sh = {}
        self.sv = {}
        self.consts = {}
        self.cur = None
        self.pend = []
        self.pb = [nc.alloc_psum_tensor(f"pb{i}", [128, 512], F32)
                   for i in range(6)]
        self.last_ins = {}
        self.dirty = {}
        nc.push_inst_callback(self._on_inst)
        self.padA = nc.alloc_sbuf_tensor("padA", [128, 896], F32)
        self.padV = nc.alloc_sbuf_tensor("padV", [128, 1], F32)
        self.padS = nc.alloc_sbuf_tensor("padS", [128, 640], F32)
        self.padG = nc.alloc_sbuf_tensor("padG", [128, 448], F32)
        self.vec = EngProxy(nc.vector,
                            lambda: nc.vector.reduce_sum(self.padV[:, 0:1],
                                                         self.padA[:, 0:896],
                                                         axis=AX))
        self.sca = EngProxy(nc.scalar,
                            lambda: nc.scalar.copy(self.padS[:, 0:640],
                                                   self.padA[:, 0:640]))
        self.gps = EngProxy(nc.gpsimd,
                            lambda: nc.gpsimd.tensor_copy(self.padG[:, 0:448],
                                                          self.padA[:, 0:448]))

    def _on_inst(self, ins):
        try:
            eng = ins.engine
        except Exception:
            return
        if eng == mybir.EngineType.SP:
            return
        nm = type(ins).__name__
        if "DMA" in nm or "Sem" in nm or "Event" in nm or nm.endswith("Nop"):
            return
        if not getattr(ins, "outs", None):
            return
        self.last_ins[eng] = ins
        self.dirty[eng] = True

    def sem(self, name):
        if name not in self.sh:
            self.sh[name] = self.stack.enter_context(self.nc.semaphore(name))
            self.sv[name] = 0
        return self.sh[name]

    def inc(self, instr, name, n=1):
        h = self.sem(name)
        self.sv[name] += n
        instr.then_inc(h, n)
        return self.sv[name]

    def wait(self, eng, name, val=None):
        h = self.sem(name)
        eng.wait_ge(h, self.sv[name] if val is None else val)

    def to(self, eng):
        if self.cur is eng and not self.pend:
            return eng
        if self.cur is not None and self.cur is not eng:
            self.inc(self.cur.drain(), "ch", 1)
            self.wait(eng, "ch")
        for qn in self.pend:
            self.wait(eng, qn)
        self.pend = []
        self.cur = eng
        if eng is self.nc.vector:
            return self.vec
        if eng is self.nc.scalar:
            return self.sca
        if eng is self.nc.gpsimd:
            return self.gps
        return eng

    def _dma(self, q, qn, dst, src, transpose_load=False, **kw):
        if transpose_load:
            ins = q.dma_start_transpose(dst, src)
        else:
            ins = q.dma_start(dst, src, **kw)
        self.inc(ins, qn, 16)
        if qn not in self.pend:
            self.pend.append(qn)

    def ldma(self, dst, src, **kw):
        self.to(self.nc.sync)
        self._dma(self.nc.sync, "sld", dst, src, **kw)

    def ldma_pre(self, dst, src, sem_name):
        self.to(self.nc.sync)
        ins = self.nc.sync.dma_start(dst, src)
        return self.inc(ins, sem_name, 16)

    def tdma(self, dst, src):
        self.to(self.nc.sync)
        self._dma(self.nc.sync, "sld", dst, src, transpose_load=True)

    def gdma(self, dst, src, **kw):
        self.to(self.nc.gpsimd)
        self._dma(self.nc.gpsimd, "gld", dst, src, **kw)

    def finish(self):
        for qn in ("sld", "ast", "gld"):
            if qn in self.sh:
                self.wait(self.nc.sync, qn)
        self.pend = []


def emit_small_transpose(b, dst, src_dram, P, F, stg, psum, idt16):
    nc = b.nc
    b.ldma(stg[0:P, 0:F], src_dram[0:P, 0:F])
    t = b.to(nc.tensor)
    t.transpose(psum[0:F, 0:P], stg[0:P, 0:F], idt16[0:P, 0:P])
    v = b.to(nc.vector)
    v.tensor_copy(dst[0:F, 0:P], psum[0:F, 0:P])


def emit_T64(b, dst, dram, r0, c0, stg, psum, idt16):
    nc = b.nc
    b.ldma(stg[0:128, 0:64], dram[r0:r0 + 128, c0:c0 + 64])
    t = b.to(nc.tensor)
    t.transpose(psum[0:64, 0:128], stg[0:128, 0:64], idt16[0:128, 0:128])
    v = b.to(nc.vector)
    v.tensor_copy(dst[0:64, 0:128], psum[0:64, 0:128])


def emit_identity(b, idt):
    g = b.to(b.nc.gpsimd)
    g.memset(idt[:, :], 1.0)
    n = idt.shape[1]
    g.affine_select(idt[:, :], idt[:, :], [[-1, n]], ALU.is_ge, 0.0,
                    channel_multiplier=1)
    g.affine_select(idt[:, :], idt[:, :], [[1, n]], ALU.is_ge, 0.0,
                    channel_multiplier=-1)


def emit_layernorm(b, x_src, xn_tm, fst, bst, stats, epsc, xlo_tm=None):
    nc = b.nc
    for i in range(NT):
        sl = i % 2
        xt = fst[:, sl, 0:1024]
        sqt = fst[:, sl, 1024:2048]
        b.ldma(xt, x_src[i * 128:(i + 1) * 128, :])
        v = b.to(nc.vector)
        v.reduce_sum(stats[:, 0:1], xt, axis=AX)
        sc = b.to(nc.scalar)
        sc.activation(sqt, xt, AF.Square, accum_out=stats[:, 1:2])
        v = b.to(nc.vector)
        v.tensor_scalar(stats[:, 2:3], stats[:, 0:1], 1.0 / D, None, ALU.mult)
        v.tensor_scalar(stats[:, 3:4], stats[:, 1:2], 1.0 / D, None, ALU.mult)
        v.tensor_tensor(stats[:, 4:5], stats[:, 2:3], stats[:, 2:3], ALU.mult)
        v.tensor_sub(stats[:, 3:4], stats[:, 3:4], stats[:, 4:5])
        sc = b.to(nc.scalar)
        sc.activation(stats[:, 4:5], stats[:, 3:4], AF.Sqrt, bias=epsc[:, 0:1])
        v = b.to(nc.vector)
        v.reciprocal(stats[:, 4:5], stats[:, 4:5])
        v.tensor_tensor(stats[:, 5:6], stats[:, 2:3], stats[:, 4:5], ALU.mult)
        v.tensor_scalar(stats[:, 5:6], stats[:, 5:6], -1.0, None, ALU.mult)
        sc = b.to(nc.scalar)
        sc.activation(bst[:, sl, :], xt, AF.Identity,
                      bias=stats[:, 5:6], scale=stats[:, 4:5])
        b.ldma(xn_tm[i * 128:(i + 1) * 128, :], bst[:, sl, :])
        if xlo_tm is not None:
            sc = b.to(nc.scalar)
            sc.activation(sqt, xt, AF.Identity,
                          bias=stats[:, 5:6], scale=stats[:, 4:5])
            v = b.to(nc.vector)
            v.tensor_copy(fst[:, 1 - sl, 0:1024], bst[:, sl, :])
            v.tensor_sub(sqt, sqt, fst[:, 1 - sl, 0:1024])
            v.tensor_copy(bst[:, 1 - sl, 0:1024], sqt)
            b.ldma(xlo_tm[i * 128:(i + 1) * 128, :], bst[:, 1 - sl, :])


def load_xnT(b, xn_tm, xnT):
    for f in range(8):
        b.tdma(xnT[:, f, 0:T], xn_tm[:, f * 128:(f + 1) * 128])



class NS:
    pass


def alloc_common(b):
    nc = b.nc
    C = NS()
    C.AB1h = nc.alloc_sbuf_tensor("AB1", [128, 32768], BF16)
    C.AB2h = nc.alloc_sbuf_tensor("AB2", [128, 32768], BF16)
    C.AF3h = nc.alloc_sbuf_tensor("AF3", [128, 8192], F32)
    C.WBh = nc.alloc_sbuf_tensor("WB", [128, 8192], BF16)
    C.idt32 = nc.alloc_sbuf_tensor("idt32", [128, 128], F32)
    C.idt16 = nc.alloc_sbuf_tensor("idt16", [128, 128], BF16)
    C.pt16 = nc.alloc_psum_tensor("pt16", [128, 256], BF16)
    C.stg16 = nc.alloc_sbuf_tensor("stg16", [128, 64], BF16)
    C.cpool = nc.alloc_sbuf_tensor("cpool", [128, 8], F32)
    C.stats = nc.alloc_sbuf_tensor("stats", [128, 8], F32)
    C.smallA = nc.alloc_sbuf_tensor("smallA", [128, 64], F32)
    C.lnrow = nc.alloc_sbuf_tensor("lnrow", [1, 512], F32)
    C.onesc = nc.alloc_sbuf_tensor("onesc", [128, 64], F32)
    C.bis = nc.alloc_sbuf_tensor("bis", [8, 8], F32)
    C.sb8 = nc.alloc_sbuf_tensor("sb8", [8, 128], F32)
    C.coeff = nc.alloc_sbuf_tensor("coeff", [128, NT, 8], F32)
    C.ltm = nc.alloc_sbuf_tensor("ltm", [128, NT, 8], F32)
    C.htm = nc.alloc_sbuf_tensor("htm", [128, NT, 8], F32)
    C.xnT = View(C.AB1h, [128, 8, T])
    C.fst = View(C.AF3h, [128, 2, 2048])
    C.af = View(C.AF3h, [128, 8192])
    C.em = View(C.AF3h, [8, 4096])
    C.emt = View(C.AF3h, [8, 4096], base=4096)
    C.wslot = View(C.WBh, [128, 2, 8, 512])
    C.bst = View(C.AB2h, [128, 2, 1024], base=30720)
    C.acc = View(C.AB2h, [128, NT, 1024])
    C.xn_tm = nc.dram_tensor("xn_tm", [T, D], BF16)
    return C


def emit_common_init(b, C):
    nc = b.nc
    g = b.to(nc.gpsimd)
    g.memset(b.padA[:, :], 0.0)
    emit_identity(b, C.idt32)
    emit_identity(b, C.idt16)
    g = b.to(nc.gpsimd)
    g.memset(C.onesc[:, :], 1.0)
    g.memset(C.cpool[:, 0:1], 1e-5)
    g.memset(C.cpool[:, 1:2], -math.pi)


def my_fori(nc, start, end, engines, body_fn):
    loop_id = nc.next_id()
    name = f"myfori_{loop_id}"
    registers = nc.alloc_registers(f"{name}_i", engines=engines)
    nc.regs_mov(registers, start)
    nc.br(name + "_loop", engines=engines)
    with nc.body(name + "_loop", valid_engines=engines, nocache=True):
        iv = nc.snap(registers, min_val=start, max_val=end - 1)
        body_fn(iv)
        nc.regs_alu(registers, registers, 1, op=ALU.add)
        nc.br_lt(registers, end, on_true=name + "_loop", on_false=name + "_end",
                 engines=engines)
    nc.switch_bb(name + "_end")
    for h in registers:
        nc.free_register(h)
    try:
        for h in iv.val:
            nc.free_register(h)
    except Exception:
        pass


def build_layer0(b, C, upto=99, dbg=False, dbg2=False, x_in_t=None, x_out_t=None):
    nc = b.nc

    def snap(name, src_ap, shape, dt):
        if not dbg2:
            return
        od = nc.dram_tensor(name, shape, dt, kind="ExternalOutput")
        b.ldma(od.ap(), src_ap)

    x_in = x_in_t if x_in_t is not None else \
        nc.dram_tensor("x", [T, D], F32, kind="ExternalInput")
    cw = nc.dram_tensor("L0_attn_c_attn_w", [3 * D, D], F32, kind="ExternalInput")
    cbv = nc.dram_tensor("L0_attn_c_attn_b", [3 * D], F32, kind="ExternalInput")
    q2w = nc.dram_tensor("L0_attn_q2l_w", [32, 64], F32, kind="ExternalInput")
    q2b = nc.dram_tensor("L0_attn_q2l_b", [32], F32, kind="ExternalInput")
    kv2w = nc.dram_tensor("L0_attn_kv2l_w", [32, 64], F32, kind="ExternalInput")
    kv2b = nc.dram_tensor("L0_attn_kv2l_b", [32], F32, kind="ExternalInput")
    l2ow = nc.dram_tensor("L0_attn_l2o_w", [64, 32], F32, kind="ExternalInput")
    l2ob = nc.dram_tensor("L0_attn_l2o_b", [64], F32, kind="ExternalInput")
    wqkw = nc.dram_tensor("L0_attn_wqk_w", [32, 32], F32, kind="ExternalInput")
    wqkb = nc.dram_tensor("L0_attn_wqk_b", [32], F32, kind="ExternalInput")
    cpw = nc.dram_tensor("L0_attn_cproj_w", [D, D], F32, kind="ExternalInput")
    cpb = nc.dram_tensor("L0_attn_cproj_b", [D], F32, kind="ExternalInput")
    gwt = nc.dram_tensor("L0_ffn_gate_w", [1, D, 8], F32, kind="ExternalInput")
    exps = nc.dram_tensor("L0_ffn_experts", [8, D, D], F32, kind="ExternalInput")
    ctab = nc.dram_tensor("ropeA", [T, 512], BF16, kind="ExternalInput")
    stabt = nc.dram_tensor("ropeB", [T, 512], BF16, kind="ExternalInput")
    A, Bt = rope_tables_np()
    b.consts["ropeA"] = np_bf16(A)
    b.consts["ropeB"] = np_bf16(Bt)
    x1_out = x_out_t if x_out_t is not None else \
        nc.dram_tensor("x1", [T, D], F32, kind="ExternalOutput")

    cw16 = nc.dram_tensor("cw16", [3 * D, D], BF16)
    cb16 = nc.dram_tensor("cb16", [48, 64], BF16)
    q2w16 = nc.dram_tensor("q2w16", [32, 64], BF16)
    kv2w16 = nc.dram_tensor("kv2w16", [32, 64], BF16)
    wqkw16 = nc.dram_tensor("wqkw16", [32, 32], BF16)
    cpw16 = nc.dram_tensor("cpw16", [D, D], BF16)
    l2ow16 = nc.dram_tensor("l2ow16", [64, 32], BF16)
    gw16 = nc.dram_tensor("gw16", [D, 8], BF16)
    ex16 = nc.dram_tensor("ex16", [8, D, D], BF16)
    qrope_tm = nc.dram_tensor("qrope_tm", [T, 512], BF16)
    krope_tm = nc.dram_tensor("krope_tm", [T, 512], BF16)
    vlat_tm = nc.dram_tensor("vlat_tm", [T, 512], BF16)
    latb_dram = nc.dram_tensor("latb_dram", [48, 32], F32)
    sb8_dram = nc.dram_tensor("sb8_dram", [8, 128], F32)
    s_dram = nc.dram_tensor("s_dram", [T, D], F32)

    latb_bc = nc.alloc_sbuf_tensor("latb_bc", [128, 1536], F32)
    sbias_bc = nc.alloc_sbuf_tensor("sbias_bc", [128, 1024], F32)
    latb32 = nc.alloc_sbuf_tensor("latb32", [32, 48], F32)
    q2T = nc.alloc_sbuf_tensor("q2T", [64, 32], BF16)
    kv2T = nc.alloc_sbuf_tensor("kv2T", [64, 32], BF16)
    bseg = nc.alloc_sbuf_tensor("bseg", [64, 48], BF16)
    wqkT = nc.alloc_sbuf_tensor("wqkT", [32, 32], BF16)
    wqkT4 = nc.alloc_sbuf_tensor("wqkT4", [128, 32], BF16)
    wqkb_sb = nc.alloc_sbuf_tensor("wqkb_sb", [32, 1], F32)
    l2o_sb = nc.alloc_sbuf_tensor("l2o_sb", [64, 32], BF16)
    masks = nc.alloc_sbuf_tensor("masks", [128, 4, 512], BF16)

    cstg = View(C.AB2h, [128, 2, 1024])
    ct16 = View(C.AB2h, [128, 2, 512], base=2048)
    st16 = View(C.AB2h, [128, 2, 512], base=3072)
    ost = View(C.AB2h, [128, 2, 3, 512], base=4096)
    wlatT = View(C.AB2h, [128, 8, 1536], base=8192)
    MoT = View(C.WBh, [128, 4, 1024])
    ptile = View(C.AB1h, [128, 2, 512])
    va = View(C.AB1h, [128, NT, 33], base=1024)
    qT = View(C.AB1h, [128, 4096], base=2112)
    kT = View(C.AB1h, [128, 4096], base=6208)
    qfkT = View(C.AB1h, [128, 4096], base=10304)
    ylatT = View(C.AB1h, [128, 4, T], base=14400)

    pb = b.pb

    b.gdma(cw16[:, :], cw[:, :])
    b.gdma(cb16.ap(), bass.AP(cbv, 0, [[64, 48], [1, 64]]))
    b.gdma(q2w16[:, :], q2w[:, :])
    b.gdma(kv2w16[:, :], kv2w[:, :])
    b.gdma(wqkw16[:, :], wqkw[:, :])
    b.gdma(cpw16[:, :], cpw[:, :])
    b.gdma(l2ow16[:, :], l2ow[:, :])
    b.gdma(gw16[:, :], gwt[0, :, :])
    for e in range(8):
        b.gdma(ex16[e, :, :], exps[e, :, :])
    g = b.to(nc.gpsimd)
    for mi in range(4):
        g.memset(masks[:, mi, :], 1.0)
        g.affine_select(masks[:, mi, :], masks[:, mi, :], [[1, 512]],
                        ALU.is_ge, 0.0, base=-(mi * 128), channel_multiplier=-1)

    emit_layernorm(b, x_in, C.xn_tm, C.fst, C.bst, C.stats, C.cpool)
    load_xnT(b, C.xn_tm, C.xnT)
    if upto <= 1:
        b.finish()
        return

    emit_small_transpose(b, q2T, q2w16, 32, 64, C.stg16, C.pt16, C.idt16)
    emit_small_transpose(b, kv2T, kv2w16, 32, 64, C.stg16, C.pt16, C.idt16)
    emit_small_transpose(b, bseg, cb16, 48, 64, C.stg16, C.pt16, C.idt16)
    for s in range(48):
        b.ldma(cstg[0:64, (s % 2), :], cw16[s * 64:(s + 1) * 64, :])
        lat = q2T if s < 16 else kv2T
        for m in range(8):
            t = b.to(nc.tensor)
            t.matmul(pb[0][:, 0:32], cstg[0:64, (s % 2), m * 128:(m + 1) * 128],
                     lat[:, :], start=True, stop=True)
            v = b.to(nc.vector)
            v.tensor_copy(wlatT[:, m, s * 32:(s + 1) * 32], pb[0][:, 0:32])
    t = b.to(nc.tensor)
    t.matmul(pb[1][0:32, 0:16], q2T[:, :], bseg[:, 0:16], start=True, stop=True)
    t.matmul(pb[1][0:32, 16:48], kv2T[:, :], bseg[:, 16:48], start=True, stop=True)
    b.ldma(C.smallA[0:32, 0:1], bass.AP(q2b, 0, [[1, 32], [1, 1]]))
    b.ldma(C.smallA[0:32, 1:2], bass.AP(kv2b, 0, [[1, 32], [1, 1]]))
    v = b.to(nc.vector)
    v.tensor_scalar(latb32[:, 0:16], pb[1][0:32, 0:16], C.smallA[0:32, 0:1], None, ALU.add)
    v.tensor_scalar(latb32[:, 16:48], pb[1][0:32, 16:48], C.smallA[0:32, 1:2], None, ALU.add)
    t = b.to(nc.tensor)
    t.transpose(pb[2][0:48, 0:32], latb32[:, :], C.idt32[0:32, 0:32])
    v = b.to(nc.vector)
    v.tensor_copy(C.smallA[0:48, 16:48], pb[2][0:48, 0:32])
    b.ldma(latb_dram[:, :], C.smallA[0:48, 16:48])
    b.ldma(latb_bc[:, :], bass.AP(latb_dram, 0, [[0, 128], [1, 1536]]))
    if upto <= 2:
        b.finish()
        return

    for i in range(NT):
        sl = i % 2
        b.ldma(ct16[:, sl, :], ctab[i * 128:(i + 1) * 128, :])
        b.ldma(st16[:, sl, :], stabt[i * 128:(i + 1) * 128, :])
        t = b.to(nc.tensor)
        for n in range(3):
            for k in range(8):
                t.matmul(pb[n][:, :], C.xnT[:, k, i * 128:(i + 1) * 128],
                         wlatT[:, k, n * 512:(n + 1) * 512],
                         start=(k == 0), stop=(k == 7))
        v = b.to(nc.vector)
        qf, kf = C.af[:, 0:512], C.af[:, 512:1024]
        tmp, t2 = C.af[:, 1024:1536], C.af[:, 1536:2048]
        for nn in (0, 1):
            dst_f = qf if nn == 0 else kf
            doff = nn * 512
            v.tensor_add(dst_f, pb[nn][:, :], latb_bc[:, nn * 512:(nn + 1) * 512])
            v.tensor_copy(bass.AP(C.AF3h, 1024, [[8192, 128], [2, 256]]),
                          bass.AP(C.AF3h, doff + 1, [[8192, 128], [2, 256]]))
            v.tensor_copy(bass.AP(C.AF3h, 1024 + 1, [[8192, 128], [2, 256]]),
                          bass.AP(C.AF3h, doff, [[8192, 128], [2, 256]]))
            v.tensor_mul(dst_f, dst_f, ct16[:, sl, :])
            v.tensor_mul(tmp, tmp, st16[:, sl, :])
            v.tensor_add(t2, dst_f, tmp)
            v.tensor_copy(ost[:, sl, nn, :], t2)
        v.tensor_add(ost[:, sl, 2, :], pb[2][:, :], latb_bc[:, 1024:1536])
        b.ldma(qrope_tm[i * 128:(i + 1) * 128, :], ost[:, sl, 0, :])
        b.ldma(krope_tm[i * 128:(i + 1) * 128, :], ost[:, sl, 1, :])
        b.ldma(vlat_tm[i * 128:(i + 1) * 128, :], ost[:, sl, 2, :])
    if upto <= 3:
        b.finish()
        return

    gwsb = View(C.WBh, [128, 8, 8])
    b.ldma(gwsb[:, 0:8, 0:8], gw16.rearrange("(k p) e -> p k e", p=128))
    for i in range(NT):
        t = b.to(nc.tensor)
        for k in range(8):
            t.matmul(pb[i % 2][:, 0:8], C.xnT[:, k, i * 128:(i + 1) * 128],
                     gwsb[:, k, 0:8], start=(k == 0), stop=(k == 7))
        v = b.to(nc.vector)
        v.tensor_scalar_max(C.ltm[:, i, :], pb[i % 2][:, 0:8], 1e-6)
    sc = b.to(nc.scalar)
    sc.activation(C.ltm[:, 0:NT, 0:8], C.ltm[:, 0:NT, 0:8], AF.Ln)
    for i in range(NT):
        t = b.to(nc.tensor)
        t.transpose(pb[i % 2][0:8, 0:128], C.ltm[:, i, :], C.idt32[:, :])
        v = b.to(nc.vector)
        v.tensor_copy(C.em[0:8, i * 128:(i + 1) * 128], pb[i % 2][0:8, 0:128])
    snap("z_em0", C.em[0:8, 0:T], [8, T], F32)
    for it in range(8):
        sc = b.to(nc.scalar)
        sc.activation(C.emt[0:8, 0:T], C.em[0:8, 0:T], AF.Exp, accum_out=C.bis[0:8, 0:1])
        sc.activation(C.bis[0:8, 1:2], C.bis[0:8, 0:1], AF.Ln)
        v = b.to(nc.vector)
        v.tensor_scalar(C.em[0:8, 0:T], C.em[0:8, 0:T], C.bis[0:8, 1:2], None, ALU.subtract)
        sc = b.to(nc.scalar)
        sc.activation(C.emt[0:8, 0:T], C.em[0:8, 0:T], AF.Exp)
        for c in range(8):
            t = b.to(nc.tensor)
            t.matmul(pb[0][0:1, 0:512], C.onesc[0:8, 0:1],
                     C.emt[0:8, c * 512:(c + 1) * 512], start=True, stop=True)
            sc = b.to(nc.scalar)
            sc.activation(C.lnrow[0:1, 0:512], pb[0][0:1, 0:512], AF.Ln)
            t = b.to(nc.tensor)
            t.matmul(pb[1][0:8, 0:512], C.onesc[0:1, 0:8],
                     C.lnrow[0:1, 0:512], start=True, stop=True)
            v = b.to(nc.vector)
            v.tensor_sub(C.em[0:8, c * 512:(c + 1) * 512],
                         C.em[0:8, c * 512:(c + 1) * 512], pb[1][0:8, 0:512])
    sc = b.to(nc.scalar)
    sc.activation(C.em[0:8, 0:T], C.em[0:8, 0:T], AF.Exp)
    snap("z_gates", C.em[0:8, 0:T], [8, T], F32)
    v = b.to(nc.vector)
    v.memset(C.bis[0:8, 0:1], 0.0)
    v.memset(C.bis[0:8, 1:2], 1.0)
    for it in range(40):
        v.tensor_tensor(C.bis[0:8, 2:3], C.bis[0:8, 0:1], C.bis[0:8, 1:2], ALU.add)
        v.tensor_scalar(C.bis[0:8, 2:3], C.bis[0:8, 2:3], 0.5, None, ALU.mult)
        v.tensor_scalar(C.emt[0:8, 0:T], C.em[0:8, 0:T], C.bis[0:8, 2:3], None,
                        ALU.is_ge, ALU.add, accum_out=C.bis[0:8, 3:4])
        v.tensor_scalar(C.bis[0:8, 4:5], C.bis[0:8, 3:4], 512.0, None, ALU.is_ge)
        v.tensor_sub(C.bis[0:8, 5:6], C.bis[0:8, 2:3], C.bis[0:8, 0:1])
        v.tensor_tensor(C.bis[0:8, 5:6], C.bis[0:8, 5:6], C.bis[0:8, 4:5], ALU.mult)
        v.tensor_add(C.bis[0:8, 0:1], C.bis[0:8, 0:1], C.bis[0:8, 5:6])
        v.tensor_scalar(C.bis[0:8, 6:7], C.bis[0:8, 4:5], -1.0, 1.0, ALU.mult, ALU.add)
        v.tensor_sub(C.bis[0:8, 5:6], C.bis[0:8, 2:3], C.bis[0:8, 1:2])
        v.tensor_tensor(C.bis[0:8, 5:6], C.bis[0:8, 5:6], C.bis[0:8, 6:7], ALU.mult)
        v.tensor_add(C.bis[0:8, 1:2], C.bis[0:8, 1:2], C.bis[0:8, 5:6])
    snap("z_bis", C.bis[0:8, 0:8], [8, 8], F32)
    v = b.to(nc.vector)
    v.tensor_scalar(C.emt[0:8, 0:T], C.em[0:8, 0:T], C.bis[0:8, 0:1], None, ALU.is_ge)
    v.tensor_scalar(C.em[0:8, 0:T], C.em[0:8, 0:T], 0.5, None, ALU.is_gt)
    v.tensor_tensor(C.em[0:8, 0:T], C.em[0:8, 0:T], C.emt[0:8, 0:T], ALU.mult)
    snap("z_m", C.emt[0:8, 0:T], [8, T], F32)
    for src, dst in ((C.emt, C.ltm), (C.em, C.htm)):
        for i in range(NT):
            t = b.to(nc.tensor)
            t.transpose(pb[i % 2][0:128, 0:8], src[0:8, i * 128:(i + 1) * 128],
                        C.idt32[0:8, 0:8])
            v = b.to(nc.vector)
            v.tensor_copy(dst[:, i, :], pb[i % 2][0:128, 0:8])
    lw_flush = nc.dram_tensor("lw_flush", [128, NT * 8], F32)
    b.ldma(lw_flush.ap(), C.ltm[:, 0:NT, 0:8])
    b.ldma(lw_flush.ap(), C.htm[:, 0:NT, 0:8])
    snap("z_ltm", C.ltm[:, 0:NT, 0:8], [128, NT * 8], F32)
    snap("z_htm", C.htm[:, 0:NT, 0:8], [128, NT * 8], F32)
    v = b.to(nc.vector)
    for i in range(NT):
        nt = C.smallA[:, 2:3]
        v.memset(nt, 1.0)
        for e in range(7, -1, -1):
            v.tensor_mul(C.coeff[:, i, e:e + 1], C.htm[:, i, e:e + 1], nt)
            v.tensor_scalar(C.smallA[:, 3:4], C.ltm[:, i, e:e + 1], -1.0, 1.0,
                            ALU.mult, ALU.add)
            v.tensor_mul(nt, nt, C.smallA[:, 3:4])
    snap("z_cf4", C.coeff[:, 0:NT, 0:8], [128, NT * 8], F32)
    if upto <= 4:
        b.finish()
        return

    v = b.to(nc.vector)
    v.memset(C.acc[:, 0:NT, 0:1024], 0.0)
    for e in range(8):
        exr = ex16[e, :, :].rearrange("(k p) n -> p k n", p=128)
        for n in range(2):
            sl = (e * 2 + n) % 2
            b.ldma(C.wslot[:, sl, 0:8, 0:512], exr[:, 0:8, n * 512:(n + 1) * 512])
            for i in range(NT):
                t = b.to(nc.tensor)
                for k in range(8):
                    t.matmul(pb[i % 2][:, :], C.xnT[:, k, i * 128:(i + 1) * 128],
                             C.wslot[:, sl, k, 0:512], start=(k == 0), stop=(k == 7))
                v = b.to(nc.vector)
                if e == 0:
                    v.tensor_scalar(C.acc[:, i, n * 512:(n + 1) * 512],
                                    pb[i % 2][:, :], C.coeff[:, i, e:e + 1],
                                    None, ALU.mult)
                else:
                    v.scalar_tensor_tensor(C.acc[:, i, n * 512:(n + 1) * 512],
                                           pb[i % 2][:, :], C.coeff[:, i, e:e + 1],
                                           C.acc[:, i, n * 512:(n + 1) * 512],
                                           ALU.mult, ALU.add)
    if upto <= 5:
        b.finish()
        return

    emit_small_transpose(b, wqkT, wqkw16, 32, 32, C.stg16, C.pt16, C.idt16)
    v = b.to(nc.vector)
    for h4 in range(4):
        v.tensor_copy(wqkT4[h4 * 32:(h4 + 1) * 32, 0:32], wqkT[0:32, 0:32])
    b.ldma(wqkb_sb[:, :], bass.AP(wqkb, 0, [[1, 32], [1, 1]]))
    b.ldma(l2o_sb[:, :], l2ow16[:, :])
    cslc = View(C.WBh, [64, 512], base=6144)
    for h in range(16):
        for dn in range(2):
            for q4 in range(4):
                emit_T64(b, View(C.WBh, [64, 128], base=6144 + q4 * 128),
                         cpw16, dn * 512 + q4 * 128, h * 64, C.stg16, C.pt16, C.idt16)
            t = b.to(nc.tensor)
            t.matmul(pb[0][0:32, 0:512], l2o_sb[:, :], cslc[0:64, 0:512],
                     start=True, stop=True)
            v = b.to(nc.vector)
            v.tensor_copy(MoT[(h % 4) * 32:(h % 4) * 32 + 32, h // 4,
                              dn * 512:(dn + 1) * 512], pb[0][0:32, 0:512])
    l2ob16 = nc.alloc_sbuf_tensor("l2ob16", [64, 1], BF16)
    b.gdma(l2ob16[:, :], bass.AP(l2ob, 0, [[1, 64], [1, 1]]))
    for dc in range(8):
        for h in range(16):
            cslc2 = View(C.WBh, [64, 128], base=6144)
            emit_T64(b, cslc2, cpw16, dc * 128, h * 64, C.stg16, C.pt16, C.idt16)
            t = b.to(nc.tensor)
            t.matmul(pb[3][:, dc:dc + 1], cslc2[0:64, 0:128],
                     l2ob16[:, 0:1], start=(h == 0), stop=(h == 15))
    v = b.to(nc.vector)
    v.tensor_copy(C.smallA[:, 16:24], pb[3][:, 0:8])
    t = b.to(nc.tensor)
    t.transpose(pb[4][0:8, 0:128], C.smallA[:, 16:24], C.idt32[:, :])
    b.ldma(C.sb8[0:8, 0:128], bass.AP(cpb, 0, [[128, 8], [1, 128]]))
    v = b.to(nc.vector)
    v.tensor_add(C.sb8[0:8, 0:128], C.sb8[0:8, 0:128], pb[4][0:8, 0:128])
    b.ldma(sb8_dram[:, :], C.sb8[0:8, 0:128])
    b.ldma(sbias_bc[:, :], bass.AP(sb8_dram, 0, [[0, 128], [1, 1024]]))
    g = b.to(nc.gpsimd)
    g.memset(va[:, 0:NT, 32:33], 1.0)
    for gidx in range(4):
        b.tdma(qT[0:128, 0:T], qrope_tm[:, gidx * 128:(gidx + 1) * 128])
        b.tdma(kT[0:128, 0:T], krope_tm[:, gidx * 128:(gidx + 1) * 128])
        for h4 in range(4):
            p0 = h4 * 32
            for qc in range(8):
                t = b.to(nc.tensor)
                t.matmul(pb[4][0:32, 0:512], wqkT4[p0:p0 + 32, 0:32],
                         qT[p0:p0 + 32, qc * 512:(qc + 1) * 512],
                         start=True, stop=True, tile_position=(p0, 0))
                sc = b.to(nc.scalar)
                sc.activation(qfkT[p0:p0 + 32, qc * 512:(qc + 1) * 512],
                              pb[4][0:32, 0:512], AF.Identity, bias=wqkb_sb[:, 0:1])
        for h4 in range(4):
            h = gidx * 4 + h4
            p0 = h4 * 32
            b.ldma(va[:, 0:NT, 0:32],
                   bass.AP(vlat_tm, h * 32, [[512, 128], [512 * 128, NT], [1, 32]]))
            for qb in range(8):
                kcn = 4 * (qb + 1)
                for kc in range(kcn):
                    t = b.to(nc.tensor)
                    t.matmul(pb[kc % 2][:, :], kT[p0:p0 + 32, kc * 128:(kc + 1) * 128],
                             qfkT[p0:p0 + 32, qb * 512:(qb + 1) * 512],
                             start=True, stop=True, tile_position=(p0, 0))
                    sc = b.to(nc.scalar)
                    sc.activation(ptile[:, kc % 2, :], pb[kc % 2][:, :], AF.Exp,
                                  scale=ISQ32)
                    cdiag = kc * 128 - qb * 512
                    if cdiag in (0, 128, 256, 384):
                        v = b.to(nc.vector)
                        v.tensor_mul(ptile[:, kc % 2, :], ptile[:, kc % 2, :],
                                     masks[:, cdiag // 128, :])
                    t = b.to(nc.tensor)
                    t.matmul(pb[2][0:33, 0:512], va[:, kc, 0:33], ptile[:, kc % 2, :],
                             start=(kc == 0), stop=(kc == kcn - 1))
                v = b.to(nc.vector)
                v.reciprocal(C.lnrow[0:1, 0:512], pb[2][32:33, 0:512])
                t = b.to(nc.tensor)
                t.matmul(pb[3][0:32, 0:512], C.onesc[0:1, 0:32], C.lnrow[0:1, 0:512],
                         start=True, stop=True)
                v = b.to(nc.vector)
                v.tensor_copy(C.af[0:32, 0:512], pb[3][0:32, 0:512])
                v.tensor_mul(ylatT[p0:p0 + 32, gidx, qb * 512:(qb + 1) * 512],
                             pb[2][0:32, 0:512], C.af[0:32, 0:512])
    for i in range(NT):
        for n in range(2):
            t = b.to(nc.tensor)
            for gc in range(4):
                t.matmul(pb[i % 2][:, :], ylatT[:, gc, i * 128:(i + 1) * 128],
                         MoT[:, gc, n * 512:(n + 1) * 512],
                         start=(gc == 0), stop=(gc == 3))
            v = b.to(nc.vector)
            v.tensor_add(C.af[:, 1024 + n * 512:1024 + (n + 1) * 512], pb[i % 2][:, :],
                         sbias_bc[:, n * 512:(n + 1) * 512])
        b.ldma(s_dram[i * 128:(i + 1) * 128, :], C.af[:, 1024:2048])
    if upto <= 6:
        b.finish()
        return

    for i in range(NT):
        b.ldma(C.af[:, 0:1024], x_in[i * 128:(i + 1) * 128, :])
        b.ldma(C.af[:, 1024:2048], s_dram[i * 128:(i + 1) * 128, :])
        sc = b.to(nc.scalar)
        sc.activation(C.af[:, 2048:3072], C.acc[:, i, :], AF.Silu)
        v = b.to(nc.vector)
        v.tensor_mul(C.af[:, 2048:3072], C.af[:, 2048:3072], C.af[:, 1024:2048])
        v.tensor_add(C.af[:, 3072:4096], C.af[:, 0:1024], C.af[:, 2048:3072])
        b.ldma(x1_out[i * 128:(i + 1) * 128, :], C.af[:, 3072:4096])
    if dbg:
        for nm, src, shp, dt in [("d_xn", C.xn_tm, [T, D], BF16),
                                 ("d_qr", qrope_tm, [T, 512], BF16),
                                 ("d_kr", krope_tm, [T, 512], BF16),
                                 ("d_vl", vlat_tm, [T, 512], BF16),
                                 ("d_s", s_dram, [T, D], F32)]:
            od = nc.dram_tensor(nm, shp, dt, kind="ExternalOutput")
            b.ldma(od[:, :], src[:, :])
        oacc = nc.dram_tensor("d_acc", [128, NT * 1024], BF16, kind="ExternalOutput")
        b.ldma(oacc[:, :], C.acc[:, 0:NT, 0:1024])
        ocf = nc.dram_tensor("d_cf", [128, NT * 8], F32, kind="ExternalOutput")
        b.ldma(ocf[:, :], C.coeff[:, 0:NT, 0:8])
    b.finish()



L0_KEYS = ["L0_attn_c_attn_w", "L0_attn_c_attn_b", "L0_attn_q2l_w",
           "L0_attn_q2l_b", "L0_attn_kv2l_w", "L0_attn_kv2l_b",
           "L0_attn_l2o_w", "L0_attn_l2o_b", "L0_attn_wqk_w",
           "L0_attn_wqk_b", "L0_attn_cproj_w", "L0_attn_cproj_b",
           "L0_ffn_gate_w", "L0_ffn_experts"]


def run_layer0(x_full, weights, upto=99):
    with ExitStack() as stack:
        nc = bass.Bass("TRN2", target_bir_lowering=False)
        b = Build(nc, stack)
        C = alloc_common(b)
        emit_common_init(b, C)
        build_layer0(b, C, upto=upto)
        in_maps = []
        for c in range(NCORES):
            m = {"x": np.ascontiguousarray(x_full[c])}
            for k in L0_KEYS:
                m[k] = np.ascontiguousarray(weights[k])
            m.update(b.consts)
            in_maps.append(m)
        res = run_bass_kernel_spmd(nc, in_maps, core_ids=list(range(NCORES)))
        return np.stack([res.results[c]["x1"] for c in range(NCORES)])





L1_KEYS = ["L1_attn_Wi", "L1_attn_bi", "L1_attn_Wh", "L1_attn_bh",
           "L1_ffn_gate_w", "L1_ffn_w_gate", "L1_ffn_w_up", "L1_ffn_w_down",
           "L1_ffn_sw_gate", "L1_ffn_sw_up", "L1_ffn_sw_down"]


def build_layer1(b, C, upto=99, dbg=False, x_in_t=None, x_out_t=None):
    nc = b.nc
    ET = mybir.EngineType

    x_in = x_in_t if x_in_t is not None else \
        nc.dram_tensor("x", [T, D], F32, kind="ExternalInput")
    Wi = nc.dram_tensor("L1_attn_Wi", [D, D], F32, kind="ExternalInput")
    bi = nc.dram_tensor("L1_attn_bi", [D], F32, kind="ExternalInput")
    Wh = nc.dram_tensor("L1_attn_Wh", [D, D], F32, kind="ExternalInput")
    bh = nc.dram_tensor("L1_attn_bh", [D], F32, kind="ExternalInput")
    gw = nc.dram_tensor("L1_ffn_gate_w", [8, D], F32, kind="ExternalInput")
    wg = nc.dram_tensor("L1_ffn_w_gate", [8, D, 1408], F32, kind="ExternalInput")
    wu = nc.dram_tensor("L1_ffn_w_up", [8, D, 1408], F32, kind="ExternalInput")
    wd = nc.dram_tensor("L1_ffn_w_down", [8, 1408, D], F32, kind="ExternalInput")
    swg = nc.dram_tensor("L1_ffn_sw_gate", [D, 2816], F32, kind="ExternalInput")
    swu = nc.dram_tensor("L1_ffn_sw_up", [D, 2816], F32, kind="ExternalInput")
    swd = nc.dram_tensor("L1_ffn_sw_down", [2816, D], F32, kind="ExternalInput")
    x2_out = x_out_t if x_out_t is not None else \
        nc.dram_tensor("x2", [T, D], F32, kind="ExternalOutput")

    wi16 = nc.dram_tensor("wi16", [D, D], BF16)
    wh16 = nc.dram_tensor("wh16", [D, D], BF16)
    gw16 = nc.dram_tensor("gw16l1", [8, D], BF16)
    wg16 = nc.dram_tensor("wg16", [8, D, 1408], BF16)
    wu16 = nc.dram_tensor("wu16", [8, D, 1408], BF16)
    wd16 = nc.dram_tensor("wd16", [8, 1408, D], BF16)
    swg16 = nc.dram_tensor("swg16", [D, 2816], BF16)
    swu16 = nc.dram_tensor("swu16", [D, 2816], BF16)
    swd16 = nc.dram_tensor("swd16", [2816, D], BF16)
    xi2_dram = nc.dram_tensor("xi2_dram", [8, 128, 512, 8], F32)
    sT_dram = nc.dram_tensor("sT_dram", [D, T], BF16)
    gt_dram = nc.dram_tensor("gt_dram", [8, 11, 128, T], BF16)
    gt2_dram = nc.dram_tensor("gt2_dram", [22, 128, T], BF16)

    pb = b.pb

    b.gdma(wi16[:, :], Wi[:, :])
    b.gdma(wh16[:, :], Wh[:, :])
    b.gdma(gw16[:, :], gw[:, :])
    for e in range(8):
        b.gdma(wg16[e, :, :], wg[e, :, :])
        b.gdma(wu16[e, :, :], wu[e, :, :])
        b.gdma(wd16[e, :, :], wd[e, :, :])
    b.gdma(swg16[:, :], swg[:, :])
    b.gdma(swu16[:, :], swu[:, :])
    b.gdma(swd16[:, :], swd[:, :])

    emit_layernorm(b, x_in, C.xn_tm, C.fst, C.bst, C.stats, C.cpool)
    load_xnT(b, C.xn_tm, C.xnT)
    if upto <= 1:
        b.finish()
        return

    wiT = View(C.WBh, [128, 8, 8, 128])
    for hc in range(8):
        for kf in range(8):
            b.tdma(wiT[:, hc, kf, 0:128],
                   wi16[hc * 128:(hc + 1) * 128, kf * 128:(kf + 1) * 128])
    bib = nc.alloc_sbuf_tensor("bib", [128, 8], F32)
    for hc in range(8):
        b.ldma(C.smallA[:, 0:1], bass.AP(bi, hc * 128, [[1, 128], [1, 1]]))
        b.ldma(C.smallA[:, 1:2], bass.AP(bh, hc * 128, [[1, 128], [1, 1]]))
        v = b.to(nc.vector)
        v.tensor_add(bib[:, hc:hc + 1], C.smallA[:, 0:1], C.smallA[:, 1:2])
    stage = View(C.AF3h, [128, 512, 8])
    for tb in range(8):
        for hc in range(8):
            t = b.to(nc.tensor)
            for kf in range(8):
                t.matmul(pb[hc % 2][:, :], wiT[:, hc, kf, 0:128],
                         C.xnT[:, kf, tb * 512:(tb + 1) * 512],
                         start=(kf == 0), stop=(kf == 7))
            sc = b.to(nc.scalar)
            sc.activation(bass.AP(C.AF3h, hc, [[8192, 128], [8, 512], [1, 1]]),
                          pb[hc % 2][:, :], AF.Identity, bias=bib[:, hc:hc + 1])
        b.ldma(xi2_dram[tb, :, :, :], stage[:, 0:512, 0:8])
    if upto <= 2:
        b.finish()
        return

    whT = View(C.AB2h, [128, 8, 8, 128])
    for m in range(8):
        for k in range(8):
            b.tdma(whT[:, k, m, 0:128],
                   wh16[m * 128:(m + 1) * 128, k * 128:(k + 1) * 128])
    shs = View(C.AB2h, [128, 2, 8, 512], base=8192)
    hsb = nc.alloc_sbuf_tensor("hsb", [128, 8], BF16)
    zf = nc.alloc_sbuf_tensor("zf", [128, 8], F32)
    v = b.to(nc.vector)
    v.memset(hsb[:, :], 0.0)
    mm = b.sem("r_mm")
    zz = b.sem("r_zz")
    hb = b.sem("r_hb")
    b.to(nc.sync)
    b.wait(nc.sync, "sld")
    ch = b.sem("ch")
    b.inc(nc.sync.nop(), "ch", 1)
    for eng in (nc.tensor, nc.vector, nc.scalar):
        eng.wait_ge(ch, b.sv["ch"])
    xslot = View(C.AF3h, [128, 2, 512, 8])
    sldv = b.sv["sld"]
    for blk in range(8):
        ins = nc.sync.dma_start(xslot[:, blk % 2, 0:512, 0:8],
                                xi2_dram[blk, :, :, :])
        sldv += 16
        ins.then_inc(b.sh["sld"], 16)
        b.sv["sld"] = sldv
        nc.tensor.wait_ge(b.sh["sld"], sldv)
        nc.vector.wait_ge(b.sh["sld"], sldv)
        base = blk * 512

        def rnn_body(i, blk=blk, base=base):
            nc.tensor.wait_ge(hb, base + i)
            nc.tensor.wait_ge(zz, base + i)
            ins = None
            for m in range(8):
                for k in range(8):
                    ins = nc.tensor.matmul(pb[5][:, m:m + 1], whT[:, k, m, 0:128],
                                           hsb[:, k:k + 1],
                                           start=(k == 0), stop=(k == 7))
            ins.then_inc(mm, 1)
            nc.vector.wait_ge(mm, base + i + 1)
            nc.vector.tensor_add(zf[:, :], pb[5][:, 0:8],
                                 bass.AP(C.AF3h, (blk % 2) * 4096 + i * 8,
                                         [[8192, 128], [1, 8]])).then_inc(zz, 1)
            nc.scalar.wait_ge(zz, base + i + 1)
            nc.scalar.activation(hsb[:, :], zf[:, :], AF.Tanh)
            nc.scalar.activation(
                bass.AP(C.AB2h, 8192 + (blk % 2) * 4096 + i,
                        [[32768, 128], [512, 8], [1, 1]]),
                zf[:, :], AF.Tanh).then_inc(hb, 1)

        my_fori(nc, 0, 512, [ET.PE, ET.DVE, ET.Activation], rnn_body)
        nc.sync.wait_ge(hb, base + 512)
        ins = nc.sync.dma_start(
            bass.AP(sT_dram, blk * 512, [[4096, 128], [4096 * 128, 8], [1, 512]]),
            shs[:, blk % 2, 0:8, 0:512])
        sldv += 16
        ins.then_inc(b.sh["sld"], 16)
        b.sv["sld"] = sldv
        nc.scalar.wait_ge(b.sh["sld"], sldv - 16)
    nc.sync.wait_ge(b.sh["sld"], sldv)
    b.cur = nc.sync
    b.pend = []
    if upto <= 3:
        b.finish()
        return

    gwsb8 = nc.alloc_sbuf_tensor("gwsb8", [8, D], BF16)
    gwT = View(C.WBh, [128, 8, 8])
    b.ldma(gwsb8[:, :], gw16[:, :])
    for kf in range(9):
        if kf < 8:
            t = b.to(nc.tensor)
            t.transpose(C.pt16[0:128, (kf % 2) * 128:(kf % 2) * 128 + 8],
                        gwsb8[0:8, kf * 128:(kf + 1) * 128], C.idt16[0:8, 0:8])
        if kf > 0:
            v = b.to(nc.vector)
            v.tensor_copy(gwT[:, kf - 1, 0:8],
                          C.pt16[0:128, ((kf - 1) % 2) * 128:((kf - 1) % 2) * 128 + 8])
    z8 = nc.alloc_sbuf_tensor("z8", [128, 8], F32)
    v = b.to(nc.vector)
    v.memset(z8[:, :], 0.0)
    af = C.af
    for i in range(NT):
        t = b.to(nc.tensor)
        for kf in range(8):
            t.matmul(pb[i % 2][:, 0:8], C.xnT[:, kf, i * 128:(i + 1) * 128],
                     gwT[:, kf, 0:8], start=(kf == 0), stop=(kf == 7))
        v = b.to(nc.vector)
        lg = af[:, 0:8]
        pr = af[:, 8:16]
        t2 = af[:, 16:24]
        m1 = af[:, 24:25]
        nm1 = af[:, 25:26]
        den = af[:, 26:27]
        m2 = af[:, 27:28]
        scl = af[:, 28:29]
        fo1 = af[:, 32:40]
        fo2 = af[:, 40:48]
        v.tensor_copy(lg, pb[i % 2][:, 0:8])
        v.reduce_max(m1, lg, axis=AX)
        v.tensor_scalar(nm1, m1, -1.0, None, ALU.mult)
        sc = b.to(nc.scalar)
        sc.activation(pr, lg, AF.Exp, bias=nm1[0:128, 0:1], accum_out=den)
        v = b.to(nc.vector)
        v.reciprocal(den, den)
        v.tensor_scalar(pr, pr, den[0:128, 0:1], None, ALU.mult)
        v.reduce_max(m1, pr, axis=AX)
        v.tensor_scalar(t2, pr, m1[0:128, 0:1], None, ALU.is_equal)
        v.tensor_tensor_scan(fo1, t2, z8[:, :], 0.0, ALU.add, ALU.add)
        v.tensor_scalar(fo1, fo1, 1.0, None, ALU.is_equal)
        v.tensor_tensor(fo1, fo1, t2, ALU.mult)
        v.tensor_scalar(t2, fo1, -1.0, 1.0, ALU.mult, ALU.add)
        v.tensor_tensor(t2, pr, t2, ALU.mult)
        v.reduce_max(m2, t2, axis=AX)
        v.tensor_scalar(pr, t2, m2[0:128, 0:1], None, ALU.is_equal)
        v.tensor_tensor_scan(fo2, pr, z8[:, :], 0.0, ALU.add, ALU.add)
        v.tensor_scalar(fo2, fo2, 1.0, None, ALU.is_equal)
        v.tensor_tensor(fo2, fo2, pr, ALU.mult)
        v.tensor_tensor(den, m1, m2, ALU.add)
        v.tensor_scalar(den, den, 1e-20, None, ALU.add)
        v.reciprocal(den, den)
        v.tensor_scalar(scl, den, 2.5, None, ALU.mult)
        v.tensor_scalar(fo1, fo1, m1[0:128, 0:1], None, ALU.mult)
        v.tensor_scalar(fo2, fo2, m2[0:128, 0:1], None, ALU.mult)
        v.tensor_add(fo1, fo1, fo2)
        v.tensor_scalar(C.coeff[:, i, :], fo1, scl[0:128, 0:1], None, ALU.mult)
    if dbg:
        ocf = nc.dram_tensor("d_cf", [128, NT * 8], F32, kind="ExternalOutput")
        b.ldma(ocf[:, :], C.coeff[:, 0:NT, 0:8])
    if upto <= 4:
        b.finish()
        return

    wgu = View(C.WBh, [128, 2, 8, 128])
    gtile = View(C.WBh, [128, 2, 512], base=2 * 8 * 128)
    for e in range(9):
        nch = 11 if e < 8 else 22
        for ffc in range(nch):
            if e < 8:
                b.ldma(wgu[:, 0, 0:8, 0:128],
                       wg16[e].rearrange("(kf p) f -> p kf f", p=128)[:, :, ffc * 128:(ffc + 1) * 128])
                b.ldma(wgu[:, 1, 0:8, 0:128],
                       wu16[e].rearrange("(kf p) f -> p kf f", p=128)[:, :, ffc * 128:(ffc + 1) * 128])
            else:
                b.ldma(wgu[:, 0, 0:8, 0:128],
                       swg16.rearrange("(kf p) f -> p kf f", p=128)[:, :, ffc * 128:(ffc + 1) * 128])
                b.ldma(wgu[:, 1, 0:8, 0:128],
                       swu16.rearrange("(kf p) f -> p kf f", p=128)[:, :, ffc * 128:(ffc + 1) * 128])
            for tb in range(8):
                t = b.to(nc.tensor)
                for kf in range(8):
                    t.matmul(pb[0][:, :], wgu[:, 0, kf, 0:128],
                             C.xnT[:, kf, tb * 512:(tb + 1) * 512],
                             start=(kf == 0), stop=(kf == 7))
                for kf in range(8):
                    t.matmul(pb[1][:, :], wgu[:, 1, kf, 0:128],
                             C.xnT[:, kf, tb * 512:(tb + 1) * 512],
                             start=(kf == 0), stop=(kf == 7))
                sc = b.to(nc.scalar)
                sc.activation(gtile[:, 0, :], pb[0][:, :], AF.Silu)
                v = b.to(nc.vector)
                v.tensor_tensor(gtile[:, 1, :], gtile[:, 0, :], pb[1][:, :], ALU.mult)
                if e < 8:
                    b.ldma(gt_dram[e, ffc, :, tb * 512:(tb + 1) * 512], gtile[:, 1, :])
                else:
                    b.ldma(gt2_dram[ffc, :, tb * 512:(tb + 1) * 512], gtile[:, 1, :])
    if upto <= 5:
        b.finish()
        return

    wdv = View(C.AB1h, [128, 22, 512])
    gtv = View(C.AB1h, [128, 22, 512], base=22 * 512)
    for e in range(9):
        nch = 11 if e < 8 else 22
        for n in range(2):
            if e < 8:
                b.ldma(wdv[:, 0:11, 0:512],
                       wd16[e, :, n * 512:(n + 1) * 512].rearrange("(c p) n -> p c n", p=128))
            else:
                b.ldma(wdv[:, 0:22, 0:512],
                       swd16[:, n * 512:(n + 1) * 512].rearrange("(c p) n -> p c n", p=128))
            for tb in range(8):
                if e < 8:
                    b.ldma(gtv[:, 0:11, 0:512],
                           gt_dram[e, :, :, tb * 512:(tb + 1) * 512].rearrange("c p n -> p c n"))
                else:
                    b.ldma(gtv[:, 0:22, 0:512],
                           gt2_dram[:, :, tb * 512:(tb + 1) * 512].rearrange("c p n -> p c n"))
                for t4 in range(4):
                    i = tb * 4 + t4
                    t = b.to(nc.tensor)
                    for ffc in range(nch):
                        t.matmul(pb[i % 2][:, :], gtv[:, ffc, t4 * 128:(t4 + 1) * 128],
                                 wdv[:, ffc, n * 512:(n + 1) * 512] if False else wdv[:, ffc, 0:512],
                                 start=(ffc == 0), stop=(ffc == nch - 1))
                    v = b.to(nc.vector)
                    if e == 0:
                        v.tensor_scalar(C.acc[:, i, n * 512:(n + 1) * 512],
                                        pb[i % 2][:, :], C.coeff[:, i, e:e + 1],
                                        None, ALU.mult)
                    elif e < 8:
                        v.scalar_tensor_tensor(C.acc[:, i, n * 512:(n + 1) * 512],
                                               pb[i % 2][:, :], C.coeff[:, i, e:e + 1],
                                               C.acc[:, i, n * 512:(n + 1) * 512],
                                               ALU.mult, ALU.add)
                    else:
                        v.scalar_tensor_tensor(C.acc[:, i, n * 512:(n + 1) * 512],
                                               pb[i % 2][:, :], 1.0,
                                               C.acc[:, i, n * 512:(n + 1) * 512],
                                               ALU.mult, ALU.add)
    if upto <= 6:
        b.finish()
        return

    af = C.af
    stile = View(C.AB1h, [128, 1024])
    for i in range(NT):
        b.ldma(af[:, 0:1024], x_in[i * 128:(i + 1) * 128, :])
        for f in range(8):
            b.tdma(stile[:, f * 128:(f + 1) * 128],
                   sT_dram[f * 128:(f + 1) * 128, i * 128:(i + 1) * 128])
        sc = b.to(nc.scalar)
        sc.activation(af[:, 2048:3072], C.acc[:, i, :], AF.Silu)
        v = b.to(nc.vector)
        v.tensor_mul(af[:, 2048:3072], af[:, 2048:3072], stile[:, :])
        v.tensor_add(af[:, 3072:4096], af[:, 0:1024], af[:, 2048:3072])
        b.ldma(x2_out[i * 128:(i + 1) * 128, :], af[:, 3072:4096])
    b.finish()


def idt32c(C):
    return C.idt32


def run_layer1(x_full, weights, upto=99):
    with ExitStack() as stack:
        nc = bass.Bass("TRN2", target_bir_lowering=False)
        b = Build(nc, stack)
        C = alloc_common(b)
        emit_common_init(b, C)
        build_layer1(b, C, upto=upto)
        in_maps = []
        for c in range(NCORES):
            m = {"x": np.ascontiguousarray(x_full[c])}
            for k in L1_KEYS:
                m[k] = np.ascontiguousarray(weights[k])
            m.update(b.consts)
            in_maps.append(m)
        res = run_bass_kernel_spmd(nc, in_maps, core_ids=list(range(NCORES)))
        return np.stack([res.results[c]["x2"] for c in range(NCORES)])



L2_KEYS = ["L2_attn_W_qkv", "L2_attn_W_out", "L2_attn_b_out", "L2_attn_rel_emb",
           "L2_ffn_w_gate", "L2_ffn_w_up", "L2_ffn_w_down"]


def build_layer2(b, C, upto=99, dbg=False, x_in_t=None, x_out_t=None):
    nc = b.nc
    x_in = x_in_t if x_in_t is not None else \
        nc.dram_tensor("x", [T, D], F32, kind="ExternalInput")
    wqkv = nc.dram_tensor("L2_attn_W_qkv", [3 * D, D], F32, kind="ExternalInput")
    wout = nc.dram_tensor("L2_attn_W_out", [D, D], F32, kind="ExternalInput")
    bout = nc.dram_tensor("L2_attn_b_out", [D], F32, kind="ExternalInput")
    remb = nc.dram_tensor("L2_attn_rel_emb", [2049, 16], F32, kind="ExternalInput")
    wgi = nc.dram_tensor("L2_ffn_w_gate", [3072, D], I32, kind="ExternalInput")
    wui = nc.dram_tensor("L2_ffn_w_up", [3072, D], I32, kind="ExternalInput")
    wdi = nc.dram_tensor("L2_ffn_w_down", [D, 3072], I32, kind="ExternalInput")
    x3_out = x_out_t if x_out_t is not None else \
        nc.dram_tensor("x3", [T, D], F32, kind="ExternalOutput")

    wqkv16 = nc.dram_tensor("wqkv16", [3 * D, D], BF16)
    wout16 = nc.dram_tensor("wout16", [D, D], BF16)
    wg16i = nc.dram_tensor("wg16i", [3072, D], BF16)
    wu16i = nc.dram_tensor("wu16i", [3072, D], BF16)
    wd16i = nc.dram_tensor("wd16i", [D, 3072], BF16)
    qkv_tm = nc.dram_tensor("qkv_tm", [T, 3 * D], BF16)
    Vdram = nc.dram_tensor("Vdram", [16, 8448], F32)
    Ddram = nc.dram_tensor("Ddram", [128, 16, 8320], F32)
    gtd = nc.dram_tensor("gtd_l2", [24, 128, T], BF16)
    s_dram = nc.dram_tensor("s_dram_l2", [T, D], F32)
    SCL = 1.0 / 127.0

    pb = b.pb

    b.gdma(wqkv16[:, :], wqkv[:, :])
    b.gdma(wout16[:, :], wout[:, :])
    b.gdma(wg16i[:, :], wgi[:, :])
    b.gdma(wu16i[:, :], wui[:, :])
    b.gdma(wd16i[:, :], wdi[:, :])

    emit_layernorm(b, x_in, C.xn_tm, C.fst, C.bst, C.stats, C.cpool)
    load_xnT(b, C.xn_tm, C.xnT)
    if upto <= 1:
        b.finish()
        return

    otile = View(C.AB2h, [128, 2, 512])
    for n in range(6):
        sl = n % 2
        for kf in range(8):
            b.tdma(C.wslot[:, sl, kf, 0:512],
                   wqkv16[n * 512:(n + 1) * 512, kf * 128:(kf + 1) * 128])
        for i in range(NT):
            t = b.to(nc.tensor)
            for kf in range(8):
                t.matmul(pb[i % 2][:, :], C.xnT[:, kf, i * 128:(i + 1) * 128],
                         C.wslot[:, sl, kf, 0:512], start=(kf == 0), stop=(kf == 7))
            v = b.to(nc.vector)
            v.tensor_copy(otile[:, i % 2, :], pb[i % 2][:, :])
            b.ldma(qkv_tm[i * 128:(i + 1) * 128, n * 512:(n + 1) * 512],
                   otile[:, i % 2, :])
    if upto <= 2:
        b.finish()
        return

    wgu = View(C.WBh, [128, 2, 8, 128])
    gtile = View(C.WBh, [128, 2, 512], base=2 * 8 * 128)
    for ffc in range(24):
        for kf in range(8):
            b.tdma(bass.AP(C.WBh, 0 * 2048 + 0 + kf * 128,
                           [[8192, 128], [1, 128]]) if False else wgu[:, 0, kf, 0:128],
                   wg16i[ffc * 128:(ffc + 1) * 128, kf * 128:(kf + 1) * 128])
            b.tdma(wgu[:, 1, kf, 0:128],
                   wu16i[ffc * 128:(ffc + 1) * 128, kf * 128:(kf + 1) * 128])
        for tb in range(8):
            t = b.to(nc.tensor)
            for kf in range(8):
                t.matmul(pb[0][:, :], wgu[:, 0, kf, 0:128],
                         C.xnT[:, kf, tb * 512:(tb + 1) * 512],
                         start=(kf == 0), stop=(kf == 7))
            for kf in range(8):
                t.matmul(pb[1][:, :], wgu[:, 1, kf, 0:128],
                         C.xnT[:, kf, tb * 512:(tb + 1) * 512],
                         start=(kf == 0), stop=(kf == 7))
            sc = b.to(nc.scalar)
            sc.activation(gtile[:, 0, :], pb[0][:, :], AF.Silu, scale=SCL)
            v = b.to(nc.vector)
            v.tensor_tensor(gtile[:, 1, :], gtile[:, 0, :], pb[1][:, :], ALU.mult)
            b.ldma(gtd[ffc, :, tb * 512:(tb + 1) * 512], gtile[:, 1, :])
    if upto <= 3:
        b.finish()
        return

    embs = nc.alloc_sbuf_tensor("embs", [128, 17, 16], F32)
    vrevT = View(C.AF3h, [16, 2064], base=1024)
    anti = nc.alloc_sbuf_tensor("anti", [128, 128], F32)
    g = b.to(nc.gpsimd)
    g.memset(anti[:, :], 1.0)
    g.affine_select(anti[:, :], anti[:, :], [[1, 128]], ALU.is_ge, 0.0,
                    base=-127, channel_multiplier=1)
    g.affine_select(anti[:, :], anti[:, :], [[-1, 128]], ALU.is_ge, 0.0,
                    base=127, channel_multiplier=-1)
    for ck in range(16):
        b.ldma(embs[:, ck, 0:16], remb[ck * 128:(ck + 1) * 128, 0:16])
    b.ldma(embs[0:1, 16, 0:16], remb[2048:2049, 0:16])
    for ck in range(17):
        if ck < 16:
            t = b.to(nc.tensor)
            t.matmul(pb[ck % 2][0:16, 0:128], embs[:, ck, 0:16], anti[:, :],
                     start=True, stop=True)
        if ck > 0:
            v = b.to(nc.vector)
            c0 = 1921 - (ck - 1) * 128
            v.tensor_copy(vrevT[0:16, c0:c0 + 128], pb[(ck - 1) % 2][0:16, 0:128])
    b.gdma(vrevT[0:16, 0:1], embs[0:1, 16, 0:16])
    ones1 = nc.alloc_sbuf_tensor("ones1", [1, 128], F32)
    bcv = nc.alloc_sbuf_tensor("bcv", [128, 32], F32)
    v = b.to(nc.vector)
    v.memset(ones1[0:1, :], 1.0)
    t = b.to(nc.tensor)
    t.matmul(pb[0][:, 0:16], ones1[0:1, 0:128], embs[0:1, 16, 0:16],
             start=True, stop=True)
    t.matmul(pb[1][:, 0:16], ones1[0:1, 0:128], embs[0:1, 0, 0:16],
             start=True, stop=True)
    v = b.to(nc.vector)
    v.tensor_copy(bcv[:, 0:16], pb[0][:, 0:16])
    v.tensor_copy(bcv[:, 16:32], pb[1][:, 0:16])
    cA = View(C.AF3h, [128, 32])
    for h in range(16):
        v = b.to(nc.vector)
        v.memset(cA[:, 0:26], 0.0)
        v.tensor_scalar(cA[:, 0:26], cA[:, 0:26], bcv[:, h:h + 1], None, ALU.add)
        b.ldma(bass.AP(Vdram, h * 8448, [[26, 128], [1, 26]]), cA[:, 0:26])
        b.gdma(bass.AP(Vdram, h * 8448 + 3327, [[2049, 1], [1, 2049]]),
               vrevT[h:h + 1, 0:2049])
        v = b.to(nc.vector)
        v.memset(cA[:, 0:24], 0.0)
        v.tensor_scalar(cA[:, 0:24], cA[:, 0:24], bcv[:, 16 + h:17 + h], None, ALU.add)
        b.ldma(bass.AP(Vdram, h * 8448 + 5376, [[24, 128], [1, 24]]), cA[:, 0:24])
    for p in range(128):
        b.ldma(Ddram[p, :, :], bass.AP(Vdram, 128 - p, [[8448, 16], [1, 8320]]))
    if upto <= 4:
        b.finish()
        return

    qT2 = View(C.AB1h, [128, 4096])
    kT2 = View(C.AB1h, [128, 4096], base=4096)
    va = View(C.AB1h, [128, NT, 65], base=8192)
    Dres = View(C.AB1h, [128, 8320], base=10304)
    PT = View(C.AB1h, [128, 2, 512], base=18624)
    yT = View(C.AB2h, [128, 8, T])
    g = b.to(nc.gpsimd)
    g.memset(va[:, 0:NT, 64:65], 1.0)
    af = C.af
    for gi in [1, 0, 2, 3, 4, 5, 6, 7]:
      for h in (gi * 2, gi * 2 + 1):
        p0 = (h % 2) * 64
        if h % 2 == 0:
            b.tdma(qT2[0:128, 0:T], qkv_tm[:, gi * 128:(gi + 1) * 128])
            b.tdma(kT2[0:128, 0:T], qkv_tm[:, 1024 + gi * 128:1024 + (gi + 1) * 128])
        b.gdma(Dres[:, 0:8320],
               bass.AP(Ddram, h * 8320, [[16 * 8320, 128], [1, 8320]]))
        b.ldma(va[:, 0:NT, 0:64],
               bass.AP(qkv_tm, 2048 + h * 64, [[3072, 128], [3072 * 128, NT], [1, 64]]))
        for qb in range(8):
            for kc in range(32):
                t = b.to(nc.tensor)
                t.matmul(pb[kc % 2][:, :], kT2[p0:p0 + 64, kc * 128:(kc + 1) * 128],
                         qT2[p0:p0 + 64, qb * 512:(qb + 1) * 512],
                         start=True, stop=True,
                         tile_position=(p0, 0) if p0 else None)
                v = b.to(nc.vector)
                dbase = 4223 - kc * 128 + qb * 512
                v.scalar_tensor_tensor(af[:, 0:512], pb[kc % 2][:, :], 0.125,
                                       Dres[:, dbase:dbase + 512],
                                       ALU.mult, ALU.add)
                if dbg and qb == 0 and kc == 0 and h in (0, 2):
                    dsc = nc.dram_tensor(f"dbg_sc{h}", [128, 512], F32,
                                         kind="ExternalOutput")
                    b.ldma(dsc[:, :], af[:, 0:512])
                    drw = nc.dram_tensor(f"dbg_raw{h}", [128, 512], F32,
                                         kind="ExternalOutput")
                    v = b.to(nc.vector)
                    v.tensor_copy(af[:, 512:1024], pb[kc % 2][:, :])
                    b.ldma(drw[:, :], af[:, 512:1024])
                sc = b.to(nc.scalar)
                sc.activation(PT[:, kc % 2, :], af[:, 0:512], AF.Exp)
                t = b.to(nc.tensor)
                t.matmul(pb[2][0:65, 0:512], va[:, kc, 0:65], PT[:, kc % 2, :],
                         start=(kc == 0), stop=(kc == 31))
            if dbg and qb == 0 and h in (0, 2):
                dyr = nc.dram_tensor(f"dbg_yraw{h}", [65, 512], F32,
                                     kind="ExternalOutput")
                v = b.to(nc.vector)
                v.tensor_copy(af[0:65, 512:1024], pb[2][0:65, 0:512])
                b.ldma(dyr[:, :], af[0:65, 512:1024])
            v = b.to(nc.vector)
            v.reciprocal(C.lnrow[0:1, 0:512], pb[2][64:65, 0:512])
            t = b.to(nc.tensor)
            t.matmul(pb[3][0:64, 0:512], C.onesc[0:1, 0:64], C.lnrow[0:1, 0:512],
                     start=True, stop=True)
            v = b.to(nc.vector)
            v.tensor_copy(af[0:64, 512:1024], pb[3][0:64, 0:512])
            v.tensor_mul(yT[p0:p0 + 64, gi, qb * 512:(qb + 1) * 512],
                         pb[2][0:64, 0:512], af[0:64, 512:1024])
    if upto <= 5:
        b.finish()
        return

    bout_bc = nc.alloc_sbuf_tensor("bout_bc", [128, 1024], F32)
    b.ldma(bout_bc[:, :], bass.AP(bout, 0, [[0, 128], [1, 1024]]))
    for n in range(2):
        sl = n % 2
        for kf in range(8):
            b.tdma(C.wslot[:, sl, kf, 0:512],
                   wout16[n * 512:(n + 1) * 512, kf * 128:(kf + 1) * 128])
        for i in range(NT):
            t = b.to(nc.tensor)
            for kf in range(8):
                t.matmul(pb[i % 2][:, :], yT[:, kf, i * 128:(i + 1) * 128],
                         C.wslot[:, sl, kf, 0:512], start=(kf == 0), stop=(kf == 7))
            v = b.to(nc.vector)
            v.tensor_add(af[:, 0:512], pb[i % 2][:, :],
                         bout_bc[:, n * 512:(n + 1) * 512])
            b.ldma(s_dram[i * 128:(i + 1) * 128, n * 512:(n + 1) * 512],
                   af[:, 0:512])
    if upto <= 6:
        b.finish()
        return

    if dbg:
        d_yt = nc.dram_tensor("dbg_yt", [128, 8 * T], BF16, kind="ExternalOutput")
        b.ldma(d_yt[:, :], yT[:, 0:8, 0:T])
        d_qkv = nc.dram_tensor("dbg_qkv", [T, 3 * D], BF16, kind="ExternalOutput")
        b.ldma(d_qkv[:, :], qkv_tm[:, :])
        d_dd0 = nc.dram_tensor("dbg_dd0", [128, 8320], F32, kind="ExternalOutput")
        b.ldma(d_dd0[:, :], bass.AP(Ddram, 0, [[16 * 8320, 128], [1, 8320]]))
        d_dd2 = nc.dram_tensor("dbg_dd2", [128, 8320], F32, kind="ExternalOutput")
        b.ldma(d_dd2[:, :], bass.AP(Ddram, 2 * 8320, [[16 * 8320, 128], [1, 8320]]))
        d_vd = nc.dram_tensor("dbg_vd", [16, 8448], F32, kind="ExternalOutput")
        b.ldma(d_vd[:, :], Vdram[:, :])
    wdv = View(C.AB1h, [128, 24, 512])
    gtv = View(C.AB1h, [128, 24, 512], base=24 * 512)
    for n in range(2):
        for kc24 in range(3):
            for kf in range(8):
                b.tdma(wdv[:, kc24 * 8 + kf, 0:512],
                       wd16i[n * 512:(n + 1) * 512,
                             (kc24 * 8 + kf) * 128:(kc24 * 8 + kf + 1) * 128])
        for tb in range(8):
            b.ldma(gtv[:, 0:24, 0:512],
                   gtd[:, :, tb * 512:(tb + 1) * 512].rearrange("c p n -> p c n"))
            for t4 in range(4):
                i = tb * 4 + t4
                t = b.to(nc.tensor)
                for ffc in range(24):
                    t.matmul(pb[i % 2][:, :], gtv[:, ffc, t4 * 128:(t4 + 1) * 128],
                             wdv[:, ffc, 0:512], start=(ffc == 0), stop=(ffc == 23))
                v = b.to(nc.vector)
                v.tensor_scalar(C.acc[:, i, n * 512:(n + 1) * 512],
                                pb[i % 2][:, :], SCL * SCL, None, ALU.mult)
    if upto <= 7:
        b.finish()
        return

    if dbg:
        d_s = nc.dram_tensor("dbg_s", [T, D], F32, kind="ExternalOutput")
        b.ldma(d_s[:, :], s_dram[:, :])
        d_acc = nc.dram_tensor("dbg_acc", [128, NT * 1024], BF16, kind="ExternalOutput")
        b.ldma(d_acc[:, :], C.acc[:, 0:NT, 0:1024])
    for i in range(NT):
        b.ldma(af[:, 0:1024], x_in[i * 128:(i + 1) * 128, :])
        b.ldma(af[:, 1024:2048], s_dram[i * 128:(i + 1) * 128, :])
        sc = b.to(nc.scalar)
        sc.activation(af[:, 2048:3072], C.acc[:, i, :], AF.Silu)
        v = b.to(nc.vector)
        v.tensor_mul(af[:, 2048:3072], af[:, 2048:3072], af[:, 1024:2048])
        v.tensor_add(af[:, 3072:4096], af[:, 0:1024], af[:, 2048:3072])
        b.ldma(x3_out[i * 128:(i + 1) * 128, :], af[:, 3072:4096])
    b.finish()


def run_layer2(x_full, weights, upto=99):
    with ExitStack() as stack:
        nc = bass.Bass("TRN2", target_bir_lowering=False)
        b = Build(nc, stack)
        C = alloc_common(b)
        emit_common_init(b, C)
        build_layer2(b, C, upto=upto)
        in_maps = []
        for c in range(NCORES):
            m = {"x": np.ascontiguousarray(x_full[c])}
            for k in L2_KEYS:
                m[k] = np.ascontiguousarray(weights[k])
            m.update(b.consts)
            in_maps.append(m)
        res = run_bass_kernel_spmd(nc, in_maps, core_ids=list(range(NCORES)))
        return np.stack([res.results[c]["x3"] for c in range(NCORES)])



L3_KEYS = ["L3_attn_Wq", "L3_attn_bq", "L3_attn_Wk", "L3_attn_bk",
           "L3_attn_Wv", "L3_attn_bv", "L3_attn_phase_bias",
           "L3_ffn_Wr", "L3_ffn_W1", "L3_ffn_b1", "L3_ffn_W2", "L3_ffn_b2",
           "L3_ffn_fw1", "L3_ffn_fb1", "L3_ffn_fw2", "L3_ffn_fb2"]

TWO_PI = 2.0 * math.pi
CAP3 = 10240


def build_layer3(b, C, upto=99, sim1=False, dbg=False, x_in_t=None, x_out_t=None):
    nc = b.nc
    x_in = x_in_t if x_in_t is not None else \
        nc.dram_tensor("x", [T, D], F32, kind="ExternalInput")
    Wq = nc.dram_tensor("L3_attn_Wq", [D, D], F32, kind="ExternalInput")
    bq = nc.dram_tensor("L3_attn_bq", [D], F32, kind="ExternalInput")
    Wk = nc.dram_tensor("L3_attn_Wk", [D, D], F32, kind="ExternalInput")
    bk = nc.dram_tensor("L3_attn_bk", [D], F32, kind="ExternalInput")
    Wv = nc.dram_tensor("L3_attn_Wv", [D, D], F32, kind="ExternalInput")
    bv = nc.dram_tensor("L3_attn_bv", [D], F32, kind="ExternalInput")
    pbv = nc.dram_tensor("L3_attn_phase_bias", [D], F32, kind="ExternalInput")
    Wr = nc.dram_tensor("L3_ffn_Wr", [8, D], F32, kind="ExternalInput")
    W1 = nc.dram_tensor("L3_ffn_W1", [8, D, 4096], F32, kind="ExternalInput")
    b1 = nc.dram_tensor("L3_ffn_b1", [8, 4096], F32, kind="ExternalInput")
    W2 = nc.dram_tensor("L3_ffn_W2", [8, 4096, D], F32, kind="ExternalInput")
    b2 = nc.dram_tensor("L3_ffn_b2", [8, D], F32, kind="ExternalInput")
    fw1 = nc.dram_tensor("L3_ffn_fw1", [D, 4096], F32, kind="ExternalInput")
    fb1 = nc.dram_tensor("L3_ffn_fb1", [4096], F32, kind="ExternalInput")
    fw2 = nc.dram_tensor("L3_ffn_fw2", [4096, D], F32, kind="ExternalInput")
    fb2 = nc.dram_tensor("L3_ffn_fb2", [D], F32, kind="ExternalInput")
    invwl_in = nc.dram_tensor("invwl", [1, D], F32, kind="ExternalInput")
    coremask = nc.dram_tensor("coremask", [8, 1], F32, kind="ExternalInput")
    iw = 1.0 / (np.arange(1, D + 1, dtype=np.float32) * np.float32(TWO_PI / D)
                + np.float32(1e-8))
    b.consts["invwl"] = np.ascontiguousarray(iw[None, :].astype(np.float32))
    x4_out = x_out_t if x_out_t is not None else \
        nc.dram_tensor("x4", [T, D], F32, kind="ExternalOutput")

    wv16 = nc.dram_tensor("wv16", [D, D], BF16)
    wr16 = nc.dram_tensor("wr16", [8, D], BF16)
    w116 = nc.dram_tensor("w116", [8, D, 4096], BF16)
    w216 = nc.dram_tensor("w216", [8, 4096, D], BF16)
    b216 = nc.dram_tensor("b216", [8, D], BF16)
    fw116 = nc.dram_tensor("fw116", [D, 4096], BF16)
    fw216 = nc.dram_tensor("fw216", [4096, D], BF16)
    wqhi = nc.dram_tensor("wqhi", [D, D], BF16)
    wqlo = nc.dram_tensor("wqlo", [D, D], BF16)
    wkhi = nc.dram_tensor("wkhi", [D, D], BF16)
    wklo = nc.dram_tensor("wklo", [D, D], BF16)
    xnlo_tm = nc.dram_tensor("xnlo_tm", [T, D], BF16)
    cq_tm = nc.dram_tensor("cq_tm", [T, D], BF16)
    sq_tm = nc.dram_tensor("sq_tm", [T, D], BF16)
    ck_tm = nc.dram_tensor("ck_tm", [T, D], BF16)
    sk_tm = nc.dram_tensor("sk_tm", [T, D], BF16)
    v_tm = nc.dram_tensor("v_tm_l3", [T, D], BF16)
    s_dram = nc.dram_tensor("s_dram_l3", [T, D], F32)
    gt3a = nc.dram_tensor("gt3a", [5, 32, 128, T], BF16)
    gt3b = nc.dram_tensor("gt3b", [4, 32, 128, T], BF16)
    def gt3s(src):
        return (gt3a, src) if src < 5 else (gt3b, src - 5)
    yfb_tm = nc.dram_tensor("yfb_tm", [T, D], BF16)
    cnt_dram = nc.dram_tensor("cnt_dram", [1, 8], F32)
    cntg_dram = nc.dram_tensor("cntg_dram", [8, 8], F32)
    em_dram = nc.dram_tensor("em_dram", [8, T], F32)
    kept_dram = nc.dram_tensor("kept_dram", [8, T], F32)
    fb_dram = nc.dram_tensor("fb_dram", [1, T], F32)

    pb = b.pb
    af = C.af
    stage6 = nc.alloc_sbuf_tensor("l3stg", [128, 3, 1024], BF16)
    small3 = nc.alloc_sbuf_tensor("small3", [8, 16], F32)
    cmsk = nc.alloc_sbuf_tensor("cmsk", [8, 1], F32)
    dent = nc.alloc_sbuf_tensor("dent", [128, 4], F32)
    onesk = nc.alloc_sbuf_tensor("onesk", [128, 1], BF16)
    pden = nc.alloc_psum_tensor("pden", [1, 512], F32)

    b.gdma(wv16[:, :], Wv[:, :])
    b.gdma(wr16[:, :], Wr[:, :])
    b.gdma(b216[:, :], b2[:, :])
    for e in range(8):
        b.gdma(w116[e, :, :], W1[e, :, :])
        b.gdma(w216[e, :, :], W2[e, :, :])
    b.gdma(fw116[:, :], fw1[:, :])
    b.gdma(fw216[:, :], fw2[:, :])

    emit_layernorm(b, x_in, C.xn_tm, C.fst, C.bst, C.stats, C.cpool,
                   xlo_tm=xnlo_tm)
    load_xnT(b, C.xn_tm, C.xnT)
    if upto <= 1:
        b.finish()
        return

    biasb = View(C.AF3h, [128, 4, 1024], base=4096)
    invwl_bc = View(C.AF3h, [128, 1024], base=2048)
    b.ldma(biasb[:, 0, :], bass.AP(bq, 0, [[0, 128], [1, 1024]]))
    b.ldma(biasb[:, 1, :], bass.AP(bk, 0, [[0, 128], [1, 1024]]))
    b.ldma(biasb[:, 2, :], bass.AP(bv, 0, [[0, 128], [1, 1024]]))
    b.ldma(biasb[:, 3, :], bass.AP(pbv, 0, [[0, 128], [1, 1024]]))
    b.ldma(invwl_bc[:, :], bass.AP(invwl_in, 0, [[0, 128], [1, 1024]]))
    v = b.to(nc.vector)
    for wi in (0, 1):
        v.tensor_mul(biasb[:, wi, :], biasb[:, wi, :], invwl_bc[:, :])
        v.tensor_add(biasb[:, wi, :], biasb[:, wi, :], biasb[:, 3, :])
    iwcol = nc.alloc_sbuf_tensor("iwcol", [128, 1], F32)
    for wsrc, whid, wlod in ((Wq, wqhi, wqlo), (Wk, wkhi, wklo)):
        for rt in range(8):
            b.ldma(af[:, 0:1024], wsrc[rt * 128:(rt + 1) * 128, :])
            b.ldma(iwcol[:, 0:1], bass.AP(invwl_in, rt * 128, [[1, 128], [1, 1]]))
            v = b.to(nc.vector)
            v.tensor_scalar(af[:, 0:1024], af[:, 0:1024], iwcol[0:128, 0:1],
                            None, ALU.mult)
            v.tensor_copy(stage6[:, 0, :], af[:, 0:1024])
            v.tensor_copy(af[:, 1024:2048], stage6[:, 0, :])
            v.tensor_sub(af[:, 1024:2048], af[:, 0:1024], af[:, 1024:2048])
            v.tensor_copy(stage6[:, 1, :], af[:, 1024:2048])
            b.ldma(whid[rt * 128:(rt + 1) * 128, :], stage6[:, 0, :])
            b.ldma(wlod[rt * 128:(rt + 1) * 128, :], stage6[:, 1, :])
    xloT = View(C.AB2h, [128, 8, T])
    for f in range(8):
        b.tdma(xloT[:, f, 0:T], xnlo_tm[:, f * 128:(f + 1) * 128])
    for wi, (whid, wlod, o_cos, o_sin) in enumerate(
            ((wqhi, wqlo, cq_tm, sq_tm), (wkhi, wklo, ck_tm, sk_tm))):
        for n in range(2):
            for kf in range(8):
                b.tdma(C.wslot[:, 0, kf, 0:512],
                       whid[n * 512:(n + 1) * 512, kf * 128:(kf + 1) * 128])
                b.tdma(C.wslot[:, 1, kf, 0:512],
                       wlod[n * 512:(n + 1) * 512, kf * 128:(kf + 1) * 128])
            ns = slice(n * 512, (n + 1) * 512)
            for i in range(NT):
                t = b.to(nc.tensor)
                for kf in range(8):
                    t.matmul(pb[i % 2][:, :], C.xnT[:, kf, i * 128:(i + 1) * 128],
                             C.wslot[:, 0, kf, 0:512], start=(kf == 0), stop=False)
                for kf in range(8):
                    t.matmul(pb[i % 2][:, :], C.xnT[:, kf, i * 128:(i + 1) * 128],
                             C.wslot[:, 1, kf, 0:512], start=False, stop=False)
                for kf in range(8):
                    t.matmul(pb[i % 2][:, :], xloT[:, kf, i * 128:(i + 1) * 128],
                             C.wslot[:, 0, kf, 0:512], start=False, stop=(kf == 7))
                v = b.to(nc.vector)
                th = af[:, 0:512]
                ps = af[:, 512:1024]
                rr = af[:, 1536:2048]
                v.tensor_add(th, pb[i % 2][:, :], biasb[:, wi, ns])
                for path, offs in ((0, math.pi / 2.0), (1, 0.0)):
                    v = b.to(nc.vector)
                    v.tensor_scalar(ps, th, offs, None, ALU.add)
                    v.tensor_scalar(rr, ps, 1.0 / TWO_PI, None, ALU.mult)
                    v.tensor_scalar(rr, rr, 12582912.0, None, ALU.add)
                    v.tensor_scalar(rr, rr, -12582912.0, None, ALU.add)
                    v.scalar_tensor_tensor(ps, rr, -TWO_PI, ps, ALU.mult, ALU.add)
                    v.tensor_scalar(ps, ps, math.pi, None, ALU.min)
                    v.tensor_scalar(ps, ps, -math.pi, None, ALU.max)
                    sc = b.to(nc.scalar)
                    sc.activation(stage6[:, path, 0:512], ps, AF.Sin)
                b.ldma(o_cos[i * 128:(i + 1) * 128, ns], stage6[:, 0, 0:512])
                b.ldma(o_sin[i * 128:(i + 1) * 128, ns], stage6[:, 1, 0:512])
    for n in range(2):
        for kf in range(8):
            b.tdma(C.wslot[:, 0, kf, 0:512],
                   wv16[n * 512:(n + 1) * 512, kf * 128:(kf + 1) * 128])
        ns = slice(n * 512, (n + 1) * 512)
        for i in range(NT):
            t = b.to(nc.tensor)
            for kf in range(8):
                t.matmul(pb[i % 2][:, :], C.xnT[:, kf, i * 128:(i + 1) * 128],
                         C.wslot[:, 0, kf, 0:512], start=(kf == 0), stop=(kf == 7))
            v = b.to(nc.vector)
            v.tensor_add(stage6[:, 2, 0:512], pb[i % 2][:, :], biasb[:, 2, ns])
            b.ldma(v_tm[i * 128:(i + 1) * 128, ns], stage6[:, 2, 0:512])
    if upto <= 2:
        b.finish()
        return

    ckT = View(C.AB1h, [128, 8, T])
    skT = View(C.AB2h, [128, 8, T])
    for f in range(8):
        b.tdma(ckT[:, f, 0:T], ck_tm[:, f * 128:(f + 1) * 128])
        b.tdma(skT[:, f, 0:T], sk_tm[:, f * 128:(f + 1) * 128])
    cqs = View(C.WBh, [128, 2, 8, 512])
    PT = View(stage6, [128, 2, 512])
    vt3 = View(stage6, [128, 2, 512], base=1024)
    v = b.to(nc.vector)
    v.memset(onesk[:, :], 1.0)
    for qb in range(8):
        for f in range(8):
            b.tdma(cqs[:, 0, f, 0:512], cq_tm[qb * 512:(qb + 1) * 512,
                                              f * 128:(f + 1) * 128])
            b.tdma(cqs[:, 1, f, 0:512], sq_tm[qb * 512:(qb + 1) * 512,
                                              f * 128:(f + 1) * 128])
        for dn in range(2):
            for kc in range(32):
                vtv = b.ldma_pre(vt3[:, kc % 2, 0:512],
                                 v_tm[kc * 128:(kc + 1) * 128,
                                      dn * 512:(dn + 1) * 512], "vtp")
                t = b.to(nc.tensor)
                for f in range(8):
                    t.matmul(pb[kc % 2][:, :], ckT[:, f, kc * 128:(kc + 1) * 128],
                             cqs[:, 0, f, 0:512], start=(f == 0), stop=False)
                for f in range(8):
                    t.matmul(pb[kc % 2][:, :], skT[:, f, kc * 128:(kc + 1) * 128],
                             cqs[:, 1, f, 0:512], start=False, stop=(f == 7))
                sc = b.to(nc.scalar)
                sc.activation(PT[:, kc % 2, :], pb[kc % 2][:, :], AF.Exp,
                              scale=ISQ32 / math.sqrt(32.0))
                t = b.to(nc.tensor)
                b.wait(nc.tensor, "vtp", val=vtv)
                for q4 in range(4):
                    t.matmul(pb[2 + q4][:, :],
                             PT[:, kc % 2, q4 * 128:(q4 + 1) * 128],
                             vt3[:, kc % 2, 0:512],
                             start=(kc == 0), stop=(kc == 31))
                if dn == 0:
                    t.matmul(pden[0:1, 0:512], onesk[:, 0:1], PT[:, kc % 2, :],
                             start=(kc == 0), stop=(kc == 31))
            if dn == 0:
                sc = b.to(nc.scalar)
                sc.copy(C.lnrow[0:1, 0:512], pden[0:1, 0:512])
                t = b.to(nc.tensor)
                for q4 in range(5):
                    if q4 < 4:
                        t.transpose(pb[q4 % 2][0:128, 0:1],
                                    C.lnrow[0:1, q4 * 128:(q4 + 1) * 128],
                                    C.idt32[0:1, 0:1])
                    else:
                        t.transpose(pb[0][0:128, 1:2],
                                    C.lnrow[0:1, 0:128], C.idt32[0:1, 0:1])
                    if q4 > 0:
                        v = b.to(nc.vector)
                        v.tensor_copy(dent[:, q4 - 1:q4],
                                      pb[(q4 - 1) % 2][0:128, 0:1])
                        t = b.to(nc.tensor)
                v = b.to(nc.vector)
                v.reciprocal(dent[:, 0:4], dent[:, 0:4])
            for q4 in range(4):
                v = b.to(nc.vector)
                v.tensor_scalar(af[:, 0:512], pb[2 + q4][:, :], dent[:, q4:q4 + 1],
                                None, ALU.mult)
                b.ldma(s_dram[qb * 512 + q4 * 128:qb * 512 + (q4 + 1) * 128,
                              dn * 512:(dn + 1) * 512], af[:, 0:512])
    if upto <= 3:
        b.finish()
        return

    load_xnT(b, C.xn_tm, C.xnT)
    gwsb8 = View(C.WBh, [8, D], base=4096)
    gwT = View(C.WBh, [128, 8, 8])
    b.ldma(gwsb8[:, :], wr16[:, :])
    for kf in range(9):
        if kf < 8:
            t = b.to(nc.tensor)
            t.transpose(C.pt16[0:128, (kf % 2) * 128:(kf % 2) * 128 + 8],
                        gwsb8[0:8, kf * 128:(kf + 1) * 128], C.idt16[0:8, 0:8])
        if kf > 0:
            v = b.to(nc.vector)
            v.tensor_copy(gwT[:, kf - 1, 0:8],
                          C.pt16[0:128, ((kf - 1) % 2) * 128:((kf - 1) % 2) * 128 + 8])
    z8 = nc.alloc_sbuf_tensor("z8_l3", [128, 8], F32)
    v = b.to(nc.vector)
    v.memset(z8[:, :], 0.0)
    for i in range(NT):
        t = b.to(nc.tensor)
        for kf in range(8):
            t.matmul(pb[i % 2][:, 0:8], C.xnT[:, kf, i * 128:(i + 1) * 128],
                     gwT[:, kf, 0:8], start=(kf == 0), stop=(kf == 7))
        v = b.to(nc.vector)
        lg = af[:, 0:8]
        pr = af[:, 8:16]
        t2 = af[:, 16:24]
        m1 = af[:, 24:25]
        nm1 = af[:, 25:26]
        den = af[:, 26:27]
        m2 = af[:, 27:28]
        fo1 = af[:, 32:40]
        fo2 = af[:, 40:48]
        v.tensor_copy(lg, pb[i % 2][:, 0:8])
        v.reduce_max(m1, lg, axis=AX)
        v.tensor_scalar(nm1, m1, -1.0, None, ALU.mult)
        sc = b.to(nc.scalar)
        sc.activation(pr, lg, AF.Exp, bias=nm1[0:128, 0:1], accum_out=den)
        v = b.to(nc.vector)
        v.reciprocal(den, den)
        v.tensor_scalar(pr, pr, den[0:128, 0:1], None, ALU.mult)
        v.reduce_max(m1, pr, axis=AX)
        v.tensor_scalar(t2, pr, m1[0:128, 0:1], None, ALU.is_equal)
        v.tensor_tensor_scan(fo1, t2, z8[:, :], 0.0, ALU.add, ALU.add)
        v.tensor_scalar(fo1, fo1, 1.0, None, ALU.is_equal)
        v.tensor_tensor(fo1, fo1, t2, ALU.mult)
        v.tensor_scalar(t2, fo1, -1.0, 1.0, ALU.mult, ALU.add)
        v.tensor_tensor(t2, pr, t2, ALU.mult)
        v.reduce_max(m2, t2, axis=AX)
        v.tensor_scalar(pr, t2, m2[0:128, 0:1], None, ALU.is_equal)
        v.tensor_tensor_scan(fo2, pr, z8[:, :], 0.0, ALU.add, ALU.add)
        v.tensor_scalar(fo2, fo2, 1.0, None, ALU.is_equal)
        v.tensor_tensor(fo2, fo2, pr, ALU.mult)
        v.tensor_add(C.ltm[:, i, :], fo1, fo2)
        v.tensor_tensor(den, m1, m2, ALU.add)
        v.reciprocal(den, den)
        v.tensor_scalar(fo1, fo1, m1[0:128, 0:1], None, ALU.mult)
        v.tensor_scalar(fo2, fo2, m2[0:128, 0:1], None, ALU.mult)
        v.tensor_add(fo1, fo1, fo2)
        v.tensor_scalar(C.coeff[:, i, :], fo1, den[0:128, 0:1], None, ALU.mult)
    el8 = C.em
    cu8 = C.emt
    for i in range(NT + 1):
        if i < NT:
            t = b.to(nc.tensor)
            t.transpose(pb[i % 2][0:8, 0:128], C.ltm[:, i, :], C.idt32[:, :])
        if i > 0:
            v = b.to(nc.vector)
            v.tensor_copy(cu8[0:8, (i - 1) * 128:i * 128], pb[(i - 1) % 2][0:8, 0:128])
    b.ldma(em_dram[:, :], cu8[0:8, 0:T])
    v = b.to(nc.vector)
    v.memset(el8[0:8, 0:T], 0.0)
    b.ldma(fb_dram[0:1, 0:T], el8[0:1, 0:T])
    b.ldma(cmsk[0:8, 0:1], coremask[:, :])
    CAPX = float(2 * CAP3)
    for e in range(8):
        b.ldma(cu8[0:8, 0:T], bass.AP(fb_dram, 0, [[0, 8], [1, T]]))
        b.ldma(el8[0:8, 0:T], em_dram[:, :])
        v = b.to(nc.vector)
        v.tensor_tensor(cu8[0:8, 0:T], cu8[0:8, 0:T], el8[0:8, 0:T], ALU.mult)
        v.tensor_sub(el8[0:8, 0:T], el8[0:8, 0:T], cu8[0:8, 0:T])
        v.reduce_sum(small3[0:8, 0:1], el8[0:8, 0:T], axis=AX)
        t = b.to(nc.tensor)
        t.transpose(pb[0][0:1, 0:8], small3[0:8, 0:1], C.idt32[0:8, 0:8])
        t.transpose(pb[1][0:1, 0:8], small3[0:8, 0:1], C.idt32[0:8, 0:8])
        v = b.to(nc.vector)
        v.tensor_copy(small3[0:1, 8:16], pb[0][0:1, 0:8])
        b.ldma(cnt_dram[0:1, 0:8], small3[0:1, 8:16])
        if sim1:
            for r in range(8):
                b.ldma(cntg_dram[r:r + 1, :], cnt_dram[0:1, :])
        else:
            g = b.to(nc.gpsimd)
            ins_cc = nc.gpsimd.collective_compute(
                "AllGather", ALU.bypass, replica_groups=[list(range(NCORES))],
                ins=[cnt_dram.ap().opt()], outs=[cntg_dram.ap().opt()])
            b.inc(ins_cc, "cc", 1)
            if "cc" not in b.pend:
                b.pend.append("cc")
        b.ldma(small3[0:8, 8:16], cntg_dram[:, :])
        t = b.to(nc.tensor)
        t.matmul(pb[0][0:1, 0:8], cmsk[0:8, 0:1], small3[0:8, 8:16],
                 start=True, stop=True)
        v = b.to(nc.vector)
        v.tensor_scalar(small3[0:1, 8:16], pb[0][0:1, 0:8], -2.0, CAPX,
                        ALU.mult, ALU.add)
        t = b.to(nc.tensor)
        t.transpose(pb[1][0:8, 0:1], small3[0:1, 8:16], C.idt32[0:1, 0:1])
        t.transpose(pb[0][0:8, 0:1], small3[0:1, 8:16], C.idt32[0:1, 0:1])
        v = b.to(nc.vector)
        v.tensor_copy(small3[0:8, 0:1], pb[1][0:8, 0:1])
        v.tensor_tensor_scan(cu8[0:8, 0:T], el8[0:8, 0:T], el8[0:8, 0:T],
                             0.0, ALU.add, ALU.add)
        v.tensor_scalar(cu8[0:8, 0:T], cu8[0:8, 0:T], -1.0, None, ALU.mult)
        v.tensor_scalar(cu8[0:8, 0:T], cu8[0:8, 0:T], small3[0:8, 0:1],
                        None, ALU.add)
        v.tensor_scalar(cu8[0:8, 0:T], cu8[0:8, 0:T], 0.0, None, ALU.is_ge)
        v.tensor_tensor(cu8[0:8, 0:T], cu8[0:8, 0:T], el8[0:8, 0:T], ALU.mult)
        b.ldma(kept_dram[e:e + 1, 0:T], cu8[e:e + 1, 0:T])
        v = b.to(nc.vector)
        v.tensor_sub(el8[0:8, 0:T], el8[0:8, 0:T], cu8[0:8, 0:T])
        b.ldma(cu8[0:8, 0:T], bass.AP(fb_dram, 0, [[0, 8], [1, T]]))
        v = b.to(nc.vector)
        v.tensor_add(el8[0:8, 0:T], el8[0:8, 0:T], cu8[0:8, 0:T])
        b.ldma(fb_dram[0:1, 0:T], el8[e:e + 1, 0:T])
    b.ldma(el8[0:8, 0:T], kept_dram[:, :])
    for i in range(NT + 1):
        if i < NT:
            t = b.to(nc.tensor)
            t.transpose(pb[i % 2][0:8, 0:128], C.coeff[:, i, :], C.idt32[:, :])
        if i > 0:
            v = b.to(nc.vector)
            v.tensor_copy(cu8[0:8, (i - 1) * 128:i * 128], pb[(i - 1) % 2][0:8, 0:128])
    v = b.to(nc.vector)
    v.tensor_tensor(el8[0:8, 0:T], el8[0:8, 0:T], cu8[0:8, 0:T], ALU.mult)
    for i in range(NT + 1):
        if i < NT:
            t = b.to(nc.tensor)
            t.transpose(pb[i % 2][0:128, 0:8], el8[0:8, i * 128:(i + 1) * 128],
                        C.idt32[0:8, 0:8])
        if i > 0:
            v = b.to(nc.vector)
            v.tensor_copy(C.htm[:, i - 1, :], pb[(i - 1) % 2][0:128, 0:8])
    b.ldma(cu8[0:8, 0:T], bass.AP(fb_dram, 0, [[0, 8], [1, T]]))
    for i in range(NT + 1):
        if i < NT:
            t = b.to(nc.tensor)
            t.transpose(pb[i % 2][0:128, 0:8], cu8[0:8, i * 128:(i + 1) * 128],
                        C.idt32[0:8, 0:8])
        if i > 0:
            v = b.to(nc.vector)
            v.tensor_copy(C.ltm[:, i - 1, :], pb[(i - 1) % 2][0:128, 0:8])
    if dbg:
        okw = nc.dram_tensor("d_kw", [8, T], F32, kind="ExternalOutput")
        b.ldma(okw[:, :], el8[0:8, 0:T])
        ofb = nc.dram_tensor("d_fb", [128, NT * 8], F32, kind="ExternalOutput")
        b.ldma(ofb[:, :], C.ltm[:, 0:NT, 0:8])
    if upto <= 4:
        b.finish()
        return

    w1v = View(C.WBh, [128, 2, 8, 512])
    gtile = View(stage6, [128, 512], base=2048)
    b1v = nc.alloc_sbuf_tensor("b1v", [128, 32], F32)
    b1s = nc.alloc_sbuf_tensor("b1s", [32, 128], F32)

    w1vals = {}

    def w1load(src, fg):
        wsrc = w116[src] if src < 8 else fw116
        w1vals[(src, fg)] = b.ldma_pre(
            w1v[:, fg % 2, 0:8, 0:512],
            wsrc.rearrange("(kf p) f -> p kf f", p=128)
            [:, :, fg * 512:(fg + 1) * 512], "w1p")
    w1load(0, 0)
    for src in range(9):
        if src < 8:
            b.ldma(b1s[0:32, 0:128], bass.AP(b1, src * 4096, [[128, 32], [1, 128]]))
        else:
            b.ldma(b1s[0:32, 0:128], bass.AP(fb1, 0, [[128, 32], [1, 128]]))
        t = b.to(nc.tensor)
        t.transpose(pb[0][0:128, 0:32], b1s[0:32, 0:128], C.idt32[0:32, 0:32])
        t.transpose(pb[1][0:128, 0:32], b1s[0:32, 0:128], C.idt32[0:32, 0:32])
        v = b.to(nc.vector)
        v.tensor_copy(b1v[:, 0:32], pb[0][0:128, 0:32])
        for fg in range(8):
            if fg < 7:
                w1load(src, fg + 1)
            elif src < 8:
                w1load(src + 1, 0)
            for fc4 in range(4):
                ffc = fg * 4 + fc4
                for tb in range(8):
                    t = b.to(nc.tensor)
                    b.wait(nc.tensor, "w1p", val=w1vals[(src, fg)])
                    for kf in range(8):
                        t.matmul(pb[tb % 2][:, :],
                                 w1v[:, fg % 2, kf, fc4 * 128:(fc4 + 1) * 128],
                                 C.xnT[:, kf, tb * 512:(tb + 1) * 512],
                                 start=(kf == 0), stop=(kf == 7))
                    sc = b.to(nc.scalar)
                    sc.activation(gtile[:, 0:512], pb[tb % 2][:, :], AF.Gelu,
                                  bias=b1v[:, ffc:ffc + 1])
                    gth, gsi = gt3s(src)
                    b.ldma(gth[gsi, ffc, :, tb * 512:(tb + 1) * 512],
                           gtile[:, 0:512])
    if upto <= 5:
        b.finish()
        return

    wdv = View(C.AB1h, [128, 32, 512])
    gtv = View(C.AB1h, [128, 32, 512], base=32 * 512)
    b2bc = View(C.AF3h, [128, 1024], base=4096)
    b.ldma(b2bc[:, :], bass.AP(fb2, 0, [[0, 128], [1, 1024]]))
    for src in range(9):
        for n in range(2):
            if src < 8:
                b.ldma(wdv[:, 0:32, 0:512],
                       w216[src, :, n * 512:(n + 1) * 512]
                       .rearrange("(c p) n -> p c n", p=128))
            else:
                b.ldma(wdv[:, 0:32, 0:512],
                       fw216[:, n * 512:(n + 1) * 512]
                       .rearrange("(c p) n -> p c n", p=128))
            for tb in range(8):
                gth, gsi = gt3s(src)
                b.ldma(gtv[:, 0:32, 0:512],
                       gth[gsi, :, :, tb * 512:(tb + 1) * 512]
                       .rearrange("c p n -> p c n"))
                for t4 in range(4):
                    i = tb * 4 + t4
                    t = b.to(nc.tensor)
                    for ffc in range(32):
                        t.matmul(pb[i % 2][:, :],
                                 gtv[:, ffc, t4 * 128:(t4 + 1) * 128],
                                 wdv[:, ffc, 0:512],
                                 start=(ffc == 0), stop=(ffc == 31))
                    v = b.to(nc.vector)
                    if src == 0:
                        v.tensor_scalar(C.acc[:, i, n * 512:(n + 1) * 512],
                                        pb[i % 2][:, :], C.htm[:, i, src:src + 1],
                                        None, ALU.mult)
                    elif src < 8:
                        v.scalar_tensor_tensor(C.acc[:, i, n * 512:(n + 1) * 512],
                                               pb[i % 2][:, :],
                                               C.htm[:, i, src:src + 1],
                                               C.acc[:, i, n * 512:(n + 1) * 512],
                                               ALU.mult, ALU.add)
                    else:
                        v.tensor_add(gtile[:, 0:512], pb[i % 2][:, :],
                                     b2bc[:, n * 512:(n + 1) * 512])
                        b.ldma(yfb_tm[i * 128:(i + 1) * 128,
                                      n * 512:(n + 1) * 512], gtile[:, 0:512])
    if upto <= 6:
        b.finish()
        return

    kwb = View(C.WBh, [8, T])
    b2sb = View(C.WBh, [8, 1024], base=T)
    v = b.to(nc.vector)
    v.tensor_copy(kwb[0:8, 0:T], el8[0:8, 0:T])
    b.ldma(b2sb[0:8, 0:1024], b216[:, :])
    yfbs = View(stage6, [128, 1024])
    for i in range(NT):
        t = b.to(nc.tensor)
        t.matmul(pb[2][:, 0:512], kwb[0:8, i * 128:(i + 1) * 128],
                 b2sb[0:8, 0:512], start=True, stop=True)
        t.matmul(pb[3][:, 0:512], kwb[0:8, i * 128:(i + 1) * 128],
                 b2sb[0:8, 512:1024], start=True, stop=True)
        b.ldma(af[:, 0:1024], x_in[i * 128:(i + 1) * 128, :])
        b.ldma(af[:, 1024:2048], s_dram[i * 128:(i + 1) * 128, :])
        b.ldma(yfbs[:, 0:1024], yfb_tm[i * 128:(i + 1) * 128, :])
        v = b.to(nc.vector)
        acc_f = af[:, 2048:3072]
        yv = af[:, 3072:4096]
        v.tensor_copy(acc_f, C.acc[:, i, 0:1024])
        v.tensor_add(af[:, 2048:2560], af[:, 2048:2560], pb[2][:, 0:512])
        v.tensor_add(af[:, 2560:3072], af[:, 2560:3072], pb[3][:, 0:512])
        v.tensor_copy(yv, yfbs[:, 0:1024])
        v.tensor_sub(yv, yv, acc_f)
        v.tensor_scalar(yv, yv, C.ltm[:, i, 0:1], None, ALU.mult)
        v.tensor_add(acc_f, acc_f, yv)
        sc = b.to(nc.scalar)
        sc.activation(yv, acc_f, AF.Silu)
        v = b.to(nc.vector)
        v.tensor_mul(yv, yv, af[:, 1024:2048])
        v.tensor_add(af[:, 0:1024], af[:, 0:1024], yv)
        b.ldma(x4_out[i * 128:(i + 1) * 128, :], af[:, 0:1024])
    b.finish()


def run_layer3(x_full, weights, upto=99):
    with ExitStack() as stack:
        nc = bass.Bass("TRN2", target_bir_lowering=False)
        b = Build(nc, stack)
        C = alloc_common(b)
        emit_common_init(b, C)
        build_layer3(b, C, upto=upto)
        in_maps = []
        for c in range(NCORES):
            m = {"x": np.ascontiguousarray(x_full[c])}
            for k in L3_KEYS:
                m[k] = np.ascontiguousarray(weights[k])
            m.update(b.consts)
            cm = np.zeros((8, 1), np.float32)
            cm[:c, 0] = 1.0
            m["coremask"] = cm
            in_maps.append(m)
        res = run_bass_kernel_spmd(nc, in_maps, core_ids=list(range(NCORES)))
        return np.stack([res.results[c]["x4"] for c in range(NCORES)])


AF_ARENA = 2609
AB_ARENA = 3073


def build_fused(b, C):
    nc = b.nc
    x0 = nc.dram_tensor("x", [T, D], F32, kind="ExternalInput")
    xh1 = nc.dram_tensor("xh1", [T, D], F32)
    xh2 = nc.dram_tensor("xh2", [T, D], F32)
    xh3 = nc.dram_tensor("xh3", [T, D], F32)
    x4 = nc.dram_tensor("x4", [T, D], F32, kind="ExternalOutput")
    fh = nc.alloc_sbuf_tensor("arena_f", [128, AF_ARENA], F32)
    bh = nc.alloc_sbuf_tensor("arena_b", [128, AB_ARENA], BF16)
    orig = nc.alloc_sbuf_tensor
    state = {"f": 0, "b": 0}

    def wrap(name, shape, dt, *a, **kw):
        n = 1
        for d in shape[1:]:
            n *= int(d)
        if dt is F32 and state["f"] + n <= AF_ARENA:
            v = View(fh, list(shape), base=state["f"])
            state["f"] += n
            return v
        if dt is BF16 and state["b"] + n <= AB_ARENA:
            v = View(bh, list(shape), base=state["b"])
            state["b"] += n
            return v
        return orig(name, shape, dt, *a, **kw)

    nc.alloc_sbuf_tensor = wrap
    try:
        for fn, src, dst in ((build_layer0, x0, xh1), (build_layer1, xh1, xh2),
                             (build_layer2, xh2, xh3), (build_layer3, xh3, x4)):
            state["f"] = 0
            state["b"] = 0
            fn(b, C, x_in_t=src, x_out_t=dst)
    finally:
        nc.alloc_sbuf_tensor = orig


ALL_KEYS = None


def run_fused(x_full, weights):
    keys = L0_KEYS + L1_KEYS + L2_KEYS + L3_KEYS
    with ExitStack() as stack:
        nc = bass.Bass("TRN2", target_bir_lowering=False)
        b = Build(nc, stack)
        C = alloc_common(b)
        emit_common_init(b, C)
        build_fused(b, C)
        in_maps = []
        for c in range(NCORES):
            m = {"x": np.ascontiguousarray(x_full[c])}
            for k in keys:
                m[k] = np.ascontiguousarray(weights[k])
            m.update(b.consts)
            cm = np.zeros((8, 1), np.float32)
            cm[:c, 0] = 1.0
            m["coremask"] = cm
            in_maps.append(m)
        res = run_bass_kernel_spmd(nc, in_maps, core_ids=list(range(NCORES)))
        return np.stack([res.results[c]["x4"] for c in range(NCORES)])


def kernel(**inputs):
    x = np.ascontiguousarray(np.asarray(inputs["L0_attn_x"], dtype=np.float32))
    x = run_fused(x, inputs)
    return np.ascontiguousarray(x.astype(np.float32, copy=False))
